# version 12
# baseline (speedup 1.0000x reference)
import math
import contextlib
import numpy as np
import concourse.bass as bass
import concourse.mybir as mybir
from concourse.bass_utils import run_bass_kernel_spmd

F32 = mybir.dt.float32
BF16 = mybir.dt.bfloat16
AF = mybir.ActivationFunctionType
ALU = mybir.AluOpType
AXX = mybir.AxisListType.X

D = 2048
NH = 16
HD = 128
B = 2
S = 8192
T = 2048
NC = 8
SCALE = 1.0 / math.sqrt(HD)
NEG = -1.0e9
E_MOE = 8
F_MOE = 1024
SF = 1024
FI = 8192
GCOLS = D + D + 16

ENGINES = ("tensor", "scalar", "vector", "gpsimd", "sync")


class Prog:

    def __init__(self, nc):
        self.nc = nc
        self.ops = []
        self.sem_counts = {}
        self.sem_handles = {}
        self.known = {e: {} for e in ENGINES}
        self._sem_names = []
        self.floor = []
        self.fence_exclude = set()

    def _sem(self, name):
        if name not in self.sem_counts:
            self.sem_counts[name] = 0
            self._sem_names.append(name)
        return name

    def op(self, engine, fn, deps=(), dma=False, key=None, no_inc=False):
        if dma:
            semn = self._sem("d_" + (key or engine)); amount = 16
        elif no_inc:
            semn, amount = None, 0
        else:
            semn = self._sem("c_" + engine); amount = 1
        pruned = []
        for tok in list(deps) + self.floor:
            if tok is None:
                continue
            s, v = tok
            if self.known[engine].get(s, 0) >= v:
                continue
            pruned.append((s, v))
            self.known[engine][s] = v
        if semn is not None:
            self.sem_counts[semn] += amount
            tok = (semn, self.sem_counts[semn])
        else:
            tok = None
        self.ops.append((engine, fn, pruned, (semn, amount) if semn else None))
        return tok

    def barrier_all(self):
        return [(s, c) for s, c in self.sem_counts.items() if c > 0]

    def fence(self):
        ex = {"d_" + k for k in self.fence_exclude}
        self.floor = [(s, c) for (s, c) in self.barrier_all() if s not in ex]
        self.fence_exclude = set()

    def build(self, stack):
        nc = self.nc
        for s in self._sem_names:
            self.sem_handles[s] = stack.enter_context(nc.semaphore(s))
        blk = stack.enter_context(nc.Block())
        def mk(engine_name):
            def run(eng):
                for (e, fn, deps, inc) in self.ops:
                    if e != engine_name:
                        continue
                    for (s, v) in deps:
                        eng.wait_ge(self.sem_handles[s], v)
                    ins = fn(eng)
                    if inc is not None:
                        ins.then_inc(self.sem_handles[inc[0]], inc[1])
            return run
        blk.tensor(mk("tensor"))
        blk.scalar(mk("scalar"))
        blk.vector(mk("vector"))
        blk.gpsimd(mk("gpsimd"))
        blk.sync(mk("sync"))



def _tile_lhsT(wT):
    din, dout = wT.shape
    a = wT.reshape(din // 128, 128, dout // 128, 128)
    return np.ascontiguousarray(a.transpose(2, 1, 0, 3))


def _tile_rhs(wT, ocols):
    din, dout = wT.shape
    a = wT.reshape(din // 128, 128, dout // ocols, ocols)
    return np.ascontiguousarray(a.transpose(2, 1, 0, 3))


def _core_token_rows(c):
    g, r = c // 4, c % 4
    return [(g * S + (4 * w + r) * 512, 512) for w in range(4)]


def _agrow_of_block(b):
    return (b % 4) * T + (b // 4) * 512


def _stage_cmask(r):
    rel = np.arange(16)[:, None, None]
    pk = np.arange(128)[None, :, None]
    fq = np.arange(512)[None, None, :]
    keep = (fq - pk + (4 * r - rel) * 128) >= 0
    return np.where(keep, np.float32(0.0), np.float32(NEG))


def stage_inputs(inputs):
    st = {}
    st["l0_wq"] = _tile_lhsT(np.ascontiguousarray(inputs["L0_attn_w_project"].T[:, :D]))
    st["l0_wkv"] = _tile_rhs(np.ascontiguousarray(inputs["L0_attn_w_project"].T[:, D:]), 512)
    st["l0_wo"] = _tile_lhsT(np.ascontiguousarray(inputs["L0_attn_w_out"].T))
    st["l0_rms"] = np.ascontiguousarray(
        inputs["L0_ffn_rms_weight"].reshape(16, 128).T)
    st["l0_router"] = np.ascontiguousarray(
        inputs["L0_ffn_router_w"].reshape(16, 128, 8).transpose(1, 0, 2)
        .reshape(128, 128))
    wg = inputs["L0_ffn_w_gate"]; wu = inputs["L0_ffn_w_up"]; wd = inputs["L0_ffn_w_down"]
    st["l0_wg"] = np.stack([_tile_lhsT(wg[e]) for e in range(E_MOE)])
    st["l0_wu"] = np.stack([_tile_lhsT(wu[e]) for e in range(E_MOE)])
    st["l0_wd"] = np.stack([_tile_lhsT(wd[e]) for e in range(E_MOE)])
    st["l0_swg"] = _tile_lhsT(inputs["L0_ffn_sw_gate"])
    st["l0_swu"] = _tile_lhsT(inputs["L0_ffn_sw_up"])
    st["l0_swd"] = _tile_lhsT(inputs["L0_ffn_sw_down"])
    w1 = inputs["L1_attn_w_qkv"]
    st["l1_wq"] = _tile_lhsT(np.ascontiguousarray(w1[:, :D]))
    st["l1_wkv"] = _tile_rhs(np.ascontiguousarray(w1[:, D:]), 512)
    st["l1_wo"] = _tile_lhsT(np.ascontiguousarray(inputs["L1_attn_w_o"]))
    perm = (4 * (np.arange(FI) % (FI // 4)) + (np.arange(FI) // (FI // 4)))
    st["l1_wg"] = _tile_lhsT(np.ascontiguousarray(inputs["L1_ffn_w_gate"][:, perm]))
    st["l1_wu"] = _tile_lhsT(np.ascontiguousarray(inputs["L1_ffn_w_up"][:, perm]))
    st["l1_wd"] = _tile_lhsT(np.ascontiguousarray(inputs["L1_ffn_w_down"][perm, :]))

    x = np.asarray(inputs["L0_attn_x"], dtype=np.float32).reshape(B * S, D)
    import ml_dtypes
    cores = []
    for c in range(NC):
        rows = _core_token_rows(c)
        xs = np.concatenate([x[s:s + n] for (s, n) in rows], axis=0)
        cores.append({"x_t": np.ascontiguousarray(xs.T),
                      "cmask": _stage_cmask(c % 4).astype(ml_dtypes.bfloat16)})
    st["cid"] = np.eye(128, dtype=np.float32)
    return st, cores


def unstage_output(shards):
    out = np.empty((B * S, D), dtype=np.float32)
    for c, sh in enumerate(shards):
        xs = np.ascontiguousarray(sh.T)
        for i, (s, n) in enumerate(_core_token_rows(c)):
            out[s:s + n] = xs[i * n:(i + 1) * n]
    return out.reshape(B, S, D)



def _sb(stack, nc, name, shape, dt=BF16):
    return stack.enter_context(nc.sbuf_tensor(name, shape, dt))


def _ps(stack, nc, name, shape):
    return stack.enter_context(nc.psum_tensor(name, shape, F32))


def phase_qkv(p, stack, dr, L, consts, x_dep=None):
    nc = p.nc
    name = f"p1_{L}"
    xt = [_sb(stack, nc, f"{name}_x{c}", [128, T]) for c in range(16)]
    wq_sb = [_sb(stack, nc, f"{name}_wq{s}", [128, 2048]) for s in range(2)]
    wkv_sb = [_sb(stack, nc, f"{name}_wkv{s}", [128, 8192]) for s in range(2)]
    qstage = [_sb(stack, nc, f"{name}_qs{s}", [128, 512]) for s in range(2)]
    kvstage = [_sb(stack, nc, f"{name}_kvs{s}", [128, 512]) for s in range(2)]
    sq = [_sb(stack, nc, f"{name}_sq{s}", [128, 512]) for s in range(2)]
    sqf = _sb(stack, nc, f"{name}_sqf", [128, 512], F32)
    qn_row = _sb(stack, nc, f"{name}_qnrow", [1, 2048], F32)
    qmax_row = _sb(stack, nc, f"{name}_qmax", [1, 16], F32)
    knorm_all = _sb(stack, nc, f"{name}_kn", [128, 256], F32)
    psq = [_ps(stack, nc, f"{name}_psq{s}", [128, 512]) for s in range(2)]
    pskv = [_ps(stack, nc, f"{name}_pskv{s}", [128, 512]) for s in range(2)]
    psqn = _ps(stack, nc, f"{name}_psqn", [1, 512])
    ones128 = consts["ones128"]

    t_x = [p.op("gpsimd", (lambda c: lambda e: e.dma_start(
        out=xt[c][:], in_=dr[f"x{L}"][c * 128:(c + 1) * 128, :]))(c),
        deps=[x_dep], dma=True, key="xt") for c in range(16)]
    t_x_all = t_x[15]

    t_wu = [None, None]
    t_qdma = [None, None]
    t_ev = [None, None]
    t_qn_mm = None
    t_qn_ev = None
    t_sq_use = [None, None]
    t_mx = None
    for oc in range(16):
        s = oc % 2
        t_w = p.op("gpsimd", (lambda oc, s: lambda e: e.dma_start(
            out=wq_sb[s][:], in_=dr[f"wq{L}"][oc]))(oc, s),
            deps=[t_wu[s]], dma=True, key="wq%d" % s)
        last_mm = None
        pend_qn = None
        for tt in range(4):
            pslot = tt % 2
            deps = [t_w, t_ev[pslot]] + ([t_x_all] if oc == 0 and tt == 0 else [])
            for c in range(16):
                last_mm = p.op("tensor", (lambda s, c, tt, pslot: lambda e: e.matmul(
                    psq[pslot][:], wq_sb[s][:, c * 128:(c + 1) * 128],
                    xt[c][:, tt * 512:(tt + 1) * 512],
                    start=(c == 0), stop=(c == 15), skip_group_check=True))(s, c, tt, pslot),
                    deps=(deps if c == 0 else []))
            t_cp = p.op("scalar", (lambda pslot: lambda e: e.copy(
                qstage[pslot][:], psq[pslot][:]))(pslot),
                deps=[last_mm, t_qdma[pslot]])
            t_qdma[pslot] = p.op("sync", (lambda oc, tt, pslot: lambda e: e.dma_start(
                out=dr[f"qf{L}"][oc][:, tt * 512:(tt + 1) * 512], in_=qstage[pslot][:]))(oc, tt, pslot),
                deps=[t_cp], dma=True, key="qout%d" % pslot)
            t_sq = p.op("scalar", (lambda pslot: lambda e: e.activation(
                out=sq[pslot][:], in_=psq[pslot][:], func=AF.Square))(pslot),
                deps=[last_mm, t_sq_use[pslot]])
            if pend_qn is not None:
                ptt, psl, psq_tok = pend_qn
                t_qn_mm = p.op("tensor", (lambda psl: lambda e: e.matmul(
                    psqn[:], ones128[:], sq[psl][:], start=True, stop=True,
                    skip_group_check=True))(psl), deps=[psq_tok, t_qn_ev])
                t_sq_use[psl] = t_qn_mm
                t_qn_ev = p.op("scalar", (lambda ptt: lambda e: e.copy(
                    qn_row[:, ptt * 512:(ptt + 1) * 512], psqn[:]))(ptt),
                    deps=[t_qn_mm, t_mx])
                t_ev[psl] = t_qn_ev
            pend_qn = (tt, pslot, t_sq)
        ptt, psl, psq_tok = pend_qn
        t_qn_mm = p.op("tensor", (lambda psl: lambda e: e.matmul(
            psqn[:], ones128[:], sq[psl][:], start=True, stop=True,
            skip_group_check=True))(psl), deps=[psq_tok, t_qn_ev])
        t_sq_use[psl] = t_qn_mm
        t_qn_ev = p.op("scalar", (lambda ptt: lambda e: e.copy(
            qn_row[:, ptt * 512:(ptt + 1) * 512], psqn[:]))(ptt),
            deps=[t_qn_mm, t_mx])
        t_ev[psl] = t_qn_ev
        t_wu[s] = last_mm
        t_mx = p.op("vector", (lambda oc: lambda e: e.reduce_max(
            qmax_row[:, oc:oc + 1], qn_row[:], axis=AXX))(oc), deps=[t_ev[0], t_ev[1]])
    t_qmax_dma = p.op("sync", lambda e: e.dma_start(
        out=dr[f"qmax{L}"], in_=qmax_row[:]), deps=[t_mx], dma=True, key="qmax")

    t_wu = [None, None]
    t_kvdma = [None, None]
    t_norm_red = None
    t_sqk_slot = [None, None]
    t_norm_dma_done = []
    for ft in range(8):
        s = ft % 2
        t_w = p.op("gpsimd", (lambda ft, s: lambda e: e.dma_start(
            out=wkv_sb[s][:], in_=dr[f"wkv{L}"][ft]))(ft, s),
            deps=[t_wu[s]], dma=True, key="wkv%d" % s)
        last_mm = None
        for tc in range(16):
            pslot = tc % 2
            deps = [t_w, t_kvdma[pslot], t_sqk_slot[pslot]]
            for c in range(16):
                last_mm = p.op("tensor", (lambda s, c, tc, pslot: lambda e: e.matmul(
                    pskv[pslot][:], xt[c][:, tc * 128:(tc + 1) * 128],
                    wkv_sb[s][:, c * 512:(c + 1) * 512],
                    start=(c == 0), stop=(c == 15), skip_group_check=True))(s, c, tc, pslot),
                    deps=(deps if c == 0 else []))
            t_cp = p.op("scalar", (lambda pslot: lambda e: e.copy(
                kvstage[pslot][:], pskv[pslot][:]))(pslot),
                deps=[last_mm, t_kvdma[pslot]])
            t_kvdma[pslot] = p.op("sync", (lambda ft, tc, pslot: lambda e: e.dma_start(
                out=dr[f"gin_kv{L}"][tc * 128:(tc + 1) * 128, ft * 512:(ft + 1) * 512],
                in_=kvstage[pslot][:]))(ft, tc, pslot),
                deps=[t_cp], dma=True, key="kvout%d" % pslot)
            if ft < 4:
                t_sqk = p.op("scalar", (lambda pslot: lambda e: e.activation(
                    out=sqf[:], in_=pskv[pslot][:], func=AF.Square))(pslot),
                    deps=[last_mm, t_norm_red])
                t_sqk_slot[pslot] = t_sqk
                t_norm_red = p.op("vector", (lambda ft, tc: lambda e: e.reduce_sum(
                    knorm_all[:, tc * 16 + ft * 4: tc * 16 + ft * 4 + 4],
                    sqf[:].rearrange("p (h d) -> p h d", d=128), axis=AXX))(ft, tc),
                    deps=[t_sqk])
        t_wu[s] = last_mm
    for tc in range(16):
        t_norm_dma_done.append(p.op("gpsimd", (lambda tc: lambda e: e.dma_start(
            out=dr[f"gin_n{L}"][tc * 128:(tc + 1) * 128, :],
            in_=knorm_all[:, tc * 16:(tc + 1) * 16]))(tc),
            deps=[t_norm_red], dma=True, key="ginn"))
    return {"gin": [t_kvdma[0], t_kvdma[1]] + t_norm_dma_done,
            "qf": [t_qdma[0], t_qdma[1]], "qmax": [t_qmax_dma]}



def _tensor_table():
    t = {}
    t["x0"] = ([D, T], np.float32, F32, None)
    for L, wq, wkv in ((0, "l0_wq", "l0_wkv"), (1, "l1_wq", "l1_wkv")):
        t[f"wq{L}"] = ([16, 128, 16, 128], np.float32, F32, None)
        t[f"wkv{L}"] = ([8, 128, 16, 512], np.float32, F32, None)
        t[f"qf{L}"] = ([16, 128, T], np.dtype("uint16"), BF16, f"qkv{L}")
        t[f"gin_kv{L}"] = ([T, 2 * D], np.dtype("uint16"), BF16, f"qkv{L}")
        t[f"gin_n{L}"] = ([T, 16], np.dtype("uint16"), BF16, f"qkv{L}")
        t[f"qmax{L}"] = ([1, 16], np.float32, F32, f"qkv{L}")
        t[f"gout_kv{L}"] = ([S, 2 * D], np.dtype("uint16"), BF16, f"gather{L}")
        t[f"gout_n{L}"] = ([S, 16], np.dtype("uint16"), BF16, f"gather{L}")
    t["cmask"] = ([16, 128, 512], np.dtype("uint16"), BF16, None)
    t["cid"] = ([128, 128], np.float32, F32, None)
    t["wo0"] = ([16, 128, 16, 128], np.float32, F32, None)
    t["a0"] = ([16, 128, T], np.dtype("uint16"), BF16, "attn0")
    t["r1"] = ([D, T], np.float32, F32, "out0")
    t["rms0"] = ([128, 16], np.float32, F32, None)
    t["router0"] = ([128, 128], np.float32, F32, None)
    t["wg0"] = ([E_MOE, 8, 128, 16, 128], np.float32, F32, None)
    t["wu0"] = ([E_MOE, 8, 128, 16, 128], np.float32, F32, None)
    t["wd0"] = ([E_MOE, 16, 128, 8, 128], np.float32, F32, None)
    t["swg0"] = ([8, 128, 16, 128], np.float32, F32, None)
    t["swu0"] = ([8, 128, 16, 128], np.float32, F32, None)
    t["swd0"] = ([16, 128, 8, 128], np.float32, F32, None)
    t["x1"] = ([D, T], np.float32, F32, "moe0")
    t["wo1"] = ([16, 128, 16, 128], np.float32, F32, None)
    t["cs_part"] = ([S, NH], np.float32, F32, "cols1")
    t["cs_stack"] = ([4, 128, 16, NH], np.float32, F32, "csred")
    t["cs_my"] = ([T, NH], np.float32, F32, "csred")
    t["r3"] = ([D, T], np.float32, F32, "vout1")
    t["wg1"] = ([64, 128, 16, 128], np.float32, F32, None)
    t["wu1"] = ([64, 128, 16, 128], np.float32, F32, None)
    t["wd1"] = ([16, 128, 64, 128], np.float32, F32, None)
    t["y"] = ([D, T], np.float32, F32, "ffn1")
    return t


_PHASE_ORDER = ["qkv0", "gather0", "attn0", "out0", "moe0",
                "qkv1", "gather1", "cols1", "csred", "vout1", "ffn1"]

_HOST_STAGE_KEY = {
    "wq0": "l0_wq", "wkv0": "l0_wkv", "wo0": "l0_wo", "rms0": "l0_rms",
    "router0": "l0_router", "wg0": "l0_wg", "wu0": "l0_wu", "wd0": "l0_wd",
    "swg0": "l0_swg", "swu0": "l0_swu", "swd0": "l0_swd",
    "wq1": "l1_wq", "wkv1": "l1_wkv", "wo1": "l1_wo",
    "wg1": "l1_wg", "wu1": "l1_wu", "wd1": "l1_wd",
}


def build_consts(p, stack, dr):
    nc = p.nc
    c = {}
    c["ones128"] = _sb(stack, nc, "c_ones128", [128, 1])
    c["ones_row"] = _sb(stack, nc, "c_onesrow", [1, 128])
    c["ones_row_f"] = _sb(stack, nc, "c_onesrowf", [1, 128], F32)
    c["epsln"] = _sb(stack, nc, "c_epsln", [1, 1], F32)
    c["epsrms"] = _sb(stack, nc, "c_epsrms", [1, 1], F32)
    c["ident"] = _sb(stack, nc, "c_ident", [128, 128])
    c["ident_f"] = _sb(stack, nc, "c_identf", [128, 128], F32)
    c["ones128_f"] = _sb(stack, nc, "c_ones128f", [128, 1], F32)
    toks = []
    toks.append(p.op("vector", lambda e: e.memset(c["ones128"][:], 1.0)))
    toks.append(p.op("vector", lambda e: e.memset(c["ones_row"][:], 1.0)))
    toks.append(p.op("vector", lambda e: e.memset(c["ones_row_f"][:], 1.0)))
    toks.append(p.op("vector", lambda e: e.memset(c["epsln"][:], 1e-5)))
    toks.append(p.op("vector", lambda e: e.memset(c["epsrms"][:], 1e-6)))
    toks.append(p.op("vector", lambda e: e.memset(c["ones128_f"][:], 1.0)))
    toks.append(p.op("gpsimd", lambda e: e.dma_start(
        out=c["ident"][:], in_=dr["cid"]), dma=True, key="cid16"))
    toks.append(p.op("sync", lambda e: e.dma_start(
        out=c["ident_f"][:], in_=dr["cid"]), dma=True, key="cidf"))
    c["tok"] = toks
    return c


PHASE_TENSORS = {
    "qkv0": (["x0", "wq0", "wkv0"], ["qf0", "gin_kv0", "gin_n0", "qmax0"]),
    "gather0": (["gin_kv0", "gin_n0"], ["gout_kv0", "gout_n0"]),
    "attn0": (["qf0", "gout_kv0", "gout_n0", "qmax0", "cmask"], ["a0"]),
    "out0": (["a0", "wo0", "x0"], ["r1"]),
    "moe0": (["r1", "rms0", "router0", "wg0", "wu0", "wd0",
              "swg0", "swu0", "swd0"], ["x1"]),
    "qkv1": (["x1", "wq1", "wkv1"], ["qf1", "gin_kv1", "gin_n1", "qmax1"]),
    "gather1": (["gin_kv1", "gin_n1"], ["gout_kv1", "gout_n1"]),
    "cols1": (["qf1", "gout_kv1", "gout_n1", "qmax1"], ["cs_part"]),
    "csred": (["cs_part"], ["cs_my"]),
    "vout1": (["gin_kv1", "cs_stack", "wo1", "x1", "wg1", "wu1", "wd1"], ["r3"]),
    "vout1f": (["gin_kv1", "cs_my", "wo1", "x1", "wg1", "wu1", "wd1"], ["r3"]),
    "ffn1": (["r3", "wg1", "wu1", "wd1"], ["y"]),
}

_PHASE_FNS = {}


def build_core_nc(phase_names, fused=False):
    table = _tensor_table()
    global _FUSED
    _FUSED = fused
    produced_here, consumed = set(), set()
    for ph in phase_names:
        ins, outs = PHASE_TENSORS[ph]
        consumed.update(ins)
        produced_here.update(outs)
    consumed.add("cid")
    later = set()
    _present = set(phase_names)
    if "vout1f" in _present:
        _present.add("vout1")
    rest = [ph for ph in _PHASE_ORDER if ph not in _present]
    for ph in rest:
        later.update(PHASE_TENSORS[ph][0])
    nc = bass.Bass("TRN2", target_bir_lowering=False, debug=False)
    dr = {}
    in_names, out_names = [], []
    for name in sorted(consumed | produced_here):
        shape, npdt, bdt, producer = table[name]
        if name in produced_here:
            if name in later or name == "y":
                kind = "ExternalOutput"; out_names.append(name)
            else:
                kind = "Internal"
        else:
            kind = "ExternalInput"; in_names.append(name)
        dr[name] = nc.dram_tensor(name, shape, bdt, kind=kind).ap()
    p = Prog(nc)
    with contextlib.ExitStack() as stack:
        consts = build_consts(p, stack, dr)
        p.fence()
        deps = {}
        for ph in phase_names:
            pstack = contextlib.ExitStack()
            _PHASE_FNS[ph](p, pstack, dr, consts, deps)
            pstack.close()
            p.fence()
        p.op("gpsimd", lambda e: e.nop(), deps=p.barrier_all(), no_inc=True)
        p.build(stack)
    return nc, in_names, out_names


_FUSED = False
_GROUPS = [[0, 1, 2, 3], [4, 5, 6, 7]]


def _ph_gather(L):
    def f(p, stack, dr, consts, deps):
        t1 = p.op("gpsimd", (lambda: lambda e: e.collective_compute(
            "AllGather", ALU.bypass, replica_groups=_GROUPS,
            ins=[dr[f"gin_kv{L}"][:]], outs=[dr[f"gout_kv{L}"][:]]))(),
            dma=True, key="agkv")
        t2 = p.op("gpsimd", (lambda: lambda e: e.collective_compute(
            "AllGather", ALU.bypass, replica_groups=_GROUPS,
            ins=[dr[f"gin_n{L}"][:]], outs=[dr[f"gout_n{L}"][:]]))(),
            dma=True, key="agn")
        deps[f"gather{L}"] = {"gout": [t1, t2]}
    return f


def _ph_csred(p, stack, dr, consts, deps):
    t = p.op("gpsimd", lambda e: e.collective_compute(
        "ReduceScatter", ALU.add, replica_groups=_GROUPS,
        ins=[dr["cs_part"][:]], outs=[dr["cs_my"][:]]),
        dma=True, key="rscs")
    deps["csred"] = {"cs_my": [t]}


_PHASE_FNS["gather0"] = _ph_gather(0)
_PHASE_FNS["gather1"] = _ph_gather(1)
_PHASE_FNS["csred"] = _ph_csred


def _ph_qkv0(p, stack, dr, consts, deps):
    deps["qkv0"] = phase_qkv(p, stack, dr, 0, consts)


def _ph_qkv1(p, stack, dr, consts, deps):
    d = deps.get("moe0")
    deps["qkv1"] = phase_qkv(p, stack, dr, 1, consts,
                             x_dep=(d["x1"][-1] if d else None))


_PHASE_FNS["qkv0"] = _ph_qkv0
_PHASE_FNS["qkv1"] = _ph_qkv1


def _np_of(name, table):
    shape, npdt, bdt, _ = table[name]
    return shape, npdt


_KERNEL_FUSED = False


def kernel(**inputs):
    import ml_dtypes
    st, cores = stage_inputs(inputs)
    table = _tensor_table()
    state = [{} for _ in range(NC)]
    for c in range(NC):
        state[c]["x0"] = cores[c]["x_t"]
        state[c]["cmask"] = cores[c]["cmask"]
        state[c]["cid"] = st["cid"]
        for tname, key in _HOST_STAGE_KEY.items():
            state[c][tname] = st[key]

    if _KERNEL_FUSED:
        launches = [["qkv0", "gather0", "attn0", "out0", "moe0", "qkv1",
                     "gather1", "cols1", "csred", "vout1f", "ffn1"]]
    else:
        launches = [["qkv0"], ["attn0", "out0", "moe0", "qkv1"],
                    ["cols1"], ["vout1", "ffn1"]]
    for li, phs in enumerate(launches):
        nc, in_names, out_names = build_core_nc(phs, fused=_KERNEL_FUSED)
        in_maps = [{n: state[c][n] for n in in_names} for c in range(NC)]
        res = run_bass_kernel_spmd(nc, in_maps, core_ids=list(range(NC)))
        for c in range(NC):
            for n in out_names:
                state[c][n] = np.asarray(res.results[c][n])
        if phs[-1] == "qkv0" or phs[-1] == "qkv1":
            L = 0 if phs[-1] == "qkv0" else 1
            for g in (0, 1):
                cs = range(4 * g, 4 * g + 4)
                gkv = np.concatenate([state[c][f"gin_kv{L}"] for c in cs], 0)
                gn = np.concatenate([state[c][f"gin_n{L}"] for c in cs], 0)
                for c in cs:
                    state[c][f"gout_kv{L}"] = gkv
                    state[c][f"gout_n{L}"] = gn
        if phs[-1] == "cols1":
            for g in (0, 1):
                for c in range(4 * g, 4 * g + 4):
                    r = c % 4
                    state[c]["cs_stack"] = np.ascontiguousarray(np.stack(
                        [state[cc]["cs_part"][r * T:(r + 1) * T]
                         for cc in range(4 * g, 4 * g + 4)])
                        .reshape(4, 16, 128, NH).transpose(0, 2, 1, 3))
    return unstage_output([state[c]["y"] for c in range(NC)])


def phase_attn0(p, stack, dr, consts, deps):
    nc = p.nc
    nm = "a0"
    mb = _sb(stack, nc, nm + "_mb", [128, 16], F32)
    cm = _sb(stack, nc, nm + "_cm", [128, 16 * 512])
    t_cm = p.op("sync", lambda e: e.dma_start(
        out=cm[:].rearrange("p (r q) -> p r q", q=512),
        in_=dr["cmask"].rearrange("r p q -> p r q")), dma=True, key="cm")
    t_mb = emit_mbias(p, stack, dr, consts, 0, mb)
    kt = [_sb(stack, nc, f"{nm}_kt{s}", [128, S]) for s in range(2)]
    vt = [_sb(stack, nc, f"{nm}_vt{s}", [128, S]) for s in range(2)]
    qh = [_sb(stack, nc, f"{nm}_qh{s}", [128, T]) for s in range(2)]
    e_sb = [_sb(stack, nc, f"{nm}_e{s}", [128, 1024]) for s in range(2)]
    ao = [_sb(stack, nc, f"{nm}_ao{s}", [128, 512]) for s in range(2)]
    rrow = _sb(stack, nc, nm + "_rrow", [1, 512], F32)
    rbc = _sb(stack, nc, nm + "_rbc", [128, 512])
    ps_s = [_ps(stack, nc, f"{nm}_pss{s}", [128, 1024]) for s in range(2)]
    ps_o = _ps(stack, nc, nm + "_pso", [128, 512])
    ps_z = _ps(stack, nc, nm + "_psz", [1, 512])
    ps_r = _ps(stack, nc, nm + "_psr", [128, 512])
    ones128, ones_row_f = consts["ones128"], consts["ones_row_f"]

    def emit_loads(h, d_kt, d_vt, d_qh):
        s = h % 2
        tk = tv = tq = None
        for b in range(16):
            agr = (b % 4) * T + (b // 4) * 512
            tk = p.op("sync", (lambda h, b, agr, s: lambda e: e.dma_start_transpose(
                out=kt[s][:, b * 512:(b + 1) * 512],
                in_=dr["gout_kv0"][agr:agr + 512, h * 128:(h + 1) * 128]))(h, b, agr, s),
                deps=[d_kt], dma=True, key="kt%d" % s)
            tv = p.op("gpsimd", (lambda h, b, agr, s: lambda e: e.dma_start(
                out=vt[s][:, b * 512:(b + 1) * 512].rearrange("p (c d) -> p c d", d=128),
                in_=dr["gout_kv0"][agr:agr + 512, D + h * 128:D + (h + 1) * 128]
                .rearrange("(c p) d -> p c d", p=128)))(h, b, agr, s),
                deps=[d_vt], dma=True, key="vt%d" % s)
        tq = p.op("sync", (lambda h, s: lambda e: e.dma_start(
            out=qh[s][:], in_=dr["qf0"][h]))(h, s), deps=[d_qh], dma=True,
            key="qh%d" % s)
        return tk, tv, tq

    t_ld = [None, None]
    t_kt_use = [None, None]
    t_vt_use = [None, None]
    t_qh_use = [None, None]
    t_ld[0] = emit_loads(0, None, None, None)
    t_ld[1] = emit_loads(1, None, None, None)
    t_e_slot = [None, None]
    t_z_slot = [None, None]
    t_ao_tok = None
    t_r_tok = None
    t_rb_tok = None
    t_rc_tok = None
    t_dma_slot = [None, None]
    for h in range(16):
        hs = h % 2
        t_kt_h, t_vt_h, t_qh_h = t_ld[hs]
        t_slast = t_avlast = None
        for w in range(4):
            nch = 16 * w + 16

            def emit_avz(sl, kc0, t_e, nch, hs):
                t_av = t_z = None
                for half in range(2):
                    t_av = p.op("tensor", (lambda hs, sl, kc0, half, nch: lambda e: e.matmul(
                        ps_o[:], vt[hs][:, (kc0 + half) * 128:(kc0 + half + 1) * 128],
                        e_sb[sl][:, half * 512:(half + 1) * 512],
                        start=(kc0 + half == 0), stop=(kc0 + half == nch - 1),
                        skip_group_check=True))(hs, sl, kc0, half, nch),
                        deps=[t_e, t_vt_h] + ([t_ao_tok] if kc0 + half == 0 else []))
                for half in range(2):
                    t_z = p.op("tensor", (lambda sl, kc0, half, nch: lambda e: e.matmul(
                        ps_z[:], ones128[:], e_sb[sl][:, half * 512:(half + 1) * 512],
                        start=(kc0 + half == 0), stop=(kc0 + half == nch - 1),
                        skip_group_check=True))(sl, kc0, half, nch),
                        deps=([t_r_tok] if kc0 + half == 0 else []))
                t_z_slot[sl] = t_z
                return t_av, t_z

            pend = None
            for jj in range(nch // 2):
                sl = jj % 2
                kc0 = 2 * jj
                d0 = [t_kt_h, t_qh_h, t_e_slot[sl], t_mb]
                t_s0 = p.op("tensor", (lambda hs, sl, kc0, w: lambda e: e.matmul(
                    ps_s[sl][:, 0:512], kt[hs][:, kc0 * 128:(kc0 + 1) * 128],
                    qh[hs][:, w * 512:(w + 1) * 512], start=True, stop=True,
                    skip_group_check=True))(hs, sl, kc0, w), deps=d0)
                t_s1 = p.op("tensor", (lambda hs, sl, kc0, w: lambda e: e.matmul(
                    ps_s[sl][:, 512:1024], kt[hs][:, (kc0 + 1) * 128:(kc0 + 2) * 128],
                    qh[hs][:, w * 512:(w + 1) * 512], start=True, stop=True,
                    skip_group_check=True))(hs, sl, kc0, w))
                tm = t_s1
                for half in range(2):
                    rel = kc0 + half - 16 * w
                    if rel >= 0:
                        tm = p.op("vector", (lambda sl, half, rel: lambda e: e.tensor_tensor(
                            ps_s[sl][:, half * 512:(half + 1) * 512],
                            ps_s[sl][:, half * 512:(half + 1) * 512],
                            cm[:, rel * 512:(rel + 1) * 512],
                            op=ALU.add))(sl, half, rel),
                            deps=[t_s1 if half else t_s0, t_cm, tm])
                t_e = p.op("scalar", (lambda sl, h: lambda e: e.activation(
                    out=e_sb[sl][:], in_=ps_s[sl][:], func=AF.Exp,
                    bias=mb[:, h:h + 1], scale=SCALE))(sl, h),
                    deps=[tm, t_mb, t_z_slot[sl]])
                t_e_slot[sl] = t_e
                if pend is not None:
                    t_avlast, _ = emit_avz(*pend)
                pend = (sl, kc0, t_e, nch, hs)
                t_slast = t_s1
            t_avlast, t_z = emit_avz(*pend)
            t_r_tok = p.op("vector", lambda e: e.reciprocal(rrow[:], ps_z[:]),
                           deps=[t_z, t_rb_tok])
            t_rb_tok = p.op("tensor", lambda e: e.matmul(
                ps_r[:], ones_row_f[:], rrow[:], start=True, stop=True,
                skip_group_check=True), deps=[t_r_tok, t_rc_tok])
            t_rc_tok = p.op("vector", lambda e: e.tensor_copy(rbc[:], ps_r[:]),
                            deps=[t_rb_tok, t_ao_tok])
            t_ao_tok = p.op("vector", (lambda w: lambda e: e.tensor_tensor(
                ao[w % 2][:], ps_o[:], rbc[:], op=ALU.mult))(w),
                deps=[t_rc_tok, t_avlast, t_dma_slot[w % 2]])
            t_dma_slot[w % 2] = p.op("sync", (lambda h, w: lambda e: e.dma_start(
                out=dr["a0"][h][:, w * 512:(w + 1) * 512], in_=ao[w % 2][:]))(h, w),
                deps=[t_ao_tok], dma=True, key="ao%d" % (w % 2))
        t_kt_use[hs], t_vt_use[hs], t_qh_use[hs] = t_slast, t_avlast, t_slast
        if h + 2 < 16:
            t_ld[hs] = emit_loads(h + 2, t_slast, t_avlast, t_slast)
    deps["attn0"] = {"a0": [t_dma_slot[0], t_dma_slot[1]]}


def emit_outproj_ln(p, stack, dr, consts, nm, ain, wname, resname, outname,
                    dep_in=None, ain_sbuf=None, ain_sbuf_deps=None):
    nc = p.nc
    if ain_sbuf is None:
        a_sb = [_sb(stack, nc, f"{nm}_a{c}", [128, T]) for c in range(16)]
    else:
        a_sb = ain_sbuf
    wo_sb = [_sb(stack, nc, f"{nm}_w{c}", [128, 2048]) for c in range(16)]
    y_sb = [_sb(stack, nc, f"{nm}_y{c}", [128, 512], F32) for c in range(16)]
    yb = [_sb(stack, nc, f"{nm}_yb{s}", [128, 512]) for s in range(2)]
    sq = [_sb(stack, nc, f"{nm}_sq{s}", [128, 512]) for s in range(2)]
    res = [_sb(stack, nc, f"{nm}_res{s}", [128, 512], F32) for s in range(2)]
    outt = [_sb(stack, nc, f"{nm}_out{s}", [128, 512], F32) for s in range(2)]
    mean_r = _sb(stack, nc, f"{nm}_mean", [1, 512], F32)
    msq_r = _sb(stack, nc, f"{nm}_msq", [1, 512], F32)
    m2_r = _sb(stack, nc, f"{nm}_m2", [1, 512], F32)
    var_r = _sb(stack, nc, f"{nm}_var", [1, 512], F32)
    sd_r = _sb(stack, nc, f"{nm}_sd", [1, 512], F32)
    rstd_r = _sb(stack, nc, f"{nm}_rstd", [1, 512], F32)
    nmean_r = _sb(stack, nc, f"{nm}_nmean", [1, 512], F32)
    mbc = _sb(stack, nc, f"{nm}_mbc", [128, 512], F32)
    rbc = _sb(stack, nc, f"{nm}_rbc", [128, 512], F32)
    ps = [_ps(stack, nc, f"{nm}_ps{s}", [128, 512]) for s in range(2)]
    ps_sum = _ps(stack, nc, f"{nm}_pssum", [1, 512])
    ps_sq = _ps(stack, nc, f"{nm}_pssq", [1, 512])
    ps_mb = _ps(stack, nc, f"{nm}_psmb", [128, 512])
    ps_rb = _ps(stack, nc, f"{nm}_psrb", [128, 512])
    ones128, ones_row_f = consts["ones128"], consts["ones_row_f"]

    dep_in = dep_in or []
    if ain_sbuf is None:
        t_a = [p.op("sync", (lambda c: lambda e: e.dma_start(
            out=a_sb[c][:], in_=ain[c]))(c), deps=dep_in, dma=True, key="oa")
            for c in range(16)]
    else:
        t_a = [ain_sbuf_deps[-1]] * 16
    t_w = [p.op("gpsimd", (lambda c: lambda e: e.dma_start(
        out=wo_sb[c][:], in_=dr[wname][c]))(c), deps=dep_in, dma=True, key="ow")
        for c in range(16)]
    t_res_slot = [None, None]
    t_ps_read = [None, None]
    t_yb_free = [None, None]
    t_sq_free = [None, None]
    t_y_dma = [None] * 16
    t_nm_prev = t_mbm_prev = t_rbm_prev = None
    t_n1_last = t_n2_last = None
    t_stat_read = None
    t_sq_mm = None
    t_mb_cp = None
    t_rb_cp = None
    t_out_dma = [None, None]
    for tt in range(4):
        t_sum_last = t_sq_last = None
        t_yw = [None] * 16
        pend_stat = None

        def emit_stats(oc, t_yb, t_sq):
            nonlocal t_sum_last, t_sq_last
            t_sum_last = p.op("tensor", (lambda oc: lambda e: e.matmul(
                ps_sum[:], ones128[:], yb[oc % 2][:], start=(oc == 0),
                stop=(oc == 15), skip_group_check=True))(oc),
                deps=[t_yb] + ([t_stat_read] if oc == 0 else []))
            t_sq_last = p.op("tensor", (lambda oc: lambda e: e.matmul(
                ps_sq[:], ones128[:], sq[oc % 2][:], start=(oc == 0),
                stop=(oc == 15), skip_group_check=True))(oc),
                deps=[t_sq] + ([t_stat_read] if oc == 0 else []))

        for oc in range(16):
            pslot = oc % 2
            rslot = oc % 2
            t_r = p.op("sync", (lambda oc, tt, rslot: lambda e: e.dma_start(
                out=res[rslot][:],
                in_=dr[resname][oc * 128:(oc + 1) * 128, tt * 512:(tt + 1) * 512]))(oc, tt, rslot),
                deps=[t_res_slot[rslot]], dma=True, key="res%d" % rslot)
            last_mm = None
            for c in range(16):
                last_mm = p.op("tensor", (lambda oc, c, tt, pslot: lambda e: e.matmul(
                    ps[pslot][:], wo_sb[oc][:, c * 128:(c + 1) * 128],
                    a_sb[c][:, tt * 512:(tt + 1) * 512],
                    start=(c == 0), stop=(c == 15), skip_group_check=True))(oc, c, tt, pslot),
                    deps=([t_w[15], t_a[15], t_ps_read[pslot]]
                          if c == 0 else []))
            t_y = p.op("vector", (lambda oc, rslot, pslot: lambda e: e.tensor_add(
                y_sb[oc][:], ps[pslot][:], res[rslot][:]))(oc, rslot, pslot),
                deps=[last_mm, t_r, t_y_dma[oc]])
            t_yw[oc] = t_y
            t_ps_read[pslot] = t_y
            t_res_slot[rslot] = t_y
            t_yb = p.op("scalar", (lambda oc: lambda e: e.copy(
                yb[oc % 2][:], y_sb[oc][:]))(oc),
                deps=[t_y, t_yb_free[oc % 2]])
            t_sq = p.op("scalar", (lambda oc: lambda e: e.activation(
                out=sq[oc % 2][:], in_=y_sb[oc][:], func=AF.Square))(oc),
                deps=[t_y, t_sq_free[oc % 2]])
            if pend_stat is not None:
                emit_stats(*pend_stat)
                t_yb_free[(oc - 1) % 2] = t_sum_last
                t_sq_free[(oc - 1) % 2] = t_sq_last
            pend_stat = (oc, t_yb, t_sq)
        emit_stats(*pend_stat)
        t_yb_free[15 % 2] = t_sum_last
        t_sq_free[15 % 2] = t_sq_last
        t_mean = p.op("vector", lambda e: e.tensor_scalar(
            out=mean_r[:], in0=ps_sum[:], scalar1=1.0 / D, scalar2=None,
            op0=ALU.mult), deps=[t_sum_last, t_nm_prev])
        t_msq = p.op("vector", lambda e: e.tensor_scalar(
            out=msq_r[:], in0=ps_sq[:], scalar1=1.0 / D, scalar2=None,
            op0=ALU.mult), deps=[t_sq_last])
        t_stat_read = t_msq
        t_m2 = p.op("scalar", lambda e: e.activation(
            out=m2_r[:], in_=mean_r[:], func=AF.Square), deps=[t_mean])
        t_var = p.op("vector", lambda e: e.tensor_sub(
            var_r[:], msq_r[:], m2_r[:]), deps=[t_m2, t_msq])
        t_sd = p.op("scalar", lambda e: e.activation(
            out=sd_r[:], in_=var_r[:], func=AF.Sqrt,
            bias=consts["epsln"][:]), deps=[t_var] + consts["tok"])
        t_rstd = p.op("vector", lambda e: e.reciprocal(rstd_r[:], sd_r[:]),
                      deps=[t_sd, t_rbm_prev])
        t_nm = p.op("scalar", lambda e: e.mul(nmean_r[:], mean_r[:], -1.0),
                    deps=[t_mean, t_mbm_prev])
        t_nm_prev = t_nm
        t_mbm = p.op("tensor", lambda e: e.matmul(
            ps_mb[:], ones_row_f[:], nmean_r[:], start=True, stop=True,
            skip_group_check=True), deps=[t_nm, t_mb_cp])
        t_mbm_prev = t_mbm
        t_mb_cp = p.op("scalar", lambda e: e.copy(mbc[:], ps_mb[:]),
                       deps=[t_mbm, t_n1_last])
        t_rbm = p.op("tensor", lambda e: e.matmul(
            ps_rb[:], ones_row_f[:], rstd_r[:], start=True, stop=True,
            skip_group_check=True), deps=[t_rstd, t_rb_cp])
        t_rbm_prev = t_rbm
        t_rb_cp = p.op("scalar", lambda e: e.copy(rbc[:], ps_rb[:]),
                       deps=[t_rbm, t_n2_last])
        for oc in range(16):
            oslot = oc % 2
            t_n1 = p.op("vector", (lambda oc: lambda e: e.tensor_add(
                y_sb[oc][:], y_sb[oc][:], mbc[:]))(oc), deps=[t_mb_cp, t_yw[oc]])
            t_n2 = p.op("vector", (lambda oc, oslot: lambda e: e.tensor_mul(
                outt[oslot][:], y_sb[oc][:], rbc[:]))(oc, oslot),
                deps=[t_n1, t_rb_cp, t_out_dma[oslot]])
            t_y_dma[oc] = p.op("sync", (lambda oc, tt, oslot: lambda e: e.dma_start(
                out=dr[outname][oc * 128:(oc + 1) * 128, tt * 512:(tt + 1) * 512],
                in_=outt[oslot][:]))(oc, tt, oslot), deps=[t_n2], dma=True,
                key="lo%d" % oslot)
            t_out_dma[oslot] = t_y_dma[oc]
            t_n1_last, t_n2_last = t_n1, t_n2
    return {outname: [t_y_dma[oc] for oc in range(16)]}


def phase_out0(p, stack, dr, consts, deps):
    deps["out0"] = emit_outproj_ln(p, stack, dr, consts, "o0",
                                   dr["a0"], "wo0", "x0", "r1")


_PHASE_FNS["out0"] = phase_out0


def emit_ln_tail(p, dr, consts, bufs, y_sb, outname, tt, prev):
    ones128_f, ones_row_f = consts["ones128_f"], consts["ones_row_f"]
    sqf, mean_r, msq_r, m2_r, var_r, sd_r, rstd_r, nmean_r, mbc, rbc, outt, \
        ps_sum, ps_sq, ps_mb, ps_rb = (
            bufs["sqf"], bufs["mean"], bufs["msq"], bufs["m2"], bufs["var"],
            bufs["sd"], bufs["rstd"], bufs["nmean"], bufs["mbc"], bufs["rbc"],
            bufs["outt"], bufs["ps_sum"], bufs["ps_sq"], bufs["ps_mb"],
            bufs["ps_rb"])
    y_sb = [a if isinstance(a, bass.AP) else a[:] for a in y_sb]
    ps_sum = ps_sum if isinstance(ps_sum, bass.AP) else ps_sum[:]
    ps_sq = ps_sq if isinstance(ps_sq, bass.AP) else ps_sq[:]
    t_sum = t_sq_mm = None
    for oc in range(16):
        t_sum = p.op("tensor", (lambda oc: lambda e: e.matmul(
            ps_sum, ones128_f[:], y_sb[oc], start=(oc == 0),
            stop=(oc == 15), skip_group_check=True))(oc),
            deps=[prev.get("y%d" % oc), prev.get("stat_read")]
                 + consts["tok"])
        t_sq = p.op("scalar", (lambda oc: lambda e: e.activation(
            out=sqf[:], in_=y_sb[oc], func=AF.Square))(oc),
            deps=[prev.get("y%d" % oc), t_sq_mm])
        t_sq_mm = p.op("tensor", (lambda oc: lambda e: e.matmul(
            ps_sq, ones128_f[:], sqf[:], start=(oc == 0), stop=(oc == 15),
            skip_group_check=True))(oc),
            deps=[t_sq] + ([prev.get("stat_read")] if oc == 0 else []))
    t_mean = p.op("vector", lambda e: e.tensor_scalar(
        out=mean_r[:], in0=ps_sum, scalar1=1.0 / D, scalar2=None,
        op0=ALU.mult), deps=[t_sum, prev.get("nm")])
    t_msq = p.op("vector", lambda e: e.tensor_scalar(
        out=msq_r[:], in0=ps_sq, scalar1=1.0 / D, scalar2=None,
        op0=ALU.mult), deps=[t_sq_mm])
    prev["stat_read"] = t_msq
    t_m2 = p.op("scalar", lambda e: e.activation(
        out=m2_r[:], in_=mean_r[:], func=AF.Square), deps=[t_mean])
    t_var = p.op("vector", lambda e: e.tensor_sub(
        var_r[:], msq_r[:], m2_r[:]), deps=[t_m2, t_msq])
    t_sd = p.op("scalar", lambda e: e.activation(
        out=sd_r[:], in_=var_r[:], func=AF.Sqrt,
        bias=consts["epsln"][:]), deps=[t_var])
    t_rstd = p.op("vector", lambda e: e.reciprocal(rstd_r[:], sd_r[:]),
                  deps=[t_sd, prev.get("rbm")])
    t_nm = p.op("scalar", lambda e: e.mul(nmean_r[:], mean_r[:], -1.0),
                deps=[t_mean, prev.get("mbm")])
    prev["nm"] = t_nm
    t_mbm = p.op("tensor", lambda e: e.matmul(
        ps_mb[:], ones_row_f[:], nmean_r[:], start=True, stop=True,
        skip_group_check=True), deps=[t_nm, prev.get("mb_cp")])
    prev["mbm"] = t_mbm
    t_mb_cp = p.op("scalar", lambda e: e.copy(mbc[:], ps_mb[:]),
                   deps=[t_mbm, prev.get("n1_last")])
    prev["mb_cp"] = t_mb_cp
    t_rbm = p.op("tensor", lambda e: e.matmul(
        ps_rb[:], ones_row_f[:], rstd_r[:], start=True, stop=True,
        skip_group_check=True), deps=[t_rstd, prev.get("rb_cp")])
    prev["rbm"] = t_rbm
    t_rb_cp = p.op("scalar", lambda e: e.copy(rbc[:], ps_rb[:]),
                   deps=[t_rbm, prev.get("n2_last")])
    prev["rb_cp"] = t_rb_cp
    outs = []
    for oc in range(16):
        oslot = oc % 2
        t_n1 = p.op("vector", (lambda oc: lambda e: e.tensor_add(
            y_sb[oc], y_sb[oc], mbc[:]))(oc), deps=[t_mb_cp])
        t_n2 = p.op("vector", (lambda oc, oslot: lambda e: e.tensor_mul(
            outt[oslot][:], y_sb[oc], rbc[:]))(oc, oslot),
            deps=[t_n1, t_rb_cp, prev.get("odma%d" % oslot)])
        t_dma = p.op("sync", (lambda oc, tt, oslot: lambda e: e.dma_start(
            out=dr[outname][oc * 128:(oc + 1) * 128, tt * 512:(tt + 1) * 512],
            in_=outt[oslot][:]))(oc, tt, oslot), deps=[t_n2], dma=True,
            key="lo%d" % oslot)
        prev["odma%d" % oslot] = t_dma
        prev["yfree%d" % oc] = t_n2
        outs.append(t_dma)
        prev["n1_last"], prev["n2_last"] = t_n1, t_n2
    return outs


def mk_ln_bufs(p, stack, nm):
    nc = p.nc
    return {
        "sqf": _sb(stack, nc, nm + "_sqf", [128, 512], F32),
        "mean": _sb(stack, nc, nm + "_mean", [1, 512], F32),
        "msq": _sb(stack, nc, nm + "_msq", [1, 512], F32),
        "m2": _sb(stack, nc, nm + "_m2", [1, 512], F32),
        "var": _sb(stack, nc, nm + "_var", [1, 512], F32),
        "sd": _sb(stack, nc, nm + "_sd", [1, 512], F32),
        "rstd": _sb(stack, nc, nm + "_rstd", [1, 512], F32),
        "nmean": _sb(stack, nc, nm + "_nmean", [1, 512], F32),
        "mbc": _sb(stack, nc, nm + "_mbc", [128, 512], F32),
        "rbc": _sb(stack, nc, nm + "_rbc", [128, 512], F32),
        "outt": [_sb(stack, nc, nm + "_outt%d" % s, [128, 512], F32)
                 for s in range(2)],
        "ps_sum": _ps(stack, nc, nm + "_pssum", [1, 512]),
        "ps_sq": _ps(stack, nc, nm + "_pssq", [1, 512]),
        "ps_mb": _ps(stack, nc, nm + "_psmb", [128, 512]),
        "ps_rb": _ps(stack, nc, nm + "_psrb", [128, 512]),
    }


_MOE_TT = 2

def phase_moe0(p, stack, dr, consts, deps):
    nc = p.nc
    nm = "m0"
    W = 1024
    h_sb = [_sb(stack, nc, f"{nm}_h{c}", [128, W]) for c in range(16)]
    acc = [_sb(stack, nc, f"{nm}_acc{c}", [128, W], F32) for c in range(16)]
    r1t = [_sb(stack, nc, f"{nm}_r1t{s}", [128, W], F32) for s in range(2)]
    wg_sb = [_sb(stack, nc, f"{nm}_wg{s}", [128, 2048]) for s in range(2)]
    wu_sb = [_sb(stack, nc, f"{nm}_wu{s}", [128, 2048]) for s in range(2)]
    wd_sb = [_sb(stack, nc, f"{nm}_wd{s}", [128, 1024]) for s in range(2)]
    rmsw = _sb(stack, nc, f"{nm}_rmsw", [128, 16], F32)
    rwf = _sb(stack, nc, f"{nm}_rwf", [128, 128], F32)
    sqf = _sb(stack, nc, f"{nm}_sqf", [128, W], F32)
    rstd_r = _sb(stack, nc, f"{nm}_rstd", [1, W], F32)
    sd_r = _sb(stack, nc, f"{nm}_sd", [1, W], F32)
    rbc = _sb(stack, nc, f"{nm}_rbc", [128, W], F32)
    lgf = _sb(stack, nc, f"{nm}_lgf", [8, W], F32)
    lgt = _sb(stack, nc, f"{nm}_lgt", [128, 64], F32)
    m1 = _sb(stack, nc, f"{nm}_m1", [128, 1], F32)
    nm1 = _sb(stack, nc, f"{nm}_nm1", [128, 1], F32)
    l2 = _sb(stack, nc, f"{nm}_l2", [128, 8], F32)
    m2 = _sb(stack, nc, f"{nm}_m2", [128, 1], F32)
    ee = _sb(stack, nc, f"{nm}_ee", [128, 8], F32)
    e2 = _sb(stack, nc, f"{nm}_e2", [128, 1], F32)
    den = _sb(stack, nc, f"{nm}_den", [128, 1], F32)
    rden = _sb(stack, nc, f"{nm}_rden", [128, 1], F32)
    ge2 = _sb(stack, nc, f"{nm}_ge2", [128, 8], F32)
    wdt = _sb(stack, nc, f"{nm}_wdt", [128, 64], F32)
    wdt16 = _sb(stack, nc, f"{nm}_wdt16", [128, 8])
    wdr = _sb(stack, nc, f"{nm}_wdr", [8, W], F32)
    wrow = _sb(stack, nc, f"{nm}_wrow", [1, W], F32)
    wbc = _sb(stack, nc, f"{nm}_wbc", [128, W], F32)
    sg = _sb(stack, nc, f"{nm}_sg", [128, 512])
    a1 = _sb(stack, nc, f"{nm}_a1", [128, 512])
    ae = [_sb(stack, nc, f"{nm}_ae{f}", [128, W]) for f in range(8)]
    ps_g = _ps(stack, nc, f"{nm}_psg", [128, 512])
    ps_u = _ps(stack, nc, f"{nm}_psu", [128, 512])
    ps_d = [_ps(stack, nc, f"{nm}_psd{s}", [128, 512]) for s in range(2)]
    ps_rows = _ps(stack, nc, f"{nm}_psrows", [128, 512])
    ps_lg = _ps(stack, nc, f"{nm}_pslg", [128, 512])
    ps_t = _ps(stack, nc, f"{nm}_pst", [128, 128])
    ps_wb = _ps(stack, nc, f"{nm}_pswb", [128, 512])
    ps_t_bf = ps_t.bitcast(BF16)
    lnb = {"sqf": _sb(stack, nc, nm + "_lnsqf", [128, 512], F32),
           "mean": _sb(stack, nc, nm + "_lnmean", [1, 512], F32),
           "msq": _sb(stack, nc, nm + "_lnmsq", [1, 512], F32),
           "m2": _sb(stack, nc, nm + "_lnm2", [1, 512], F32),
           "var": _sb(stack, nc, nm + "_lnvar", [1, 512], F32),
           "sd": _sb(stack, nc, nm + "_lnsd", [1, 512], F32),
           "rstd": _sb(stack, nc, nm + "_lnrstd", [1, 512], F32),
           "nmean": _sb(stack, nc, nm + "_lnnmean", [1, 512], F32),
           "mbc": _sb(stack, nc, nm + "_lnmbc", [128, 512], F32),
           "rbc": _sb(stack, nc, nm + "_lnrbc", [128, 512], F32),
           "outt": [_sb(stack, nc, nm + "_lnoutt%d" % s, [128, 512], F32)
                    for s in range(2)],
           "ps_sum": ps_rows[0:1, :], "ps_sq": ps_lg[0:1, :],
           "ps_mb": ps_g, "ps_rb": ps_u}
    ones128_f, ones_row_f = consts["ones128_f"], consts["ones_row_f"]
    wdr_dram = nc.dram_tensor("m0_wdr_scratch", [2, 8, W], F32,
                              kind="Internal").ap()

    t_rmsw = p.op("sync", lambda e: e.dma_start(out=rmsw[:], in_=dr["rms0"]),
                  dma=True, key="rmsw")
    t_rw0 = p.op("sync", lambda e: e.dma_start(out=rwf[:], in_=dr["router0"]),
                 dma=True, key="rw")
    t_rwp = t_rw0
    for c in range(16):
        t_rwp = p.op("vector", (lambda c: lambda e: e.tensor_scalar(
            out=rwf[:, c * 8:(c + 1) * 8], in0=rwf[:, c * 8:(c + 1) * 8],
            scalar1=rmsw[:, c:c + 1], scalar2=None, op0=ALU.mult))(c),
            deps=[t_rwp, t_rmsw])
    t_rw = t_rwp
    dep_r1 = (deps.get("out0") or {}).get("r1") or []
    pv = {}
    t_wg_use = [None, None]
    t_wu_use = [None, None]
    t_wd_use = [None, None]
    x1_outs = None
    for tt in range(_MOE_TT):
        c0 = tt * W
        t_sq_mm = [None, None]
        t_lgm = [None, None]
        t_sq = None
        for c in range(16):
            rs = c % 2
            t_l = p.op("sync", (lambda c, tt, rs: lambda e: e.dma_start(
                out=r1t[rs][:],
                in_=dr["r1"][c * 128:(c + 1) * 128, tt * W:(tt + 1) * W]))(c, tt, rs),
                deps=dep_r1 + [pv.get("r1afree%d" % rs)], dma=True,
                key="r1a%d" % rs)
            t_sq = p.op("scalar", (lambda rs: lambda e: e.activation(
                out=sqf[:], in_=r1t[rs][:], func=AF.Square))(rs),
                deps=[t_l, t_sq_mm[1]])
            for sub in range(2):
                t_sq_mm[sub] = p.op("tensor", (lambda c, sub: lambda e: e.matmul(
                    ps_rows[32 * sub:32 * sub + 1, :], ones128_f[:],
                    sqf[:, sub * 512:(sub + 1) * 512], start=(c == 0),
                    stop=(c == 15), tile_position=(0, 32 * sub),
                    skip_group_check=True))(c, sub),
                    deps=[t_sq] + ([pv.get("rows_free")] if c == 0 else []))
            for sub in range(2):
                t_lgm[sub] = p.op("tensor", (lambda c, rs, sub: lambda e: e.matmul(
                    ps_lg[32 * sub:32 * sub + 8, :], rwf[:, c * 8:(c + 1) * 8],
                    r1t[rs][:, sub * 512:(sub + 1) * 512], start=(c == 0),
                    stop=(c == 15), tile_position=(0, 32 * sub),
                    skip_group_check=True))(c, rs, sub),
                    deps=[t_rw] + ([pv.get("lg_free")] if c == 0 else []))
            pv["r1afree%d" % rs] = t_lgm[1]
        t_ms = None
        for sub in range(2):
            t_ms = p.op("vector", (lambda sub: lambda e: e.tensor_scalar(
                out=rstd_r[:, sub * 512:(sub + 1) * 512],
                in0=ps_rows[32 * sub:32 * sub + 1, :], scalar1=1.0 / D,
                scalar2=None, op0=ALU.mult))(sub),
                deps=[t_sq_mm[sub], pv.get("sd"), t_ms])
        pv["rows_free"] = t_ms
        t_sd = p.op("scalar", lambda e: e.activation(
            out=sd_r[:], in_=rstd_r[:], func=AF.Sqrt,
            bias=consts["epsrms"][:]), deps=[t_ms])
        pv["sd"] = t_sd
        t_rr = p.op("vector", lambda e: e.reciprocal(rstd_r[:], sd_r[:]),
                    deps=[t_sd])
        t_rbc = None
        for sub in range(2):
            t_wbm = p.op("tensor", (lambda sub: lambda e: e.matmul(
                ps_wb[:], ones_row_f[:], rstd_r[:, sub * 512:(sub + 1) * 512],
                start=True, stop=True, skip_group_check=True))(sub),
                deps=[t_rr, pv.get("wb_read")])
            t_rbc = p.op("scalar", (lambda sub: lambda e: e.copy(
                rbc[:, sub * 512:(sub + 1) * 512], ps_wb[:]))(sub),
                deps=[t_wbm, pv.get("rbc_free"), t_rbc])
            pv["wb_read"] = t_rbc
        t_lgf = None
        for sub in range(2):
            t_lgf = p.op("vector", (lambda sub: lambda e: e.tensor_mul(
                lgf[:, sub * 512:(sub + 1) * 512], ps_lg[32 * sub:32 * sub + 8, :],
                rbc[0:8, sub * 512:(sub + 1) * 512]))(sub),
                deps=[t_lgm[sub], t_rbc, pv.get("lgf_free"), t_lgf])
        pv["lg_free"] = t_lgf
        t_trprev = pv.get("pst_free")
        for q in range(8):
            t_tr = p.op("tensor", (lambda q: lambda e: e.transpose(
                ps_t[0:128, 0:8], lgf[:, q * 128:(q + 1) * 128],
                consts["ident_f"][0:8, 0:8]))(q),
                deps=[t_lgf, t_trprev])
            t_cp = p.op("scalar", (lambda q: lambda e: e.copy(
                lgt[:, q * 8:(q + 1) * 8], ps_t[0:128, 0:8]))(q),
                deps=[t_tr, pv.get("lgt_free%d" % q)])
            t_trprev = t_cp
            pv["lgt_cp%d" % q] = t_cp
        pv["lgf_free"] = t_trprev
        for q in range(8):
            sl = slice(q * 8, (q + 1) * 8)
            d0 = [pv["lgt_cp%d" % q], pv.get("wdt_free")]
            a = p.op("vector", (lambda sl: lambda e: e.reduce_max(
                m1[:], lgt[:, sl], axis=AXX))(sl), deps=d0)
            b = p.op("vector", lambda e: e.tensor_scalar(
                out=nm1[:], in0=m1[:], scalar1=-1.0, scalar2=None,
                op0=ALU.mult), deps=[a])
            c1 = p.op("vector", (lambda sl: lambda e: e.tensor_single_scalar(
                out=l2[:], in_=lgt[:, sl], scalar=m1[:], op=ALU.is_ge))(sl),
                deps=[b])
            d1 = p.op("vector", (lambda sl: lambda e: e.scalar_tensor_tensor(
                out=l2[:], in0=l2[:], scalar=NEG, in1=lgt[:, sl],
                op0=ALU.mult, op1=ALU.add))(sl), deps=[c1])
            f1 = p.op("vector", lambda e: e.reduce_max(m2[:], l2[:], axis=AXX),
                      deps=[d1])
            g1 = p.op("scalar", (lambda sl: lambda e: e.activation(
                out=ee[:], in_=lgt[:, sl], func=AF.Exp, bias=nm1[:]))(sl),
                deps=[f1])
            g2 = p.op("scalar", lambda e: e.activation(
                out=e2[:], in_=m2[:], func=AF.Exp, bias=nm1[:]), deps=[g1])
            g3 = p.op("vector", lambda e: e.tensor_scalar(
                out=den[:], in0=e2[:], scalar1=1.0, scalar2=None,
                op0=ALU.add), deps=[g2])
            g4 = p.op("vector", lambda e: e.reciprocal(rden[:], den[:]),
                      deps=[g3])
            g5 = p.op("vector", (lambda sl: lambda e: e.tensor_single_scalar(
                out=ge2[:], in_=lgt[:, sl], scalar=m2[:], op=ALU.is_ge))(sl),
                deps=[g4])
            g6 = p.op("vector", lambda e: e.tensor_mul(ge2[:], ge2[:], ee[:]),
                      deps=[g5])
            g7 = p.op("vector", (lambda q: lambda e: e.tensor_scalar(
                out=wdt[:, q * 8:(q + 1) * 8], in0=ge2[:], scalar1=rden[:],
                scalar2=None, op0=ALU.mult))(q), deps=[g6])
            pv["top2_%d" % q] = g7
            pv["lgt_free%d" % q] = g7
        t_wcp = None
        for q in range(8):
            t_c16 = p.op("scalar", (lambda q: lambda e: e.copy(
                wdt16[:], wdt[:, q * 8:(q + 1) * 8]))(q),
                deps=[pv["top2_%d" % q], pv.get("wdt16_free")])
            t_wtr = p.op("tensor", (lambda q: lambda e: e.transpose(
                ps_t_bf[0:8, 0:128], wdt16[:], consts["ident"][:]))(q),
                deps=[t_c16, t_wcp if t_wcp else t_trprev])
            pv["wdt16_free"] = t_wtr
            t_wcp = p.op("scalar", (lambda q: lambda e: e.copy(
                wdr[:, q * 128:(q + 1) * 128], ps_t_bf[0:8, 0:128]))(q),
                deps=[t_wtr, pv.get("wdr_free")])
        pv["pst_free"] = t_wcp
        pv["wdt_free"] = t_wcp
        t_wdma = p.op("sync", (lambda tt: lambda e: e.dma_start(
            out=wdr_dram[tt], in_=wdr[:]))(tt), deps=[t_wcp], dma=True,
            key="wdr")
        pv["wdr_free"] = t_wdma
        t_h = []
        for c in range(16):
            rs = c % 2
            t_l = p.op("sync", (lambda c, tt, rs: lambda e: e.dma_start(
                out=r1t[rs][:],
                in_=dr["r1"][c * 128:(c + 1) * 128, tt * W:(tt + 1) * W]))(c, tt, rs),
                deps=[pv.get("r1bfree%d" % rs)], dma=True, key="r1a%d" % rs)
            t_hc = p.op("vector", (lambda c, rs: lambda e: e.scalar_tensor_tensor(
                out=h_sb[c][:], in0=r1t[rs][:], scalar=rmsw[:, c:c + 1],
                in1=rbc[:], op0=ALU.mult, op1=ALU.mult))(c, rs),
                deps=[t_rbc, t_l, t_rmsw, pv.get("hfree%d" % c)])
            pv["r1bfree%d" % rs] = t_hc
            t_h.append(t_hc)
        pv["rbc_free"] = t_h[15]
        for eidx in range(9):
            shared = (eidx == 8)
            if not shared:
                t_wrow = p.op("sync", (lambda tt, eidx: lambda e: e.dma_start(
                    out=wrow[:], in_=wdr_dram[tt, eidx:eidx + 1, :]))(tt, eidx),
                    deps=[t_wdma, pv.get("wrow_free")], dma=True, key="wrow")
                t_wbc = None
                for sub in range(2):
                    t_wbm = p.op("tensor", (lambda sub: lambda e: e.matmul(
                        ps_wb[:], ones_row_f[:], wrow[:, sub * 512:(sub + 1) * 512],
                        start=True, stop=True, skip_group_check=True))(sub),
                        deps=[t_wrow, pv.get("wb_read")])
                    t_wbc = p.op("scalar", (lambda sub: lambda e: e.copy(
                        wbc[:, sub * 512:(sub + 1) * 512], ps_wb[:]))(sub),
                        deps=[t_wbm, pv.get("wbc_free"), t_wbc])
                    pv["wb_read"] = t_wbc
                pv["wrow_free"] = t_wbc
            for fc in range(8):
                ws = fc % 2
                wg_src = dr["swg0"][fc] if shared else dr["wg0"][eidx, fc]
                wu_src = dr["swu0"][fc] if shared else dr["wu0"][eidx, fc]
                t_wgl = p.op("gpsimd", (lambda wg_src, ws: lambda e: e.dma_start(
                    out=wg_sb[ws][:], in_=wg_src))(wg_src, ws),
                    deps=[t_wg_use[ws]], dma=True, key="mwg%d" % ws)
                t_wul = p.op("gpsimd", (lambda wu_src, ws: lambda e: e.dma_start(
                    out=wu_sb[ws][:], in_=wu_src))(wu_src, ws),
                    deps=[t_wu_use[ws]], dma=True, key="mwu%d" % ws)
                for sub in range(2):
                    lastg = lastu = None
                    for c in range(16):
                        lastg = p.op("tensor", (lambda fc, c, ws, sub: lambda e: e.matmul(
                            ps_g[:], wg_sb[ws][:, c * 128:(c + 1) * 128],
                            h_sb[c][:, sub * 512:(sub + 1) * 512],
                            start=(c == 0), stop=(c == 15),
                            skip_group_check=True))(fc, c, ws, sub),
                            deps=([t_wgl, t_h[15], pv.get("psg_free")]
                                  if c == 0 else []))
                    t_sg = p.op("scalar", (lambda: lambda e: e.activation(
                        out=sg[:], in_=ps_g[:], func=AF.Silu))(),
                        deps=[lastg, pv.get("sg_free")])
                    pv["psg_free"] = t_sg
                    for c in range(16):
                        lastu = p.op("tensor", (lambda fc, c, ws, sub: lambda e: e.matmul(
                            ps_u[:], wu_sb[ws][:, c * 128:(c + 1) * 128],
                            h_sb[c][:, sub * 512:(sub + 1) * 512],
                            start=(c == 0), stop=(c == 15),
                            skip_group_check=True))(fc, c, ws, sub),
                            deps=([t_wul, pv.get("psu_free")] if c == 0 else []))
                    t_a1 = p.op("vector", (lambda: lambda e: e.tensor_mul(
                        a1[:], ps_u[:], sg[:]))(),
                        deps=[t_sg, lastu, pv.get("a1_free")])
                    pv["psu_free"] = t_a1
                    pv["sg_free"] = t_a1
                    if shared:
                        t_ae = p.op("scalar", (lambda fc, sub: lambda e: e.copy(
                            ae[fc][:, sub * 512:(sub + 1) * 512], a1[:]))(fc, sub),
                            deps=[t_a1, pv.get("aefree%d" % fc)])
                    else:
                        t_ae = p.op("vector", (lambda fc, sub: lambda e: e.tensor_mul(
                            ae[fc][:, sub * 512:(sub + 1) * 512], a1[:],
                            wbc[:, sub * 512:(sub + 1) * 512]))(fc, sub),
                            deps=[t_a1, t_wbc, pv.get("aefree%d" % fc)])
                    pv["a1_free"] = t_ae
                    pv["ae%d_%d" % (fc, sub)] = t_ae
                    t_wg_use[ws] = lastg
                    t_wu_use[ws] = lastu
            if not shared:
                pv["wbc_free"] = pv["ae7_1"]
            for oc in range(16):
                ds = oc % 2
                wd_src = dr["swd0"][oc] if shared else dr["wd0"][eidx, oc]
                t_wd = p.op("gpsimd", (lambda wd_src, ds: lambda e: e.dma_start(
                    out=wd_sb[ds][:], in_=wd_src))(wd_src, ds),
                    deps=[t_wd_use[ds]], dma=True, key="mwd%d" % ds)
                for sub in range(2):
                    lastd = None
                    for fc in range(8):
                        lastd = p.op("tensor", (lambda fc, ds, sub: lambda e: e.matmul(
                            ps_d[ds][:], wd_sb[ds][:, fc * 128:(fc + 1) * 128],
                            ae[fc][:, sub * 512:(sub + 1) * 512],
                            start=(fc == 0), stop=(fc == 7),
                            skip_group_check=True))(fc, ds, sub),
                            deps=([t_wd, pv["ae7_1"], pv.get("psd_free%d" % ds)]
                                  if fc == 0 else []))
                    if eidx == 0:
                        t_acc = p.op("vector", (lambda oc, ds, sub: lambda e: e.tensor_copy(
                            acc[oc][:, sub * 512:(sub + 1) * 512], ps_d[ds][:]))(oc, ds, sub),
                            deps=[lastd, pv.get("accfree%d" % oc)])
                    else:
                        t_acc = p.op("vector", (lambda oc, ds, sub: lambda e: e.tensor_add(
                            acc[oc][:, sub * 512:(sub + 1) * 512],
                            acc[oc][:, sub * 512:(sub + 1) * 512], ps_d[ds][:]))(oc, ds, sub),
                            deps=[lastd, pv.get("acc%d" % oc)])
                    pv["psd_free%d" % ds] = t_acc
                    pv["acc%d" % oc] = t_acc
                    t_wd_use[ds] = lastd
            for fc in range(8):
                pv["aefree%d" % fc] = pv["acc15"]
        for oc in range(16):
            rs = oc % 2
            t_l = p.op("sync", (lambda oc, tt, rs: lambda e: e.dma_start(
                out=r1t[rs][:],
                in_=dr["r1"][oc * 128:(oc + 1) * 128, tt * W:(tt + 1) * W]))(oc, tt, rs),
                deps=[pv.get("r1cfree%d" % rs)], dma=True, key="r1a%d" % rs)
            t_y = p.op("vector", (lambda oc, rs: lambda e: e.tensor_add(
                acc[oc][:], acc[oc][:], r1t[rs][:]))(oc, rs),
                deps=[t_l, pv["acc%d" % oc]])
            pv["r1cfree%d" % rs] = t_y
            pv["y%d" % oc] = t_y
        pv["mbm"] = pv.get("psg_free")
        pv["rbm"] = pv.get("psu_free")
        for sub in range(2):
            pv["stat_read"] = pv.get("rows_free") if sub == 0 else pv.get("stat_read")
            ysubs = [acc[oc][:, sub * 512:(sub + 1) * 512] for oc in range(16)]
            for oc in range(16):
                pv["y%d" % oc] = pv["y%d" % oc] if sub == 0 else pv["yprev%d" % oc]
            x1_outs = emit_ln_tail(p, dr, consts, lnb, ysubs, "x1",
                                   2 * tt + sub, pv)
            for oc in range(16):
                pv["yprev%d" % oc] = pv["yfree%d" % oc]
        pv["psg_free"] = pv["mb_cp"]
        pv["psu_free"] = pv["rb_cp"]
        pv["rows_free"] = pv["stat_read"]
        pv["lg_free"] = pv["stat_read"]
        for oc in range(16):
            pv["hfree%d" % oc] = pv["yprev%d" % oc]
            pv["accfree%d" % oc] = pv["yprev%d" % oc]
    deps["moe0"] = {"x1": x1_outs}


_PHASE_FNS["moe0"] = phase_moe0


def emit_mbias(p, stack, dr, consts, L, mb):
    nc = p.nc
    nm = "mb%d" % L
    ident, ones_row = consts["ident"], consts["ones_row"]
    kn = _sb(stack, nc, nm + "_kn", [128, 1024])
    knmax = _sb(stack, nc, nm + "_knmax", [128, 16], F32)
    knmax16 = _sb(stack, nc, nm + "_knmax16", [128, 16])
    qmaxf = _sb(stack, nc, nm + "_qmaxf", [1, 16], F32)
    qmax16 = _sb(stack, nc, nm + "_qmax16", [1, 16])
    smkn = _sb(stack, nc, nm + "_smkn", [16, 1], F32)
    smm = _sb(stack, nc, nm + "_smm", [16, 1], F32)
    smm16 = _sb(stack, nc, nm + "_smm16", [16, 1])
    mrow = _sb(stack, nc, nm + "_mrow", [1, 16])
    with nc.psum_tensor(nm + "_pst", [128, 128], F32) as ps_t:
        ps_t_bf = ps_t.bitcast(BF16)
        t_kn = p.op("sync", (lambda L: lambda e: e.dma_start(
            out=kn[:],
            in_=dr[f"gout_n{L}"].rearrange("(p q) h -> p (q h)", p=128)))(L),
            dma=True, key="kn")
        t1 = p.op("vector", lambda e: e.reduce_max(
            knmax[:], kn[:].rearrange("p (q h) -> p h q", h=16), axis=AXX),
            deps=[t_kn])
        t2 = p.op("scalar", lambda e: e.copy(knmax16[:], knmax[:]), deps=[t1])
        t3 = p.op("tensor", lambda e: e.transpose(
            ps_t_bf[0:16, 0:128], knmax16[:], ident[:]), deps=[t2])
        t4 = p.op("vector", lambda e: e.reduce_max(
            smkn[:], ps_t_bf[0:16, 0:128], axis=AXX), deps=[t3])
        t5 = p.op("sync", (lambda L: lambda e: e.dma_start(
            out=qmaxf[:], in_=dr[f"qmax{L}"]))(L), dma=True, key="qmaxf")
        t6 = p.op("scalar", lambda e: e.copy(qmax16[:], qmaxf[:]), deps=[t5])
        t7 = p.op("tensor", lambda e: e.transpose(
            ps_t_bf[0:16, 0:1], qmax16[:], ident[0:1, 0:1]), deps=[t6, t4])
        t8 = p.op("vector", lambda e: e.tensor_add(
            smm[:], smkn[:], ps_t_bf[0:16, 0:1]), deps=[t7])
        t9 = p.op("vector", lambda e: e.tensor_scalar(
            out=smm[:], in0=smm[:], scalar1=-SCALE / 2.0, scalar2=None,
            op0=ALU.mult), deps=[t8])
        t10 = p.op("scalar", lambda e: e.copy(smm16[:], smm[:]), deps=[t9])
        t11 = p.op("tensor", lambda e: e.transpose(
            ps_t_bf[0:1, 0:16], smm16[:], ident[0:16, 0:16]), deps=[t10, t8])
        t12 = p.op("scalar", lambda e: e.copy(mrow[:], ps_t_bf[0:1, 0:16]),
                   deps=[t11])
        t13 = p.op("tensor", lambda e: e.matmul(
            ps_t[0:128, 0:16], ones_row[:], mrow[:], start=True, stop=True,
            skip_group_check=True), deps=[t12])
        t_mb = p.op("scalar", lambda e: e.copy(mb[:], ps_t[0:128, 0:16]),
                    deps=[t13])
    return t_mb


def phase_cols1(p, stack, dr, consts, deps):
    nc = p.nc
    nm = "c1"
    mb = _sb(stack, nc, nm + "_mb", [128, 16], F32)
    t_mb = emit_mbias(p, stack, dr, consts, 1, mb)
    kt = [_sb(stack, nc, f"{nm}_kt{s}", [128, S]) for s in range(2)]
    qh = [_sb(stack, nc, f"{nm}_qh{s}", [128, T]) for s in range(2)]
    e_all = [_sb(stack, nc, f"{nm}_e{s}", [128, S]) for s in range(2)]
    zacc = [_sb(stack, nc, nm + "_zacc%d" % s, [128, 16], F32) for s in range(2)]
    zrow = _sb(stack, nc, nm + "_z", [128, 1], F32)
    rr = _sb(stack, nc, nm + "_r", [128, 1], F32)
    r16 = _sb(stack, nc, nm + "_r16", [128, 1])
    csall = _sb(stack, nc, nm + "_csall", [16, S], F32)
    cs_tm = [_sb(stack, nc, f"{nm}_cstm{s}", [128, 16], F32) for s in range(2)]
    ps_s = [_ps(stack, nc, f"{nm}_pss{s}", [128, 1024]) for s in range(2)]
    ps_cs = [_ps(stack, nc, f"{nm}_pscs{b}", [128, 512]) for b in range(4)]
    csev = [_sb(stack, nc, f"{nm}_csev{s}", [128, 512], F32) for s in range(2)]
    cs_rows = nc.dram_tensor("c1_cs_rows", [16, S], F32, kind="Internal").ap()

    def bankrow(i):
        return ps_cs[i // 4], 32 * (i % 4)

    def emit_loads(h, d_kt, d_qh):
        s = h % 2
        tk = None
        for b4 in range(4):
            tk = p.op("sync", (lambda h, b4, s: lambda e: e.dma_start_transpose(
                out=kt[s][:, b4 * 2048:(b4 + 1) * 2048],
                in_=dr["gout_kv1"][b4 * 2048:(b4 + 1) * 2048,
                                   h * 128:(h + 1) * 128]))(h, b4, s),
                deps=[d_kt], dma=True, key="kt%d" % s)
        tq = p.op("sync", (lambda h, s: lambda e: e.dma_start(
            out=qh[s][:], in_=dr["qf1"][h]))(h, s), deps=[d_qh], dma=True,
            key="qh%d" % s)
        return tk, tq

    t_ld = [None, None]
    t_ld[0] = emit_loads(0, None, None)
    t_ld[1] = emit_loads(1, None, None)
    t_ms0 = None
    for b in range(4):
        t_ms0 = p.op("vector", (lambda b: lambda e: e.memset(
            ps_cs[b][:], 0.0))(b), deps=[t_ms0])
    t_e_strip = [None, None]
    t_zprev = [None, None]
    t_csev_dma = [None, None]
    t_cs_slot = [None, None]
    t_zred_prev = None
    t_r16_free = None
    t_csr_dma = []
    t_ev_prev = None
    for h in range(16):
        hs = h % 2
        t_kt_h, t_qh_h = t_ld[hs]
        t_slast = None
        pend_cs = None

        def emit_cs(qc, t_r16, t_e):
            t_cs = None
            for ki in range(16):
                bank, row = bankrow(ki)
                t_cs = p.op("tensor", (lambda qc, ki, bank, row: lambda e: e.matmul(
                    bank[row:row + 1, :], r16[:],
                    e_all[qc % 2][:, ki * 512:(ki + 1) * 512],
                    start=(qc == 0), stop=(qc == 15),
                    tile_position=(0, row), skip_group_check=True))(qc, ki, bank, row),
                    deps=[t_r16, t_e, t_ms0]
                         + ([t_ev_prev] if qc == 0 else []))
            t_cs_slot[qc % 2] = t_cs
            return t_cs

        for qc in range(16):
            t_e = None
            for jj in range(8):
                ki0 = 2 * jj
                sl = jj % 2
                t_s = None
                for half in range(2):
                    t_s = p.op("tensor", (lambda hs, qc, ki0, half, sl: lambda e: e.matmul(
                        ps_s[sl][:, half * 512:(half + 1) * 512],
                        qh[hs][:, qc * 128:(qc + 1) * 128],
                        kt[hs][:, (ki0 + half) * 512:(ki0 + half + 1) * 512],
                        start=True, stop=True, skip_group_check=True))(hs, qc, ki0, half, sl),
                        deps=([t_kt_h, t_qh_h, t_e_strip[sl], t_mb]
                              if half == 0 else []))
                t_e = p.op("scalar", (lambda qc, ki0, h, jj, sl: lambda e: e.activation(
                    out=e_all[qc % 2][:, ki0 * 512:(ki0 + 2) * 512], in_=ps_s[sl][:],
                    func=AF.Exp, bias=mb[:, h:h + 1], scale=SCALE,
                    accum_out=zacc[qc % 2][:, jj:jj + 1]))(qc, ki0, h, jj, sl),
                    deps=[t_s, t_cs_slot[qc % 2]]
                         + ([t_zprev[qc % 2]] if ki0 == 0 else []))
                t_e_strip[sl] = t_e
                t_slast = t_s
                if jj == 0 and pend_cs is not None:
                    t_r16_free = emit_cs(*pend_cs)
            t_zred_prev = p.op("vector", (lambda qc: lambda e: e.reduce_sum(
                zrow[:], zacc[qc % 2][:, 0:8], axis=AXX))(qc), deps=[t_e])
            t_zprev[qc % 2] = t_zred_prev
            t_rr = p.op("vector", lambda e: e.reciprocal(rr[:], zrow[:]),
                        deps=[t_zred_prev])
            t_r16 = p.op("vector", lambda e: e.tensor_copy(r16[:], rr[:]),
                         deps=[t_rr, t_r16_free])
            pend_cs = (qc, t_r16, t_e)
        t_r16_free = emit_cs(*pend_cs)
        pend_cs = None
        t_ev = None
        t_last_dma = None
        for b in range(4):
            es = b % 2
            t_ev = p.op("vector", (lambda b, es: lambda e: e.tensor_copy(
                csev[es][:], ps_cs[b][:]))(b, es),
                deps=[t_cs_slot[0], t_cs_slot[1], t_ev, t_csev_dma[es]])
            for ri in range(4):
                ki = b * 4 + ri
                t_last_dma = p.op("sync", (lambda h, ki, ri, es: lambda e: e.dma_start(
                    out=cs_rows[h:h + 1, ki * 512:(ki + 1) * 512],
                    in_=csev[es][32 * ri:32 * ri + 1, :]))(h, ki, ri, es),
                    deps=[t_ev], dma=True, key="csr%d" % es)
            t_csev_dma[es] = t_last_dma
        t_ev_prev = t_ev
        t_csr_dma.append(t_last_dma)
        if h + 2 < 16:
            t_ld[hs] = emit_loads(h + 2, t_slast, t_slast)
    t_ld2 = p.op("sync", lambda e: e.dma_start(out=csall[:], in_=cs_rows[:]),
                 deps=[t_csev_dma[0], t_csev_dma[1]], dma=True, key="csall")
    t_tm_dma = [None, None]
    t_tr_prev = t_slast
    for c64 in range(64):
        s = c64 % 2
        t_tr = p.op("tensor", (lambda c64: lambda e: e.transpose(
            ps_s[0][0:128, 0:16], csall[0:16, c64 * 128:(c64 + 1) * 128],
            consts["ident_f"][0:16, 0:16]))(c64),
            deps=[t_ld2, t_e_strip[0], t_e_strip[1], t_tr_prev,
                  t_zred_prev])
        t_cp = p.op("vector", (lambda c64, s: lambda e: e.tensor_copy(
            cs_tm[s][:], ps_s[0][0:128, 0:16]))(c64, s),
            deps=[t_tr, t_tm_dma[s]])
        t_tr_prev = t_cp
        t_tm_dma[s] = p.op("sync", (lambda c64, s: lambda e: e.dma_start(
            out=dr["cs_part"][c64 * 128:(c64 + 1) * 128, :], in_=cs_tm[s][:]))(c64, s),
            deps=[t_cp], dma=True, key="cstm%d" % s)
    deps["cols1"] = {"cs_part": [t_tm_dma[0], t_tm_dma[1]]}


_PHASE_FNS["cols1"] = phase_cols1


def phase_vout1(p, stack, dr, consts, deps, fused=False):
    nc = p.nc
    nm = "v1"
    cs4 = _sb(stack, nc, nm + "_cs4", [128, 256], F32)
    cs_my = _sb(stack, nc, nm + "_csmy", [128, 256], F32)
    v_tm = [_sb(stack, nc, f"{nm}_vtm{s}", [128, 2048]) for s in range(2)]
    vs16 = [_sb(stack, nc, f"{nm}_vs{s}", [128, 128]) for s in range(2)]
    vfm = [_sb(stack, nc, f"{nm}_vfm{h}", [128, T]) for h in range(16)]
    ps_t = [_ps(stack, nc, f"{nm}_pst{s}", [128, 128]) for s in range(2)]
    ident = consts["ident"]

    if "cs_my" in dr:
        t_l = p.op("sync", lambda e: e.dma_start(
            out=cs_my[:].rearrange("p (c h) -> p c h", h=16),
            in_=dr["cs_my"].rearrange("(c p) h -> p c h", p=128)),
            dma=True, key="cs4")
        t_cs = p.op("vector", lambda e: e.tensor_copy(cs4[:], cs_my[:]),
                    deps=[t_l])
        t_cs = p.op("vector", lambda e: e.tensor_copy(cs_my[:], cs4[:]),
                    deps=[t_cs])
    else:
        t_cs = None
        for i in range(4):
            t_l = p.op("sync", (lambda i: lambda e: e.dma_start(
                out=cs4[:].rearrange("p (c h) -> p c h", h=16),
                in_=dr["cs_stack"][i]))(i),
                deps=[t_cs], dma=True, key="cs4")
            if i == 0:
                t_cs = p.op("vector", lambda e: e.tensor_copy(cs_my[:], cs4[:]),
                            deps=[t_l])
            else:
                t_cs = p.op("vector", lambda e: e.tensor_add(
                    cs_my[:], cs_my[:], cs4[:]), deps=[t_l, t_cs])

    t_vfm = []
    t_vload = [None, None]
    t_vs_free = [None, None]
    t_pst_free = [None, None]
    t_cp_prev = None
    for tc in range(16):
        vslot = tc % 2
        t_vl = p.op("sync", (lambda tc, vslot: lambda e: e.dma_start(
            out=v_tm[vslot][:],
            in_=dr["gin_kv1"][tc * 128:(tc + 1) * 128, D:2 * D]))(tc, vslot),
            deps=[t_vload[vslot]], dma=True, key="v%d" % vslot)
        for h in range(16):
            s = h % 2
            t_vs = p.op("vector", (lambda tc, h, vslot, s: lambda e: e.tensor_scalar(
                out=vs16[s][:], in0=v_tm[vslot][:, h * 128:(h + 1) * 128],
                scalar1=cs_my[:, tc * 16 + h:tc * 16 + h + 1], scalar2=None,
                op0=ALU.mult))(tc, h, vslot, s),
                deps=[t_vl, t_cs, t_vs_free[s]])
            t_tr = p.op("tensor", (lambda s: lambda e: e.transpose(
                ps_t[s].bitcast(BF16)[0:128, 0:128], vs16[s][:], ident[:]))(s),
                deps=[t_vs, t_pst_free[s]])
            t_vs_free[s] = t_tr
            t_cp = p.op("scalar", (lambda tc, h, s: lambda e: e.copy(
                vfm[h][:, tc * 128:(tc + 1) * 128],
                ps_t[s].bitcast(BF16)[0:128, 0:128]))(tc, h, s),
                deps=[t_tr])
            t_pst_free[s] = t_cp
            t_cp_prev = t_cp
        t_vload[vslot] = t_cp
    t_vfm = [t_cp_prev]
    deps["vout1"] = emit_outproj_ln(p, stack, dr, consts, "v1o",
                                    None, "wo1", "x1", "r3",
                                    ain_sbuf=vfm, ain_sbuf_deps=t_vfm)


_FFN_TT = 4

def phase_ffn1(p, stack, dr, consts, deps):
    nc = p.nc
    nm = "f1"
    TT = 2048
    ps_g = _ps(stack, nc, f"{nm}_psg", [128, 512])
    ps_u = _ps(stack, nc, f"{nm}_psu", [128, 512])
    ps_d = [_ps(stack, nc, f"{nm}_psd{s}", [128, 512]) for s in range(2)]
    ps_rows = _ps(stack, nc, f"{nm}_psrows", [128, 512])
    ps_sq2 = _ps(stack, nc, f"{nm}_pssq2", [1, 512])
    h_dram = nc.dram_tensor("f1_h", [64, 128, TT], BF16, kind="Internal").ap()
    wd1b = nc.dram_tensor("f1_wd1b", [16, 128, 64, 128], BF16,
                          kind="Internal").ap()
    lnb = {"sqf": _sb(stack, nc, nm + "_lnsqf", [128, 512], F32),
           "mean": _sb(stack, nc, nm + "_lnmean", [1, 512], F32),
           "msq": _sb(stack, nc, nm + "_lnmsq", [1, 512], F32),
           "m2": _sb(stack, nc, nm + "_lnm2", [1, 512], F32),
           "var": _sb(stack, nc, nm + "_lnvar", [1, 512], F32),
           "sd": _sb(stack, nc, nm + "_lnsd", [1, 512], F32),
           "rstd": _sb(stack, nc, nm + "_lnrstd", [1, 512], F32),
           "nmean": _sb(stack, nc, nm + "_lnnmean", [1, 512], F32),
           "mbc": _sb(stack, nc, nm + "_lnmbc", [128, 512], F32),
           "rbc": _sb(stack, nc, nm + "_lnrbc", [128, 512], F32),
           "outt": [_sb(stack, nc, nm + "_lnoutt%d" % s, [128, 512], F32)
                    for s in range(2)],
           "ps_sum": ps_rows[0:1, :], "ps_sq": ps_sq2,
           "ps_mb": ps_g, "ps_rb": ps_u}
    pv = {}
    dep_r3 = (deps.get("vout1") or {}).get("r3") or []
    stackA = contextlib.ExitStack()
    xb = [_sb(stackA, nc, f"{nm}_x{c}", [128, TT]) for c in range(16)]
    wg_sb = [_sb(stackA, nc, f"{nm}_wg{s}", [128, 2048]) for s in range(2)]
    wu_sb = [_sb(stackA, nc, f"{nm}_wu{s}", [128, 2048]) for s in range(2)]
    plane = [_sb(stackA, nc, f"{nm}_pl{m}", [128, TT]) for m in range(4)]
    aa = [_sb(stackA, nc, f"{nm}_aa{m}", [128, TT]) for m in range(4)]
    tmp1 = _sb(stackA, nc, f"{nm}_t1", [128, TT])
    tmp2 = _sb(stackA, nc, f"{nm}_t2", [128, TT])
    tmp3 = _sb(stackA, nc, f"{nm}_t3", [128, TT])
    thr = _sb(stackA, nc, f"{nm}_thr", [128, TT])
    msk = _sb(stackA, nc, f"{nm}_msk", [128, TT])
    sg = _sb(stackA, nc, f"{nm}_sg", [128, 512])

    t_xb = []
    for c in range(16):
        t_xb.append(p.op("gpsimd", (lambda c: lambda e: e.dma_start(
            out=xb[c][:], in_=dr["r3"][c * 128:(c + 1) * 128, :]))(c),
            deps=dep_r3, dma=True, key="xb"))
    t_xb_all = t_xb[15]
    t_cv = []
    tc_prev = None
    for i in range(16):
        tc_prev = p.op("gpsimd", (lambda i: lambda e: e.dma_start(
            out=wd1b[i:i + 1], in_=dr["wd1"][i:i + 1]))(i),
            deps=[tc_prev], dma=True, key="wcvt")
        t_cv.append(tc_prev)
    t_wg_use = [None, None]
    t_wu_use = [None, None]
    for gc in range(16):
        t_pl = [None] * 4
        for m in range(4):
            oc = m * 16 + gc
            ws = oc % 2
            t_wgl = p.op("gpsimd", (lambda oc, ws: lambda e: e.dma_start(
                out=wg_sb[ws][:], in_=dr["wg1"][oc]))(oc, ws),
                deps=[t_wg_use[ws]], dma=True, key="mwg%d" % ws)
            t_wul = p.op("gpsimd", (lambda oc, ws: lambda e: e.dma_start(
                out=wu_sb[ws][:], in_=dr["wu1"][oc]))(oc, ws),
                deps=[t_wu_use[ws]], dma=True, key="mwu%d" % ws)
            lastg = lastu = None
            for sub in range(4):
                for c in range(16):
                    lastg = p.op("tensor", (lambda oc, c, ws, sub: lambda e: e.matmul(
                        ps_g[:], wg_sb[ws][:, c * 128:(c + 1) * 128],
                        xb[c][:, sub * 512:(sub + 1) * 512],
                        start=(c == 0), stop=(c == 15),
                        skip_group_check=True))(oc, c, ws, sub),
                        deps=([t_wgl, t_xb_all, pv.get("psg_free")]
                              if c == 0 else []))
                t_sg = p.op("scalar", (lambda: lambda e: e.activation(
                    out=sg[:], in_=ps_g[:], func=AF.Silu))(),
                    deps=[lastg, pv.get("sg_free")])
                pv["psg_free"] = t_sg
                for c in range(16):
                    lastu = p.op("tensor", (lambda oc, c, ws, sub: lambda e: e.matmul(
                        ps_u[:], wu_sb[ws][:, c * 128:(c + 1) * 128],
                        xb[c][:, sub * 512:(sub + 1) * 512],
                        start=(c == 0), stop=(c == 15),
                        skip_group_check=True))(oc, c, ws, sub),
                        deps=([t_wul, pv.get("psu_free")] if c == 0 else []))
                t_h = p.op("vector", (lambda m, sub: lambda e: e.tensor_mul(
                    plane[m][:, sub * 512:(sub + 1) * 512], ps_u[:], sg[:]))(m, sub),
                    deps=[t_sg, lastu, pv.get("plfree%d" % m)])
                pv["psu_free"] = t_h
                pv["sg_free"] = t_h
                t_pl[m] = t_h
            t_wg_use[ws] = lastg
            t_wu_use[ws] = lastu
        t_ab = []
        for m in range(4):
            t_ab.append(p.op("scalar", (lambda m: lambda e: e.activation(
                out=aa[m][:], in_=plane[m][:], func=AF.Abs))(m),
                deps=[t_pl[m], pv.get("aafree%d" % m)]))
        t1 = p.op("vector", lambda e: e.tensor_max(tmp1[:], aa[0][:], aa[1][:]),
                  deps=t_ab[:2] + [pv.get("t1free")])
        t2 = p.op("vector", lambda e: e.tensor_tensor(
            tmp2[:], aa[0][:], aa[1][:], op=ALU.min),
            deps=t_ab[:2] + [pv.get("t2free")])
        t3 = p.op("vector", lambda e: e.tensor_max(tmp3[:], aa[2][:], aa[3][:]),
                  deps=t_ab[2:] + [pv.get("t3free")])
        t4 = p.op("vector", lambda e: e.tensor_tensor(
            thr[:], aa[2][:], aa[3][:], op=ALU.min),
            deps=t_ab[2:] + [pv.get("thrfree")])
        t5 = p.op("vector", lambda e: e.tensor_tensor(
            tmp1[:], tmp1[:], tmp3[:], op=ALU.min), deps=[t1, t3])
        t6 = p.op("vector", lambda e: e.tensor_max(tmp2[:], tmp2[:], thr[:]),
                  deps=[t2, t4])
        t7 = p.op("vector", lambda e: e.tensor_max(thr[:], tmp1[:], tmp2[:]),
                  deps=[t5, t6])
        pv["t1free"] = t7
        pv["t2free"] = t7
        pv["t3free"] = t7
        t_hd = None
        for m in range(4):
            oc = m * 16 + gc
            t_m1 = p.op("vector", (lambda m: lambda e: e.tensor_tensor(
                msk[:], aa[m][:], thr[:], op=ALU.is_ge))(m),
                deps=[t7, t_ab[m], pv.get("mskfree")])
            t_m2 = p.op("vector", (lambda m: lambda e: e.tensor_mul(
                plane[m][:], plane[m][:], msk[:]))(m), deps=[t_m1])
            pv["mskfree"] = t_m2
            pv["aafree%d" % m] = t_m2
            t_hd = p.op("sync", (lambda m, oc: lambda e: e.dma_start(
                out=h_dram[oc], in_=plane[m][:]))(m, oc),
                deps=[t_m2, pv.get("hdma%d" % m)], dma=True, key="hd%d" % m)
            pv["hdma%d" % m] = t_hd
            pv["plfree%d" % m] = t_hd
        pv["thrfree"] = t_m2
    t_hd_all = [pv["hdma%d" % m] for m in range(4)]
    stackA.close()
    h_all = [_sb(stack, nc, f"{nm}_h{i}", [128, 512]) for i in range(64)]
    wd_sb = [_sb(stack, nc, f"{nm}_wd{s}", [128, 8192]) for s in range(2)]
    r3t = [_sb(stack, nc, f"{nm}_r3t{s}", [128, 512], F32) for s in range(2)]
    y_sb = [_sb(stack, nc, f"{nm}_y{c}", [128, 512], F32) for c in range(16)]
    t_wd_use = [None, None]
    y_outs = None
    for ttd in range(4):
        t_hl = []
        for fc in range(64):
            t_hl.append(p.op("gpsimd", (lambda fc, ttd: lambda e: e.dma_start(
                out=h_all[fc][:],
                in_=h_dram[fc][:, ttd * 512:(ttd + 1) * 512]))(fc, ttd),
                deps=t_hd_all + [pv.get("hafree")], dma=True, key="ha"))
        t_hl_all = t_hl[63]
        for oc in range(16):
            ds = oc % 2
            t_wd = p.op("sync", (lambda oc, ds: lambda e: e.dma_start(
                out=wd_sb[ds][:], in_=wd1b[oc]))(oc, ds),
                deps=[t_wd_use[ds], t_cv[oc]] + t_hd_all, dma=True,
                key="mwd%d" % ds)
            rs = oc % 2
            t_rl = p.op("sync", (lambda oc, ttd, rs: lambda e: e.dma_start(
                out=r3t[rs][:],
                in_=dr["r3"][oc * 128:(oc + 1) * 128,
                             ttd * 512:(ttd + 1) * 512]))(oc, ttd, rs),
                deps=[pv.get("r3free%d" % rs)] + t_hd_all, dma=True,
                key="r3t%d" % rs)
            lastd = None
            for fc in range(64):
                lastd = p.op("tensor", (lambda oc, fc, ds: lambda e: e.matmul(
                    ps_d[ds][:], wd_sb[ds][:, fc * 128:(fc + 1) * 128],
                    h_all[fc][:], start=(fc == 0), stop=(fc == 63),
                    skip_group_check=True))(oc, fc, ds),
                    deps=([t_wd, t_hl_all, pv.get("psd_free%d" % ds)]
                          if fc == 0 else []))
            t_wd_use[ds] = lastd
            t_y = p.op("vector", (lambda oc, ds, rs: lambda e: e.tensor_add(
                y_sb[oc][:], ps_d[ds][:], r3t[rs][:]))(oc, ds, rs),
                deps=[lastd, t_rl, pv.get("ydma%d" % oc)])
            pv["psd_free%d" % ds] = t_y
            pv["r3free%d" % rs] = t_y
            pv["y%d" % oc] = t_y
        pv["hafree"] = pv["y15"]
        pv["mbm"] = pv.get("psg_free")
        pv["rbm"] = pv.get("psu_free")
        y_outs = emit_ln_tail(p, dr, consts, lnb, y_sb, "y", ttd, pv)
        pv["psg_free"] = pv["mb_cp"]
        pv["psu_free"] = pv["rb_cp"]
        for oc in range(16):
            pv["ydma%d" % oc] = pv["yfree%d" % oc]
    deps["ffn1"] = {"y": y_outs}


_PHASE_FNS["vout1"] = phase_vout1
_PHASE_FNS["vout1f"] = phase_vout1
_PHASE_FNS["ffn1"] = phase_ffn1


_PHASE_FNS["attn0"] = phase_attn0


# version 13
# speedup vs baseline: 1.0146x; 1.0146x over previous
import math
import contextlib
import numpy as np
import concourse.bass as bass
import concourse.mybir as mybir
from concourse.bass_utils import run_bass_kernel_spmd

F32 = mybir.dt.float32
BF16 = mybir.dt.bfloat16
AF = mybir.ActivationFunctionType
ALU = mybir.AluOpType
AXX = mybir.AxisListType.X

D = 2048
NH = 16
HD = 128
B = 2
S = 8192
T = 2048
NC = 8
SCALE = 1.0 / math.sqrt(HD)
NEG = -1.0e9
E_MOE = 8
F_MOE = 1024
SF = 1024
FI = 8192
GCOLS = D + D + 16

ENGINES = ("tensor", "scalar", "vector", "gpsimd", "sync")


class Prog:

    def __init__(self, nc):
        self.nc = nc
        self.ops = []
        self.sem_counts = {}
        self.sem_handles = {}
        self.known = {e: {} for e in ENGINES}
        self._sem_names = []
        self.floor = []
        self.fence_exclude = set()

    def _sem(self, name):
        if name not in self.sem_counts:
            self.sem_counts[name] = 0
            self._sem_names.append(name)
        return name

    def op(self, engine, fn, deps=(), dma=False, key=None, no_inc=False):
        if dma:
            semn = self._sem("d_" + (key or engine)); amount = 16
        elif no_inc:
            semn, amount = None, 0
        else:
            semn = self._sem("c_" + engine); amount = 1
        pruned = []
        for tok in list(deps) + self.floor:
            if tok is None:
                continue
            s, v = tok
            if self.known[engine].get(s, 0) >= v:
                continue
            pruned.append((s, v))
            self.known[engine][s] = v
        if semn is not None:
            self.sem_counts[semn] += amount
            tok = (semn, self.sem_counts[semn])
        else:
            tok = None
        self.ops.append((engine, fn, pruned, (semn, amount) if semn else None))
        return tok

    def barrier_all(self):
        return [(s, c) for s, c in self.sem_counts.items() if c > 0]

    def fence(self):
        ex = {"d_" + k for k in self.fence_exclude}
        self.floor = [(s, c) for (s, c) in self.barrier_all() if s not in ex]
        self.fence_exclude = set()

    def build(self, stack):
        nc = self.nc
        for s in self._sem_names:
            self.sem_handles[s] = stack.enter_context(nc.semaphore(s))
        blk = stack.enter_context(nc.Block())
        def mk(engine_name):
            def run(eng):
                for (e, fn, deps, inc) in self.ops:
                    if e != engine_name:
                        continue
                    for (s, v) in deps:
                        eng.wait_ge(self.sem_handles[s], v)
                    ins = fn(eng)
                    if inc is not None:
                        ins.then_inc(self.sem_handles[inc[0]], inc[1])
            return run
        blk.tensor(mk("tensor"))
        blk.scalar(mk("scalar"))
        blk.vector(mk("vector"))
        blk.gpsimd(mk("gpsimd"))
        blk.sync(mk("sync"))



def _tile_lhsT(wT):
    din, dout = wT.shape
    a = wT.reshape(din // 128, 128, dout // 128, 128)
    return np.ascontiguousarray(a.transpose(2, 1, 0, 3))


def _tile_rhs(wT, ocols):
    din, dout = wT.shape
    a = wT.reshape(din // 128, 128, dout // ocols, ocols)
    return np.ascontiguousarray(a.transpose(2, 1, 0, 3))


def _core_token_rows(c):
    g, r = c // 4, c % 4
    return [(g * S + (4 * w + r) * 512, 512) for w in range(4)]


def _agrow_of_block(b):
    return (b % 4) * T + (b // 4) * 512


def _stage_cmask(r):
    rel = np.arange(16)[:, None, None]
    pk = np.arange(128)[None, :, None]
    fq = np.arange(512)[None, None, :]
    keep = (fq - pk + (4 * r - rel) * 128) >= 0
    return np.where(keep, np.float32(0.0), np.float32(NEG))


def stage_inputs(inputs):
    st = {}
    st["l0_wq"] = _tile_lhsT(np.ascontiguousarray(inputs["L0_attn_w_project"].T[:, :D]))
    st["l0_wkv"] = _tile_rhs(np.ascontiguousarray(inputs["L0_attn_w_project"].T[:, D:]), 512)
    st["l0_wo"] = _tile_lhsT(np.ascontiguousarray(inputs["L0_attn_w_out"].T))
    st["l0_rms"] = np.ascontiguousarray(
        inputs["L0_ffn_rms_weight"].reshape(16, 128).T)
    st["l0_router"] = np.ascontiguousarray(
        inputs["L0_ffn_router_w"].reshape(16, 128, 8).transpose(1, 0, 2)
        .reshape(128, 128))
    wg = inputs["L0_ffn_w_gate"]; wu = inputs["L0_ffn_w_up"]; wd = inputs["L0_ffn_w_down"]
    st["l0_wg"] = np.stack([_tile_lhsT(wg[e]) for e in range(E_MOE)])
    st["l0_wu"] = np.stack([_tile_lhsT(wu[e]) for e in range(E_MOE)])
    st["l0_wd"] = np.stack([_tile_lhsT(wd[e]) for e in range(E_MOE)])
    st["l0_swg"] = _tile_lhsT(inputs["L0_ffn_sw_gate"])
    st["l0_swu"] = _tile_lhsT(inputs["L0_ffn_sw_up"])
    st["l0_swd"] = _tile_lhsT(inputs["L0_ffn_sw_down"])
    w1 = inputs["L1_attn_w_qkv"]
    st["l1_wq"] = _tile_lhsT(np.ascontiguousarray(w1[:, :D]))
    st["l1_wkv"] = _tile_rhs(np.ascontiguousarray(w1[:, D:]), 512)
    st["l1_wo"] = _tile_lhsT(np.ascontiguousarray(inputs["L1_attn_w_o"]))
    perm = (4 * (np.arange(FI) % (FI // 4)) + (np.arange(FI) // (FI // 4)))
    st["l1_wg"] = _tile_lhsT(np.ascontiguousarray(inputs["L1_ffn_w_gate"][:, perm]))
    st["l1_wu"] = _tile_lhsT(np.ascontiguousarray(inputs["L1_ffn_w_up"][:, perm]))
    st["l1_wd"] = _tile_lhsT(np.ascontiguousarray(inputs["L1_ffn_w_down"][perm, :]))

    x = np.asarray(inputs["L0_attn_x"], dtype=np.float32).reshape(B * S, D)
    import ml_dtypes
    cores = []
    for c in range(NC):
        rows = _core_token_rows(c)
        xs = np.concatenate([x[s:s + n] for (s, n) in rows], axis=0)
        cores.append({"x_t": np.ascontiguousarray(xs.T),
                      "cmask": _stage_cmask(c % 4).astype(ml_dtypes.bfloat16)})
    st["cid"] = np.eye(128, dtype=np.float32)
    return st, cores


def unstage_output(shards):
    out = np.empty((B * S, D), dtype=np.float32)
    for c, sh in enumerate(shards):
        xs = np.ascontiguousarray(sh.T)
        for i, (s, n) in enumerate(_core_token_rows(c)):
            out[s:s + n] = xs[i * n:(i + 1) * n]
    return out.reshape(B, S, D)



def _sb(stack, nc, name, shape, dt=BF16):
    return stack.enter_context(nc.sbuf_tensor(name, shape, dt))


def _ps(stack, nc, name, shape):
    return stack.enter_context(nc.psum_tensor(name, shape, F32))


def phase_qkv(p, stack, dr, L, consts, x_dep=None):
    nc = p.nc
    name = f"p1_{L}"
    xt = [_sb(stack, nc, f"{name}_x{c}", [128, T]) for c in range(16)]
    wq_sb = [_sb(stack, nc, f"{name}_wq{s}", [128, 2048]) for s in range(2)]
    wkv_sb = [_sb(stack, nc, f"{name}_wkv{s}", [128, 8192]) for s in range(2)]
    qstage = [_sb(stack, nc, f"{name}_qs{s}", [128, 512]) for s in range(2)]
    kvstage = [_sb(stack, nc, f"{name}_kvs{s}", [128, 512]) for s in range(2)]
    sq = [_sb(stack, nc, f"{name}_sq{s}", [128, 512]) for s in range(2)]
    sqf = _sb(stack, nc, f"{name}_sqf", [128, 512], F32)
    qn_row = _sb(stack, nc, f"{name}_qnrow", [1, 2048], F32)
    qmax_row = _sb(stack, nc, f"{name}_qmax", [1, 16], F32)
    knorm_all = _sb(stack, nc, f"{name}_kn", [128, 256], F32)
    psq = [_ps(stack, nc, f"{name}_psq{s}", [128, 512]) for s in range(2)]
    pskv = [_ps(stack, nc, f"{name}_pskv{s}", [128, 512]) for s in range(2)]
    psqn = _ps(stack, nc, f"{name}_psqn", [1, 512])
    ones128 = consts["ones128"]

    t_x = [p.op("gpsimd", (lambda c: lambda e: e.dma_start(
        out=xt[c][:], in_=dr[f"x{L}"][c * 128:(c + 1) * 128, :]))(c),
        deps=[x_dep], dma=True, key="xt") for c in range(16)]
    t_x_all = t_x[15]

    t_wu = [None, None]
    t_qdma = [None, None]
    t_ev = [None, None]
    t_qn_mm = None
    t_qn_ev = None
    t_sq_use = [None, None]
    t_mx = None
    for oc in range(16):
        s = oc % 2
        t_w = p.op("gpsimd", (lambda oc, s: lambda e: e.dma_start(
            out=wq_sb[s][:], in_=dr[f"wq{L}"][oc]))(oc, s),
            deps=[t_wu[s]], dma=True, key="wq%d" % s)
        last_mm = None
        pend_qn = None
        for tt in range(4):
            pslot = tt % 2
            deps = [t_w, t_ev[pslot]] + ([t_x_all] if oc == 0 and tt == 0 else [])
            for c in range(16):
                last_mm = p.op("tensor", (lambda s, c, tt, pslot: lambda e: e.matmul(
                    psq[pslot][:], wq_sb[s][:, c * 128:(c + 1) * 128],
                    xt[c][:, tt * 512:(tt + 1) * 512],
                    start=(c == 0), stop=(c == 15), skip_group_check=True))(s, c, tt, pslot),
                    deps=(deps if c == 0 else []))
            t_cp = p.op("scalar", (lambda pslot: lambda e: e.copy(
                qstage[pslot][:], psq[pslot][:]))(pslot),
                deps=[last_mm, t_qdma[pslot]])
            t_qdma[pslot] = p.op("sync", (lambda oc, tt, pslot: lambda e: e.dma_start(
                out=dr[f"qf{L}"][oc][:, tt * 512:(tt + 1) * 512], in_=qstage[pslot][:]))(oc, tt, pslot),
                deps=[t_cp], dma=True, key="qout%d" % pslot)
            t_sq = p.op("scalar", (lambda pslot: lambda e: e.activation(
                out=sq[pslot][:], in_=psq[pslot][:], func=AF.Square))(pslot),
                deps=[last_mm, t_sq_use[pslot]])
            if pend_qn is not None:
                ptt, psl, psq_tok = pend_qn
                t_qn_mm = p.op("tensor", (lambda psl: lambda e: e.matmul(
                    psqn[:], ones128[:], sq[psl][:], start=True, stop=True,
                    skip_group_check=True))(psl), deps=[psq_tok, t_qn_ev])
                t_sq_use[psl] = t_qn_mm
                t_qn_ev = p.op("scalar", (lambda ptt: lambda e: e.copy(
                    qn_row[:, ptt * 512:(ptt + 1) * 512], psqn[:]))(ptt),
                    deps=[t_qn_mm, t_mx])
                t_ev[psl] = t_qn_ev
            pend_qn = (tt, pslot, t_sq)
        ptt, psl, psq_tok = pend_qn
        t_qn_mm = p.op("tensor", (lambda psl: lambda e: e.matmul(
            psqn[:], ones128[:], sq[psl][:], start=True, stop=True,
            skip_group_check=True))(psl), deps=[psq_tok, t_qn_ev])
        t_sq_use[psl] = t_qn_mm
        t_qn_ev = p.op("scalar", (lambda ptt: lambda e: e.copy(
            qn_row[:, ptt * 512:(ptt + 1) * 512], psqn[:]))(ptt),
            deps=[t_qn_mm, t_mx])
        t_ev[psl] = t_qn_ev
        t_wu[s] = last_mm
        t_mx = p.op("vector", (lambda oc: lambda e: e.reduce_max(
            qmax_row[:, oc:oc + 1], qn_row[:], axis=AXX))(oc), deps=[t_ev[0], t_ev[1]])
    t_qmax_dma = p.op("sync", lambda e: e.dma_start(
        out=dr[f"qmax{L}"], in_=qmax_row[:]), deps=[t_mx], dma=True, key="qmax")

    t_wu = [None, None]
    t_kvdma = [None, None]
    t_norm_red = None
    t_sqk_slot = [None, None]
    t_norm_dma_done = []
    for ft in range(8):
        s = ft % 2
        t_w = p.op("gpsimd", (lambda ft, s: lambda e: e.dma_start(
            out=wkv_sb[s][:], in_=dr[f"wkv{L}"][ft]))(ft, s),
            deps=[t_wu[s]], dma=True, key="wkv%d" % s)
        last_mm = None
        for tc in range(16):
            pslot = tc % 2
            deps = [t_w, t_kvdma[pslot], t_sqk_slot[pslot]]
            for c in range(16):
                last_mm = p.op("tensor", (lambda s, c, tc, pslot: lambda e: e.matmul(
                    pskv[pslot][:], xt[c][:, tc * 128:(tc + 1) * 128],
                    wkv_sb[s][:, c * 512:(c + 1) * 512],
                    start=(c == 0), stop=(c == 15), skip_group_check=True))(s, c, tc, pslot),
                    deps=(deps if c == 0 else []))
            t_cp = p.op("scalar", (lambda pslot: lambda e: e.copy(
                kvstage[pslot][:], pskv[pslot][:]))(pslot),
                deps=[last_mm, t_kvdma[pslot]])
            t_kvdma[pslot] = p.op("sync", (lambda ft, tc, pslot: lambda e: e.dma_start(
                out=dr[f"gin_kv{L}"][tc * 128:(tc + 1) * 128, ft * 512:(ft + 1) * 512],
                in_=kvstage[pslot][:]))(ft, tc, pslot),
                deps=[t_cp], dma=True, key="kvout%d" % pslot)
            if ft < 4:
                t_sqk = p.op("scalar", (lambda pslot: lambda e: e.activation(
                    out=sqf[:], in_=pskv[pslot][:], func=AF.Square))(pslot),
                    deps=[last_mm, t_norm_red])
                t_sqk_slot[pslot] = t_sqk
                t_norm_red = p.op("vector", (lambda ft, tc: lambda e: e.reduce_sum(
                    knorm_all[:, tc * 16 + ft * 4: tc * 16 + ft * 4 + 4],
                    sqf[:].rearrange("p (h d) -> p h d", d=128), axis=AXX))(ft, tc),
                    deps=[t_sqk])
        t_wu[s] = last_mm
    for tc in range(16):
        t_norm_dma_done.append(p.op("gpsimd", (lambda tc: lambda e: e.dma_start(
            out=dr[f"gin_n{L}"][tc * 128:(tc + 1) * 128, :],
            in_=knorm_all[:, tc * 16:(tc + 1) * 16]))(tc),
            deps=[t_norm_red], dma=True, key="ginn"))
    return {"gin": [t_kvdma[0], t_kvdma[1]] + t_norm_dma_done,
            "qf": [t_qdma[0], t_qdma[1]], "qmax": [t_qmax_dma]}



def _tensor_table():
    t = {}
    t["x0"] = ([D, T], np.float32, F32, None)
    for L, wq, wkv in ((0, "l0_wq", "l0_wkv"), (1, "l1_wq", "l1_wkv")):
        t[f"wq{L}"] = ([16, 128, 16, 128], np.float32, F32, None)
        t[f"wkv{L}"] = ([8, 128, 16, 512], np.float32, F32, None)
        t[f"qf{L}"] = ([16, 128, T], np.dtype("uint16"), BF16, f"qkv{L}")
        t[f"gin_kv{L}"] = ([T, 2 * D], np.dtype("uint16"), BF16, f"qkv{L}")
        t[f"gin_n{L}"] = ([T, 16], np.dtype("uint16"), BF16, f"qkv{L}")
        t[f"qmax{L}"] = ([1, 16], np.float32, F32, f"qkv{L}")
        t[f"gout_kv{L}"] = ([S, 2 * D], np.dtype("uint16"), BF16, f"gather{L}")
        t[f"gout_n{L}"] = ([S, 16], np.dtype("uint16"), BF16, f"gather{L}")
    t["cmask"] = ([16, 128, 512], np.dtype("uint16"), BF16, None)
    t["cid"] = ([128, 128], np.float32, F32, None)
    t["wo0"] = ([16, 128, 16, 128], np.float32, F32, None)
    t["a0"] = ([16, 128, T], np.dtype("uint16"), BF16, "attn0")
    t["r1"] = ([D, T], np.float32, F32, "out0")
    t["rms0"] = ([128, 16], np.float32, F32, None)
    t["router0"] = ([128, 128], np.float32, F32, None)
    t["wg0"] = ([E_MOE, 8, 128, 16, 128], np.float32, F32, None)
    t["wu0"] = ([E_MOE, 8, 128, 16, 128], np.float32, F32, None)
    t["wd0"] = ([E_MOE, 16, 128, 8, 128], np.float32, F32, None)
    t["swg0"] = ([8, 128, 16, 128], np.float32, F32, None)
    t["swu0"] = ([8, 128, 16, 128], np.float32, F32, None)
    t["swd0"] = ([16, 128, 8, 128], np.float32, F32, None)
    t["x1"] = ([D, T], np.float32, F32, "moe0")
    t["wo1"] = ([16, 128, 16, 128], np.float32, F32, None)
    t["cs_part"] = ([S, NH], np.float32, F32, "cols1")
    t["cs_stack"] = ([4, 128, 16, NH], np.float32, F32, "csred")
    t["cs_my"] = ([T, NH], np.float32, F32, "csred")
    t["r3"] = ([D, T], np.float32, F32, "vout1")
    t["wg1"] = ([64, 128, 16, 128], np.float32, F32, None)
    t["wu1"] = ([64, 128, 16, 128], np.float32, F32, None)
    t["wd1"] = ([16, 128, 64, 128], np.float32, F32, None)
    t["y"] = ([D, T], np.float32, F32, "ffn1")
    return t


_PHASE_ORDER = ["qkv0", "gather0", "attn0", "out0", "moe0",
                "qkv1", "gather1", "cols1", "csred", "vout1", "ffn1"]

_HOST_STAGE_KEY = {
    "wq0": "l0_wq", "wkv0": "l0_wkv", "wo0": "l0_wo", "rms0": "l0_rms",
    "router0": "l0_router", "wg0": "l0_wg", "wu0": "l0_wu", "wd0": "l0_wd",
    "swg0": "l0_swg", "swu0": "l0_swu", "swd0": "l0_swd",
    "wq1": "l1_wq", "wkv1": "l1_wkv", "wo1": "l1_wo",
    "wg1": "l1_wg", "wu1": "l1_wu", "wd1": "l1_wd",
}


def build_consts(p, stack, dr):
    nc = p.nc
    c = {}
    c["ones128"] = _sb(stack, nc, "c_ones128", [128, 1])
    c["ones_row"] = _sb(stack, nc, "c_onesrow", [1, 128])
    c["ones_row_f"] = _sb(stack, nc, "c_onesrowf", [1, 128], F32)
    c["epsln"] = _sb(stack, nc, "c_epsln", [1, 1], F32)
    c["epsrms"] = _sb(stack, nc, "c_epsrms", [1, 1], F32)
    c["ident"] = _sb(stack, nc, "c_ident", [128, 128])
    c["ident_f"] = _sb(stack, nc, "c_identf", [128, 128], F32)
    c["ones128_f"] = _sb(stack, nc, "c_ones128f", [128, 1], F32)
    toks = []
    toks.append(p.op("vector", lambda e: e.memset(c["ones128"][:], 1.0)))
    toks.append(p.op("vector", lambda e: e.memset(c["ones_row"][:], 1.0)))
    toks.append(p.op("vector", lambda e: e.memset(c["ones_row_f"][:], 1.0)))
    toks.append(p.op("vector", lambda e: e.memset(c["epsln"][:], 1e-5)))
    toks.append(p.op("vector", lambda e: e.memset(c["epsrms"][:], 1e-6)))
    toks.append(p.op("vector", lambda e: e.memset(c["ones128_f"][:], 1.0)))
    toks.append(p.op("gpsimd", lambda e: e.dma_start(
        out=c["ident"][:], in_=dr["cid"]), dma=True, key="cid16"))
    toks.append(p.op("sync", lambda e: e.dma_start(
        out=c["ident_f"][:], in_=dr["cid"]), dma=True, key="cidf"))
    c["tok"] = toks
    return c


PHASE_TENSORS = {
    "qkv0": (["x0", "wq0", "wkv0"], ["qf0", "gin_kv0", "gin_n0", "qmax0"]),
    "gather0": (["gin_kv0", "gin_n0"], ["gout_kv0", "gout_n0"]),
    "attn0": (["qf0", "gout_kv0", "gout_n0", "qmax0", "cmask"], ["a0"]),
    "out0": (["a0", "wo0", "x0"], ["r1"]),
    "moe0": (["r1", "rms0", "router0", "wg0", "wu0", "wd0",
              "swg0", "swu0", "swd0"], ["x1"]),
    "qkv1": (["x1", "wq1", "wkv1"], ["qf1", "gin_kv1", "gin_n1", "qmax1"]),
    "gather1": (["gin_kv1", "gin_n1"], ["gout_kv1", "gout_n1"]),
    "cols1": (["qf1", "gout_kv1", "gout_n1", "qmax1"], ["cs_part"]),
    "csred": (["cs_part"], ["cs_my"]),
    "vout1": (["gin_kv1", "cs_stack", "wo1", "x1", "wg1", "wu1", "wd1"], ["r3"]),
    "vout1f": (["gin_kv1", "cs_my", "wo1", "x1", "wg1", "wu1", "wd1"], ["r3"]),
    "ffn1": (["r3", "wg1", "wu1", "wd1"], ["y"]),
}

_PHASE_FNS = {}


def build_core_nc(phase_names, fused=False):
    table = _tensor_table()
    global _FUSED
    _FUSED = fused
    produced_here, consumed = set(), set()
    for ph in phase_names:
        ins, outs = PHASE_TENSORS[ph]
        consumed.update(ins)
        produced_here.update(outs)
    consumed.add("cid")
    later = set()
    _present = set(phase_names)
    if "vout1f" in _present:
        _present.add("vout1")
    rest = [ph for ph in _PHASE_ORDER if ph not in _present]
    for ph in rest:
        later.update(PHASE_TENSORS[ph][0])
    nc = bass.Bass("TRN2", target_bir_lowering=False, debug=False)
    dr = {}
    in_names, out_names = [], []
    for name in sorted(consumed | produced_here):
        shape, npdt, bdt, producer = table[name]
        if name in produced_here:
            if name in later or name == "y":
                kind = "ExternalOutput"; out_names.append(name)
            else:
                kind = "Internal"
        else:
            kind = "ExternalInput"; in_names.append(name)
        dr[name] = nc.dram_tensor(name, shape, bdt, kind=kind).ap()
    p = Prog(nc)
    with contextlib.ExitStack() as stack:
        consts = build_consts(p, stack, dr)
        p.fence()
        deps = {}
        for ph in phase_names:
            pstack = contextlib.ExitStack()
            _PHASE_FNS[ph](p, pstack, dr, consts, deps)
            pstack.close()
            p.fence()
        p.op("gpsimd", lambda e: e.nop(), deps=p.barrier_all(), no_inc=True)
        p.build(stack)
    return nc, in_names, out_names


_FUSED = False
_GROUPS = [[0, 1, 2, 3], [4, 5, 6, 7]]


def _ph_gather(L):
    def f(p, stack, dr, consts, deps):
        t1 = p.op("gpsimd", (lambda: lambda e: e.collective_compute(
            "AllGather", ALU.bypass, replica_groups=_GROUPS,
            ins=[dr[f"gin_kv{L}"][:]], outs=[dr[f"gout_kv{L}"][:]]))(),
            dma=True, key="agkv")
        t2 = p.op("gpsimd", (lambda: lambda e: e.collective_compute(
            "AllGather", ALU.bypass, replica_groups=_GROUPS,
            ins=[dr[f"gin_n{L}"][:]], outs=[dr[f"gout_n{L}"][:]]))(),
            dma=True, key="agn")
        deps[f"gather{L}"] = {"gout": [t1, t2]}
    return f


def _ph_csred(p, stack, dr, consts, deps):
    t = p.op("gpsimd", lambda e: e.collective_compute(
        "ReduceScatter", ALU.add, replica_groups=_GROUPS,
        ins=[dr["cs_part"][:]], outs=[dr["cs_my"][:]]),
        dma=True, key="rscs")
    deps["csred"] = {"cs_my": [t]}


_PHASE_FNS["gather0"] = _ph_gather(0)
_PHASE_FNS["gather1"] = _ph_gather(1)
_PHASE_FNS["csred"] = _ph_csred


def _ph_qkv0(p, stack, dr, consts, deps):
    deps["qkv0"] = phase_qkv(p, stack, dr, 0, consts)


def _ph_qkv1(p, stack, dr, consts, deps):
    d = deps.get("moe0")
    deps["qkv1"] = phase_qkv(p, stack, dr, 1, consts,
                             x_dep=(d["x1"][-1] if d else None))


_PHASE_FNS["qkv0"] = _ph_qkv0
_PHASE_FNS["qkv1"] = _ph_qkv1


def _np_of(name, table):
    shape, npdt, bdt, _ = table[name]
    return shape, npdt


_KERNEL_FUSED = False


def kernel(**inputs):
    import ml_dtypes
    st, cores = stage_inputs(inputs)
    table = _tensor_table()
    state = [{} for _ in range(NC)]
    for c in range(NC):
        state[c]["x0"] = cores[c]["x_t"]
        state[c]["cmask"] = cores[c]["cmask"]
        state[c]["cid"] = st["cid"]
        for tname, key in _HOST_STAGE_KEY.items():
            state[c][tname] = st[key]

    if _KERNEL_FUSED:
        launches = [["qkv0", "gather0", "attn0", "out0", "moe0", "qkv1",
                     "gather1", "cols1", "csred", "vout1f", "ffn1"]]
    else:
        launches = [["qkv0"], ["attn0", "out0", "moe0", "qkv1"],
                    ["cols1"], ["vout1", "ffn1"]]
    for li, phs in enumerate(launches):
        nc, in_names, out_names = build_core_nc(phs, fused=_KERNEL_FUSED)
        in_maps = [{n: state[c][n] for n in in_names} for c in range(NC)]
        res = run_bass_kernel_spmd(nc, in_maps, core_ids=list(range(NC)))
        for c in range(NC):
            for n in out_names:
                state[c][n] = np.asarray(res.results[c][n])
        if phs[-1] == "qkv0" or phs[-1] == "qkv1":
            L = 0 if phs[-1] == "qkv0" else 1
            for g in (0, 1):
                cs = range(4 * g, 4 * g + 4)
                gkv = np.concatenate([state[c][f"gin_kv{L}"] for c in cs], 0)
                gn = np.concatenate([state[c][f"gin_n{L}"] for c in cs], 0)
                for c in cs:
                    state[c][f"gout_kv{L}"] = gkv
                    state[c][f"gout_n{L}"] = gn
        if phs[-1] == "cols1":
            for g in (0, 1):
                for c in range(4 * g, 4 * g + 4):
                    r = c % 4
                    state[c]["cs_stack"] = np.ascontiguousarray(np.stack(
                        [state[cc]["cs_part"][r * T:(r + 1) * T]
                         for cc in range(4 * g, 4 * g + 4)])
                        .reshape(4, 16, 128, NH).transpose(0, 2, 1, 3))
    return unstage_output([state[c]["y"] for c in range(NC)])


def phase_attn0(p, stack, dr, consts, deps):
    nc = p.nc
    nm = "a0"
    mb = _sb(stack, nc, nm + "_mb", [128, 16], F32)
    cm = _sb(stack, nc, nm + "_cm", [128, 16 * 512])
    t_cm = p.op("sync", lambda e: e.dma_start(
        out=cm[:].rearrange("p (r q) -> p r q", q=512),
        in_=dr["cmask"].rearrange("r p q -> p r q")), dma=True, key="cm")
    t_mb = emit_mbias(p, stack, dr, consts, 0, mb)
    kt = [_sb(stack, nc, f"{nm}_kt{s}", [128, S]) for s in range(2)]
    vt = [_sb(stack, nc, f"{nm}_vt{s}", [128, S]) for s in range(2)]
    qh = [_sb(stack, nc, f"{nm}_qh{s}", [128, T]) for s in range(2)]
    e_sb = [_sb(stack, nc, f"{nm}_e{s}", [128, 1024]) for s in range(2)]
    ao = [_sb(stack, nc, f"{nm}_ao{s}", [128, 512]) for s in range(2)]
    rrow = _sb(stack, nc, nm + "_rrow", [1, 512], F32)
    rbc = _sb(stack, nc, nm + "_rbc", [128, 512])
    ps_s = [_ps(stack, nc, f"{nm}_pss{s}", [128, 1024]) for s in range(2)]
    ps_o = _ps(stack, nc, nm + "_pso", [128, 512])
    ps_z = _ps(stack, nc, nm + "_psz", [1, 512])
    ps_r = _ps(stack, nc, nm + "_psr", [128, 512])
    ones128, ones_row_f = consts["ones128"], consts["ones_row_f"]

    def emit_loads(h, d_kt, d_vt, d_qh):
        s = h % 2
        tk = tv = tq = None
        for b in range(16):
            agr = (b % 4) * T + (b // 4) * 512
            tk = p.op("sync", (lambda h, b, agr, s: lambda e: e.dma_start_transpose(
                out=kt[s][:, b * 512:(b + 1) * 512],
                in_=dr["gout_kv0"][agr:agr + 512, h * 128:(h + 1) * 128]))(h, b, agr, s),
                deps=[d_kt], dma=True, key="kt%d" % s)
            tv = p.op("gpsimd", (lambda h, b, agr, s: lambda e: e.dma_start(
                out=vt[s][:, b * 512:(b + 1) * 512].rearrange("p (c d) -> p c d", d=128),
                in_=dr["gout_kv0"][agr:agr + 512, D + h * 128:D + (h + 1) * 128]
                .rearrange("(c p) d -> p c d", p=128)))(h, b, agr, s),
                deps=[d_vt], dma=True, key="vt%d" % s)
        tq = p.op("sync", (lambda h, s: lambda e: e.dma_start(
            out=qh[s][:], in_=dr["qf0"][h]))(h, s), deps=[d_qh], dma=True,
            key="qh%d" % s)
        return tk, tv, tq

    t_ld = [None, None]
    t_kt_use = [None, None]
    t_vt_use = [None, None]
    t_qh_use = [None, None]
    t_ld[0] = emit_loads(0, None, None, None)
    t_ld[1] = emit_loads(1, None, None, None)
    t_e_slot = [None, None]
    t_z_slot = [None, None]
    t_ao_tok = None
    t_r_tok = None
    t_rb_tok = None
    t_rc_tok = None
    t_dma_slot = [None, None]
    for h in range(16):
        hs = h % 2
        t_kt_h, t_vt_h, t_qh_h = t_ld[hs]
        t_slast = t_avlast = None
        for w in range(4):
            nch = 16 * w + 16

            def emit_avz(sl, kc0, t_e, nch, hs):
                t_av = t_z = None
                for half in range(2):
                    t_av = p.op("tensor", (lambda hs, sl, kc0, half, nch: lambda e: e.matmul(
                        ps_o[:], vt[hs][:, (kc0 + half) * 128:(kc0 + half + 1) * 128],
                        e_sb[sl][:, half * 512:(half + 1) * 512],
                        start=(kc0 + half == 0), stop=(kc0 + half == nch - 1),
                        skip_group_check=True))(hs, sl, kc0, half, nch),
                        deps=[t_e, t_vt_h] + ([t_ao_tok] if kc0 + half == 0 else []))
                for half in range(2):
                    t_z = p.op("tensor", (lambda sl, kc0, half, nch: lambda e: e.matmul(
                        ps_z[:], ones128[:], e_sb[sl][:, half * 512:(half + 1) * 512],
                        start=(kc0 + half == 0), stop=(kc0 + half == nch - 1),
                        skip_group_check=True))(sl, kc0, half, nch),
                        deps=([t_r_tok] if kc0 + half == 0 else []))
                t_z_slot[sl] = t_z
                return t_av, t_z

            pend = None
            for jj in range(nch // 2):
                sl = jj % 2
                kc0 = 2 * jj
                d0 = [t_kt_h, t_qh_h, t_e_slot[sl], t_mb]
                t_s0 = p.op("tensor", (lambda hs, sl, kc0, w: lambda e: e.matmul(
                    ps_s[sl][:, 0:512], kt[hs][:, kc0 * 128:(kc0 + 1) * 128],
                    qh[hs][:, w * 512:(w + 1) * 512], start=True, stop=True,
                    skip_group_check=True))(hs, sl, kc0, w), deps=d0)
                t_s1 = p.op("tensor", (lambda hs, sl, kc0, w: lambda e: e.matmul(
                    ps_s[sl][:, 512:1024], kt[hs][:, (kc0 + 1) * 128:(kc0 + 2) * 128],
                    qh[hs][:, w * 512:(w + 1) * 512], start=True, stop=True,
                    skip_group_check=True))(hs, sl, kc0, w))
                tm = t_s1
                for half in range(2):
                    rel = kc0 + half - 16 * w
                    if rel >= 0:
                        tm = p.op("vector", (lambda sl, half, rel: lambda e: e.tensor_tensor(
                            ps_s[sl][:, half * 512:(half + 1) * 512],
                            ps_s[sl][:, half * 512:(half + 1) * 512],
                            cm[:, rel * 512:(rel + 1) * 512],
                            op=ALU.add))(sl, half, rel),
                            deps=[t_s1 if half else t_s0, t_cm, tm])
                t_e = p.op("scalar", (lambda sl, h: lambda e: e.activation(
                    out=e_sb[sl][:], in_=ps_s[sl][:], func=AF.Exp,
                    bias=mb[:, h:h + 1], scale=SCALE))(sl, h),
                    deps=[tm, t_mb, t_z_slot[sl]])
                t_e_slot[sl] = t_e
                if pend is not None:
                    t_avlast, _ = emit_avz(*pend)
                pend = (sl, kc0, t_e, nch, hs)
                t_slast = t_s1
            t_avlast, t_z = emit_avz(*pend)
            t_r_tok = p.op("vector", lambda e: e.reciprocal(rrow[:], ps_z[:]),
                           deps=[t_z, t_rb_tok])
            t_rb_tok = p.op("tensor", lambda e: e.matmul(
                ps_r[:], ones_row_f[:], rrow[:], start=True, stop=True,
                skip_group_check=True), deps=[t_r_tok, t_rc_tok])
            t_rc_tok = p.op("vector", lambda e: e.tensor_copy(rbc[:], ps_r[:]),
                            deps=[t_rb_tok, t_ao_tok])
            t_ao_tok = p.op("vector", (lambda w: lambda e: e.tensor_tensor(
                ao[w % 2][:], ps_o[:], rbc[:], op=ALU.mult))(w),
                deps=[t_rc_tok, t_avlast, t_dma_slot[w % 2]])
            t_dma_slot[w % 2] = p.op("sync", (lambda h, w: lambda e: e.dma_start(
                out=dr["a0"][h][:, w * 512:(w + 1) * 512], in_=ao[w % 2][:]))(h, w),
                deps=[t_ao_tok], dma=True, key="ao%d" % (w % 2))
        t_kt_use[hs], t_vt_use[hs], t_qh_use[hs] = t_slast, t_avlast, t_slast
        if h + 2 < 16:
            t_ld[hs] = emit_loads(h + 2, t_slast, t_avlast, t_slast)
    deps["attn0"] = {"a0": [t_dma_slot[0], t_dma_slot[1]]}


def emit_outproj_ln(p, stack, dr, consts, nm, ain, wname, resname, outname,
                    dep_in=None, ain_sbuf=None, ain_sbuf_deps=None):
    nc = p.nc
    if ain_sbuf is None:
        a_sb = [_sb(stack, nc, f"{nm}_a{c}", [128, T]) for c in range(16)]
    else:
        a_sb = ain_sbuf
    wo_sb = [_sb(stack, nc, f"{nm}_w{c}", [128, 2048]) for c in range(16)]
    y_sb = [_sb(stack, nc, f"{nm}_y{c}", [128, 512], F32) for c in range(16)]
    yb = [_sb(stack, nc, f"{nm}_yb{s}", [128, 512]) for s in range(2)]
    sq = [_sb(stack, nc, f"{nm}_sq{s}", [128, 512]) for s in range(2)]
    res = [_sb(stack, nc, f"{nm}_res{s}", [128, 512], F32) for s in range(2)]
    outt = [_sb(stack, nc, f"{nm}_out{s}", [128, 512], F32) for s in range(2)]
    mean_r = _sb(stack, nc, f"{nm}_mean", [1, 512], F32)
    msq_r = _sb(stack, nc, f"{nm}_msq", [1, 512], F32)
    m2_r = _sb(stack, nc, f"{nm}_m2", [1, 512], F32)
    var_r = _sb(stack, nc, f"{nm}_var", [1, 512], F32)
    sd_r = _sb(stack, nc, f"{nm}_sd", [1, 512], F32)
    rstd_r = _sb(stack, nc, f"{nm}_rstd", [1, 512], F32)
    nmean_r = _sb(stack, nc, f"{nm}_nmean", [1, 512], F32)
    mbc = _sb(stack, nc, f"{nm}_mbc", [128, 512], F32)
    rbc = _sb(stack, nc, f"{nm}_rbc", [128, 512], F32)
    ps = [_ps(stack, nc, f"{nm}_ps{s}", [128, 512]) for s in range(2)]
    ps_sum = _ps(stack, nc, f"{nm}_pssum", [1, 512])
    ps_sq = _ps(stack, nc, f"{nm}_pssq", [1, 512])
    ps_mb = _ps(stack, nc, f"{nm}_psmb", [128, 512])
    ps_rb = _ps(stack, nc, f"{nm}_psrb", [128, 512])
    ones128, ones_row_f = consts["ones128"], consts["ones_row_f"]

    dep_in = dep_in or []
    if ain_sbuf is None:
        t_a = [p.op("sync", (lambda c: lambda e: e.dma_start(
            out=a_sb[c][:], in_=ain[c]))(c), deps=dep_in, dma=True, key="oa")
            for c in range(16)]
    else:
        t_a = [ain_sbuf_deps[-1]] * 16
    t_w = [p.op("gpsimd", (lambda c: lambda e: e.dma_start(
        out=wo_sb[c][:], in_=dr[wname][c]))(c), deps=dep_in, dma=True, key="ow")
        for c in range(16)]
    t_res_slot = [None, None]
    t_ps_read = [None, None]
    t_yb_free = [None, None]
    t_sq_free = [None, None]
    t_y_dma = [None] * 16
    t_nm_prev = t_mbm_prev = t_rbm_prev = None
    t_n1_last = t_n2_last = None
    t_stat_read = None
    t_sq_mm = None
    t_mb_cp = None
    t_rb_cp = None
    t_out_dma = [None, None]
    for tt in range(4):
        t_sum_last = t_sq_last = None
        t_yw = [None] * 16
        pend_stat = None

        def emit_stats(oc, t_yb, t_sq):
            nonlocal t_sum_last, t_sq_last
            t_sum_last = p.op("tensor", (lambda oc: lambda e: e.matmul(
                ps_sum[:], ones128[:], yb[oc % 2][:], start=(oc == 0),
                stop=(oc == 15), skip_group_check=True))(oc),
                deps=[t_yb] + ([t_stat_read] if oc == 0 else []))
            t_sq_last = p.op("tensor", (lambda oc: lambda e: e.matmul(
                ps_sq[:], ones128[:], sq[oc % 2][:], start=(oc == 0),
                stop=(oc == 15), skip_group_check=True))(oc),
                deps=[t_sq] + ([t_stat_read] if oc == 0 else []))

        for oc in range(16):
            pslot = oc % 2
            rslot = oc % 2
            t_r = p.op("sync", (lambda oc, tt, rslot: lambda e: e.dma_start(
                out=res[rslot][:],
                in_=dr[resname][oc * 128:(oc + 1) * 128, tt * 512:(tt + 1) * 512]))(oc, tt, rslot),
                deps=[t_res_slot[rslot]], dma=True, key="res%d" % rslot)
            last_mm = None
            for c in range(16):
                last_mm = p.op("tensor", (lambda oc, c, tt, pslot: lambda e: e.matmul(
                    ps[pslot][:], wo_sb[oc][:, c * 128:(c + 1) * 128],
                    a_sb[c][:, tt * 512:(tt + 1) * 512],
                    start=(c == 0), stop=(c == 15), skip_group_check=True))(oc, c, tt, pslot),
                    deps=([t_w[15], t_a[15], t_ps_read[pslot]]
                          if c == 0 else []))
            t_y = p.op("vector", (lambda oc, rslot, pslot: lambda e: e.tensor_add(
                y_sb[oc][:], ps[pslot][:], res[rslot][:]))(oc, rslot, pslot),
                deps=[last_mm, t_r, t_y_dma[oc]])
            t_yw[oc] = t_y
            t_ps_read[pslot] = t_y
            t_res_slot[rslot] = t_y
            t_yb = p.op("scalar", (lambda oc: lambda e: e.copy(
                yb[oc % 2][:], y_sb[oc][:]))(oc),
                deps=[t_y, t_yb_free[oc % 2]])
            t_sq = p.op("scalar", (lambda oc: lambda e: e.activation(
                out=sq[oc % 2][:], in_=y_sb[oc][:], func=AF.Square))(oc),
                deps=[t_y, t_sq_free[oc % 2]])
            if pend_stat is not None:
                emit_stats(*pend_stat)
                t_yb_free[(oc - 1) % 2] = t_sum_last
                t_sq_free[(oc - 1) % 2] = t_sq_last
            pend_stat = (oc, t_yb, t_sq)
        emit_stats(*pend_stat)
        t_yb_free[15 % 2] = t_sum_last
        t_sq_free[15 % 2] = t_sq_last
        t_mean = p.op("vector", lambda e: e.tensor_scalar(
            out=mean_r[:], in0=ps_sum[:], scalar1=1.0 / D, scalar2=None,
            op0=ALU.mult), deps=[t_sum_last, t_nm_prev])
        t_msq = p.op("vector", lambda e: e.tensor_scalar(
            out=msq_r[:], in0=ps_sq[:], scalar1=1.0 / D, scalar2=None,
            op0=ALU.mult), deps=[t_sq_last])
        t_stat_read = t_msq
        t_m2 = p.op("scalar", lambda e: e.activation(
            out=m2_r[:], in_=mean_r[:], func=AF.Square), deps=[t_mean])
        t_var = p.op("vector", lambda e: e.tensor_sub(
            var_r[:], msq_r[:], m2_r[:]), deps=[t_m2, t_msq])
        t_sd = p.op("scalar", lambda e: e.activation(
            out=sd_r[:], in_=var_r[:], func=AF.Sqrt,
            bias=consts["epsln"][:]), deps=[t_var] + consts["tok"])
        t_rstd = p.op("vector", lambda e: e.reciprocal(rstd_r[:], sd_r[:]),
                      deps=[t_sd, t_rbm_prev])
        t_nm = p.op("scalar", lambda e: e.mul(nmean_r[:], mean_r[:], -1.0),
                    deps=[t_mean, t_mbm_prev])
        t_nm_prev = t_nm
        t_mbm = p.op("tensor", lambda e: e.matmul(
            ps_mb[:], ones_row_f[:], nmean_r[:], start=True, stop=True,
            skip_group_check=True), deps=[t_nm, t_mb_cp])
        t_mbm_prev = t_mbm
        t_mb_cp = p.op("scalar", lambda e: e.copy(mbc[:], ps_mb[:]),
                       deps=[t_mbm, t_n1_last])
        t_rbm = p.op("tensor", lambda e: e.matmul(
            ps_rb[:], ones_row_f[:], rstd_r[:], start=True, stop=True,
            skip_group_check=True), deps=[t_rstd, t_rb_cp])
        t_rbm_prev = t_rbm
        t_rb_cp = p.op("scalar", lambda e: e.copy(rbc[:], ps_rb[:]),
                       deps=[t_rbm, t_n2_last])
        for oc in range(16):
            oslot = oc % 2
            t_n1 = p.op("vector", (lambda oc: lambda e: e.tensor_add(
                y_sb[oc][:], y_sb[oc][:], mbc[:]))(oc), deps=[t_mb_cp, t_yw[oc]])
            t_n2 = p.op("vector", (lambda oc, oslot: lambda e: e.tensor_mul(
                outt[oslot][:], y_sb[oc][:], rbc[:]))(oc, oslot),
                deps=[t_n1, t_rb_cp, t_out_dma[oslot]])
            t_y_dma[oc] = p.op("sync", (lambda oc, tt, oslot: lambda e: e.dma_start(
                out=dr[outname][oc * 128:(oc + 1) * 128, tt * 512:(tt + 1) * 512],
                in_=outt[oslot][:]))(oc, tt, oslot), deps=[t_n2], dma=True,
                key="lo%d" % oslot)
            t_out_dma[oslot] = t_y_dma[oc]
            t_n1_last, t_n2_last = t_n1, t_n2
    return {outname: [t_y_dma[oc] for oc in range(16)]}


def phase_out0(p, stack, dr, consts, deps):
    deps["out0"] = emit_outproj_ln(p, stack, dr, consts, "o0",
                                   dr["a0"], "wo0", "x0", "r1")


_PHASE_FNS["out0"] = phase_out0


def emit_ln_tail(p, dr, consts, bufs, y_sb, outname, tt, prev):
    ones128_f, ones_row_f = consts["ones128_f"], consts["ones_row_f"]
    sqf, mean_r, msq_r, m2_r, var_r, sd_r, rstd_r, nmean_r, mbc, rbc, outt, \
        ps_sum, ps_sq, ps_mb, ps_rb = (
            bufs["sqf"], bufs["mean"], bufs["msq"], bufs["m2"], bufs["var"],
            bufs["sd"], bufs["rstd"], bufs["nmean"], bufs["mbc"], bufs["rbc"],
            bufs["outt"], bufs["ps_sum"], bufs["ps_sq"], bufs["ps_mb"],
            bufs["ps_rb"])
    y_sb = [a if isinstance(a, bass.AP) else a[:] for a in y_sb]
    ps_sum = ps_sum if isinstance(ps_sum, bass.AP) else ps_sum[:]
    ps_sq = ps_sq if isinstance(ps_sq, bass.AP) else ps_sq[:]
    t_sum = t_sq_mm = None
    for oc in range(16):
        t_sum = p.op("tensor", (lambda oc: lambda e: e.matmul(
            ps_sum, ones128_f[:], y_sb[oc], start=(oc == 0),
            stop=(oc == 15), skip_group_check=True))(oc),
            deps=[prev.get("y%d" % oc), prev.get("stat_read")]
                 + consts["tok"])
        t_sq = p.op("scalar", (lambda oc: lambda e: e.activation(
            out=sqf[:], in_=y_sb[oc], func=AF.Square))(oc),
            deps=[prev.get("y%d" % oc), t_sq_mm])
        t_sq_mm = p.op("tensor", (lambda oc: lambda e: e.matmul(
            ps_sq, ones128_f[:], sqf[:], start=(oc == 0), stop=(oc == 15),
            tile_position=(0, ps_sq.base_partition()
                           if hasattr(ps_sq, "base_partition") else 0),
            skip_group_check=True))(oc),
            deps=[t_sq] + ([prev.get("stat_read")] if oc == 0 else []))
    t_mean = p.op("vector", lambda e: e.tensor_scalar(
        out=mean_r[:], in0=ps_sum, scalar1=1.0 / D, scalar2=None,
        op0=ALU.mult), deps=[t_sum, prev.get("nm")])
    t_msq = p.op("vector", lambda e: e.tensor_scalar(
        out=msq_r[:], in0=ps_sq, scalar1=1.0 / D, scalar2=None,
        op0=ALU.mult), deps=[t_sq_mm])
    prev["stat_read"] = t_msq
    t_m2 = p.op("scalar", lambda e: e.activation(
        out=m2_r[:], in_=mean_r[:], func=AF.Square), deps=[t_mean])
    t_var = p.op("vector", lambda e: e.tensor_sub(
        var_r[:], msq_r[:], m2_r[:]), deps=[t_m2, t_msq])
    t_sd = p.op("scalar", lambda e: e.activation(
        out=sd_r[:], in_=var_r[:], func=AF.Sqrt,
        bias=consts["epsln"][:]), deps=[t_var])
    t_rstd = p.op("vector", lambda e: e.reciprocal(rstd_r[:], sd_r[:]),
                  deps=[t_sd, prev.get("rbm")])
    t_nm = p.op("scalar", lambda e: e.mul(nmean_r[:], mean_r[:], -1.0),
                deps=[t_mean, prev.get("mbm")])
    prev["nm"] = t_nm
    t_mbm = p.op("tensor", lambda e: e.matmul(
        ps_mb[:], ones_row_f[:], nmean_r[:], start=True, stop=True,
        skip_group_check=True), deps=[t_nm, prev.get("mb_cp")])
    prev["mbm"] = t_mbm
    t_mb_cp = p.op("scalar", lambda e: e.copy(mbc[:], ps_mb[:]),
                   deps=[t_mbm, prev.get("n1_last")])
    prev["mb_cp"] = t_mb_cp
    t_rbm = p.op("tensor", lambda e: e.matmul(
        ps_rb[:], ones_row_f[:], rstd_r[:], start=True, stop=True,
        skip_group_check=True), deps=[t_rstd, prev.get("rb_cp")])
    prev["rbm"] = t_rbm
    t_rb_cp = p.op("scalar", lambda e: e.copy(rbc[:], ps_rb[:]),
                   deps=[t_rbm, prev.get("n2_last")])
    prev["rb_cp"] = t_rb_cp
    outs = []
    for oc in range(16):
        oslot = oc % 2
        t_n1 = p.op("vector", (lambda oc: lambda e: e.tensor_add(
            y_sb[oc], y_sb[oc], mbc[:]))(oc), deps=[t_mb_cp])
        t_n2 = p.op("vector", (lambda oc, oslot: lambda e: e.tensor_mul(
            outt[oslot][:], y_sb[oc], rbc[:]))(oc, oslot),
            deps=[t_n1, t_rb_cp, prev.get("odma%d" % oslot)])
        t_dma = p.op("sync", (lambda oc, tt, oslot: lambda e: e.dma_start(
            out=dr[outname][oc * 128:(oc + 1) * 128, tt * 512:(tt + 1) * 512],
            in_=outt[oslot][:]))(oc, tt, oslot), deps=[t_n2], dma=True,
            key="lo%d" % oslot)
        prev["odma%d" % oslot] = t_dma
        prev["yfree%d" % oc] = t_n2
        outs.append(t_dma)
        prev["n1_last"], prev["n2_last"] = t_n1, t_n2
    return outs


def mk_ln_bufs(p, stack, nm):
    nc = p.nc
    return {
        "sqf": _sb(stack, nc, nm + "_sqf", [128, 512], F32),
        "mean": _sb(stack, nc, nm + "_mean", [1, 512], F32),
        "msq": _sb(stack, nc, nm + "_msq", [1, 512], F32),
        "m2": _sb(stack, nc, nm + "_m2", [1, 512], F32),
        "var": _sb(stack, nc, nm + "_var", [1, 512], F32),
        "sd": _sb(stack, nc, nm + "_sd", [1, 512], F32),
        "rstd": _sb(stack, nc, nm + "_rstd", [1, 512], F32),
        "nmean": _sb(stack, nc, nm + "_nmean", [1, 512], F32),
        "mbc": _sb(stack, nc, nm + "_mbc", [128, 512], F32),
        "rbc": _sb(stack, nc, nm + "_rbc", [128, 512], F32),
        "outt": [_sb(stack, nc, nm + "_outt%d" % s, [128, 512], F32)
                 for s in range(2)],
        "ps_sum": _ps(stack, nc, nm + "_pssum", [1, 512]),
        "ps_sq": _ps(stack, nc, nm + "_pssq", [1, 512]),
        "ps_mb": _ps(stack, nc, nm + "_psmb", [128, 512]),
        "ps_rb": _ps(stack, nc, nm + "_psrb", [128, 512]),
    }


_MOE_TT = 2

def phase_moe0(p, stack, dr, consts, deps):
    nc = p.nc
    nm = "m0"
    W = 1024
    h_sb = [_sb(stack, nc, f"{nm}_h{c}", [128, W]) for c in range(16)]
    acc = [_sb(stack, nc, f"{nm}_acc{c}", [128, W], F32) for c in range(16)]
    r1t = [_sb(stack, nc, f"{nm}_r1t{s}", [128, W], F32) for s in range(2)]
    wg_sb = [_sb(stack, nc, f"{nm}_wg{s}", [128, 2048]) for s in range(2)]
    wu_sb = [_sb(stack, nc, f"{nm}_wu{s}", [128, 2048]) for s in range(2)]
    wd_sb = [_sb(stack, nc, f"{nm}_wd{s}", [128, 1024]) for s in range(2)]
    rmsw = _sb(stack, nc, f"{nm}_rmsw", [128, 16], F32)
    rwf = _sb(stack, nc, f"{nm}_rwf", [128, 128], F32)
    sqf = _sb(stack, nc, f"{nm}_sqf", [128, W], F32)
    rstd_r = _sb(stack, nc, f"{nm}_rstd", [1, W], F32)
    sd_r = _sb(stack, nc, f"{nm}_sd", [1, W], F32)
    rbc = _sb(stack, nc, f"{nm}_rbc", [128, W], F32)
    lgf = _sb(stack, nc, f"{nm}_lgf", [8, W], F32)
    lgt = _sb(stack, nc, f"{nm}_lgt", [128, 64], F32)
    m1 = _sb(stack, nc, f"{nm}_m1", [128, 1], F32)
    nm1 = _sb(stack, nc, f"{nm}_nm1", [128, 1], F32)
    l2 = _sb(stack, nc, f"{nm}_l2", [128, 8], F32)
    m2 = _sb(stack, nc, f"{nm}_m2", [128, 1], F32)
    ee = _sb(stack, nc, f"{nm}_ee", [128, 8], F32)
    e2 = _sb(stack, nc, f"{nm}_e2", [128, 1], F32)
    den = _sb(stack, nc, f"{nm}_den", [128, 1], F32)
    rden = _sb(stack, nc, f"{nm}_rden", [128, 1], F32)
    ge2 = _sb(stack, nc, f"{nm}_ge2", [128, 8], F32)
    wdt = _sb(stack, nc, f"{nm}_wdt", [128, 64], F32)
    wdt16 = _sb(stack, nc, f"{nm}_wdt16", [128, 8])
    wdr = _sb(stack, nc, f"{nm}_wdr", [8, W], F32)
    wrow = _sb(stack, nc, f"{nm}_wrow", [1, W], F32)
    wbc = _sb(stack, nc, f"{nm}_wbc", [128, W], F32)
    sg = _sb(stack, nc, f"{nm}_sg", [128, 512])
    a1 = _sb(stack, nc, f"{nm}_a1", [128, 512])
    ae = [_sb(stack, nc, f"{nm}_ae{f}", [128, W]) for f in range(8)]
    ps_g = _ps(stack, nc, f"{nm}_psg", [128, 512])
    ps_u = _ps(stack, nc, f"{nm}_psu", [128, 512])
    ps_d = [_ps(stack, nc, f"{nm}_psd{s}", [128, 512]) for s in range(2)]
    ps_rows = _ps(stack, nc, f"{nm}_psrows", [128, 512])
    ps_lg = _ps(stack, nc, f"{nm}_pslg", [128, 512])
    ps_t = _ps(stack, nc, f"{nm}_pst", [128, 128])
    ps_wb = _ps(stack, nc, f"{nm}_pswb", [128, 512])
    ps_t_bf = ps_t.bitcast(BF16)
    lnb = {"sqf": _sb(stack, nc, nm + "_lnsqf", [128, 512], F32),
           "mean": _sb(stack, nc, nm + "_lnmean", [1, 512], F32),
           "msq": _sb(stack, nc, nm + "_lnmsq", [1, 512], F32),
           "m2": _sb(stack, nc, nm + "_lnm2", [1, 512], F32),
           "var": _sb(stack, nc, nm + "_lnvar", [1, 512], F32),
           "sd": _sb(stack, nc, nm + "_lnsd", [1, 512], F32),
           "rstd": _sb(stack, nc, nm + "_lnrstd", [1, 512], F32),
           "nmean": _sb(stack, nc, nm + "_lnnmean", [1, 512], F32),
           "mbc": _sb(stack, nc, nm + "_lnmbc", [128, 512], F32),
           "rbc": _sb(stack, nc, nm + "_lnrbc", [128, 512], F32),
           "outt": [_sb(stack, nc, nm + "_lnoutt%d" % s, [128, 512], F32)
                    for s in range(2)],
           "ps_sum": ps_rows[0:1, :], "ps_sq": ps_rows[32:33, :],
           "ps_mb": ps_g, "ps_rb": ps_u}
    ones128_f, ones_row_f = consts["ones128_f"], consts["ones_row_f"]
    wdr_dram = nc.dram_tensor("m0_wdr_scratch", [2, 8, W], F32,
                              kind="Internal").ap()

    t_rmsw = p.op("sync", lambda e: e.dma_start(out=rmsw[:], in_=dr["rms0"]),
                  dma=True, key="rmsw")
    t_rw0 = p.op("sync", lambda e: e.dma_start(out=rwf[:], in_=dr["router0"]),
                 dma=True, key="rw")
    t_rwp = t_rw0
    for c in range(16):
        t_rwp = p.op("vector", (lambda c: lambda e: e.tensor_scalar(
            out=rwf[:, c * 8:(c + 1) * 8], in0=rwf[:, c * 8:(c + 1) * 8],
            scalar1=rmsw[:, c:c + 1], scalar2=None, op0=ALU.mult))(c),
            deps=[t_rwp, t_rmsw])
    t_rw = t_rwp
    dep_r1 = (deps.get("out0") or {}).get("r1") or []
    pv = {}
    t_wg_use = [None, None]
    t_wu_use = [None, None]
    t_wd_use = [None, None]
    x1_outs = None
    for tt in range(_MOE_TT):
        c0 = tt * W
        t_sq_mm = [None, None]
        t_lgm = [None, None]
        t_sq = None
        for c in range(16):
            rs = c % 2
            t_l = p.op("sync", (lambda c, tt, rs: lambda e: e.dma_start(
                out=r1t[rs][:],
                in_=dr["r1"][c * 128:(c + 1) * 128, tt * W:(tt + 1) * W]))(c, tt, rs),
                deps=dep_r1 + [pv.get("r1afree%d" % rs)], dma=True,
                key="r1a%d" % rs)
            t_sq = p.op("scalar", (lambda rs: lambda e: e.activation(
                out=sqf[:], in_=r1t[rs][:], func=AF.Square))(rs),
                deps=[t_l, t_sq_mm[1]])
            for sub in range(2):
                t_sq_mm[sub] = p.op("tensor", (lambda c, sub: lambda e: e.matmul(
                    ps_rows[32 * sub:32 * sub + 1, :], ones128_f[:],
                    sqf[:, sub * 512:(sub + 1) * 512], start=(c == 0),
                    stop=(c == 15), tile_position=(0, 32 * sub),
                    skip_group_check=True))(c, sub),
                    deps=[t_sq] + ([pv.get("rows_free"), pv.get("stat_read")]
                                   if c == 0 else []))
            for sub in range(2):
                t_lgm[sub] = p.op("tensor", (lambda c, rs, sub: lambda e: e.matmul(
                    ps_lg[32 * sub:32 * sub + 8, :], rwf[:, c * 8:(c + 1) * 8],
                    r1t[rs][:, sub * 512:(sub + 1) * 512], start=(c == 0),
                    stop=(c == 15), tile_position=(0, 32 * sub),
                    skip_group_check=True))(c, rs, sub),
                    deps=[t_rw] + ([pv.get("lg_free")] if c == 0 else []))
            pv["r1afree%d" % rs] = t_lgm[1]
        t_ms = None
        for sub in range(2):
            t_ms = p.op("vector", (lambda sub: lambda e: e.tensor_scalar(
                out=rstd_r[:, sub * 512:(sub + 1) * 512],
                in0=ps_rows[32 * sub:32 * sub + 1, :], scalar1=1.0 / D,
                scalar2=None, op0=ALU.mult))(sub),
                deps=[t_sq_mm[sub], pv.get("sd"), t_ms])
        pv["rows_free"] = t_ms
        t_sd = p.op("scalar", lambda e: e.activation(
            out=sd_r[:], in_=rstd_r[:], func=AF.Sqrt,
            bias=consts["epsrms"][:]), deps=[t_ms])
        pv["sd"] = t_sd
        t_rr = p.op("vector", lambda e: e.reciprocal(rstd_r[:], sd_r[:]),
                    deps=[t_sd])
        t_rbc = None
        for sub in range(2):
            t_wbm = p.op("tensor", (lambda sub: lambda e: e.matmul(
                ps_wb[:], ones_row_f[:], rstd_r[:, sub * 512:(sub + 1) * 512],
                start=True, stop=True, skip_group_check=True))(sub),
                deps=[t_rr, pv.get("wb_read")])
            t_rbc = p.op("scalar", (lambda sub: lambda e: e.copy(
                rbc[:, sub * 512:(sub + 1) * 512], ps_wb[:]))(sub),
                deps=[t_wbm, pv.get("rbc_free"), t_rbc])
            pv["wb_read"] = t_rbc
        t_lgf = None
        for sub in range(2):
            t_lgf = p.op("vector", (lambda sub: lambda e: e.tensor_mul(
                lgf[:, sub * 512:(sub + 1) * 512], ps_lg[32 * sub:32 * sub + 8, :],
                rbc[0:8, sub * 512:(sub + 1) * 512]))(sub),
                deps=[t_lgm[sub], t_rbc, pv.get("lgf_free"), t_lgf])
        pv["lg_free"] = t_lgf
        t_trprev = pv.get("pst_free")
        for q in range(8):
            t_tr = p.op("tensor", (lambda q: lambda e: e.transpose(
                ps_t[0:128, 0:8], lgf[:, q * 128:(q + 1) * 128],
                consts["ident_f"][0:8, 0:8]))(q),
                deps=[t_lgf, t_trprev])
            t_cp = p.op("scalar", (lambda q: lambda e: e.copy(
                lgt[:, q * 8:(q + 1) * 8], ps_t[0:128, 0:8]))(q),
                deps=[t_tr, pv.get("lgt_free%d" % q)])
            t_trprev = t_cp
            pv["lgt_cp%d" % q] = t_cp
        pv["lgf_free"] = t_trprev
        for q in range(8):
            sl = slice(q * 8, (q + 1) * 8)
            d0 = [pv["lgt_cp%d" % q], pv.get("wdt_free")]
            a = p.op("vector", (lambda sl: lambda e: e.reduce_max(
                m1[:], lgt[:, sl], axis=AXX))(sl), deps=d0)
            b = p.op("vector", lambda e: e.tensor_scalar(
                out=nm1[:], in0=m1[:], scalar1=-1.0, scalar2=None,
                op0=ALU.mult), deps=[a])
            c1 = p.op("vector", (lambda sl: lambda e: e.tensor_single_scalar(
                out=l2[:], in_=lgt[:, sl], scalar=m1[:], op=ALU.is_ge))(sl),
                deps=[b])
            d1 = p.op("vector", (lambda sl: lambda e: e.scalar_tensor_tensor(
                out=l2[:], in0=l2[:], scalar=NEG, in1=lgt[:, sl],
                op0=ALU.mult, op1=ALU.add))(sl), deps=[c1])
            f1 = p.op("vector", lambda e: e.reduce_max(m2[:], l2[:], axis=AXX),
                      deps=[d1])
            g1 = p.op("scalar", (lambda sl: lambda e: e.activation(
                out=ee[:], in_=lgt[:, sl], func=AF.Exp, bias=nm1[:]))(sl),
                deps=[f1])
            g2 = p.op("scalar", lambda e: e.activation(
                out=e2[:], in_=m2[:], func=AF.Exp, bias=nm1[:]), deps=[g1])
            g3 = p.op("vector", lambda e: e.tensor_scalar(
                out=den[:], in0=e2[:], scalar1=1.0, scalar2=None,
                op0=ALU.add), deps=[g2])
            g4 = p.op("vector", lambda e: e.reciprocal(rden[:], den[:]),
                      deps=[g3])
            g5 = p.op("vector", (lambda sl: lambda e: e.tensor_single_scalar(
                out=ge2[:], in_=lgt[:, sl], scalar=m2[:], op=ALU.is_ge))(sl),
                deps=[g4])
            g6 = p.op("vector", lambda e: e.tensor_mul(ge2[:], ge2[:], ee[:]),
                      deps=[g5])
            g7 = p.op("vector", (lambda q: lambda e: e.tensor_scalar(
                out=wdt[:, q * 8:(q + 1) * 8], in0=ge2[:], scalar1=rden[:],
                scalar2=None, op0=ALU.mult))(q), deps=[g6])
            pv["top2_%d" % q] = g7
            pv["lgt_free%d" % q] = g7
        t_wcp = None
        for q in range(8):
            t_c16 = p.op("scalar", (lambda q: lambda e: e.copy(
                wdt16[:], wdt[:, q * 8:(q + 1) * 8]))(q),
                deps=[pv["top2_%d" % q], pv.get("wdt16_free")])
            t_wtr = p.op("tensor", (lambda q: lambda e: e.transpose(
                ps_t_bf[0:8, 0:128], wdt16[:], consts["ident"][:]))(q),
                deps=[t_c16, t_wcp if t_wcp else t_trprev])
            pv["wdt16_free"] = t_wtr
            t_wcp = p.op("scalar", (lambda q: lambda e: e.copy(
                wdr[:, q * 128:(q + 1) * 128], ps_t_bf[0:8, 0:128]))(q),
                deps=[t_wtr, pv.get("wdr_free")])
        pv["pst_free"] = t_wcp
        pv["wdt_free"] = t_wcp
        t_wdma = p.op("sync", (lambda tt: lambda e: e.dma_start(
            out=wdr_dram[tt], in_=wdr[:]))(tt), deps=[t_wcp], dma=True,
            key="wdr")
        pv["wdr_free"] = t_wdma
        t_h = []
        for c in range(16):
            rs = c % 2
            t_l = p.op("sync", (lambda c, tt, rs: lambda e: e.dma_start(
                out=r1t[rs][:],
                in_=dr["r1"][c * 128:(c + 1) * 128, tt * W:(tt + 1) * W]))(c, tt, rs),
                deps=[pv.get("r1bfree%d" % rs)], dma=True, key="r1a%d" % rs)
            t_hc = p.op("vector", (lambda c, rs: lambda e: e.scalar_tensor_tensor(
                out=h_sb[c][:], in0=r1t[rs][:], scalar=rmsw[:, c:c + 1],
                in1=rbc[:], op0=ALU.mult, op1=ALU.mult))(c, rs),
                deps=[t_rbc, t_l, t_rmsw, pv.get("hfree%d" % c)])
            pv["r1bfree%d" % rs] = t_hc
            t_h.append(t_hc)
        pv["rbc_free"] = t_h[15]
        for eidx in range(9):
            shared = (eidx == 8)
            if not shared:
                t_wrow = p.op("sync", (lambda tt, eidx: lambda e: e.dma_start(
                    out=wrow[:], in_=wdr_dram[tt, eidx:eidx + 1, :]))(tt, eidx),
                    deps=[t_wdma, pv.get("wrow_free")], dma=True, key="wrow")
                t_wbc = None
                for sub in range(2):
                    t_wbm = p.op("tensor", (lambda sub: lambda e: e.matmul(
                        ps_wb[:], ones_row_f[:], wrow[:, sub * 512:(sub + 1) * 512],
                        start=True, stop=True, skip_group_check=True))(sub),
                        deps=[t_wrow, pv.get("wb_read")])
                    t_wbc = p.op("scalar", (lambda sub: lambda e: e.copy(
                        wbc[:, sub * 512:(sub + 1) * 512], ps_wb[:]))(sub),
                        deps=[t_wbm, pv.get("wbc_free"), t_wbc])
                    pv["wb_read"] = t_wbc
                pv["wrow_free"] = t_wbc
            for fc in range(8):
                ws = fc % 2
                wg_src = dr["swg0"][fc] if shared else dr["wg0"][eidx, fc]
                wu_src = dr["swu0"][fc] if shared else dr["wu0"][eidx, fc]
                t_wgl = p.op("gpsimd", (lambda wg_src, ws: lambda e: e.dma_start(
                    out=wg_sb[ws][:], in_=wg_src))(wg_src, ws),
                    deps=[t_wg_use[ws]], dma=True, key="mwg%d" % ws)
                t_wul = p.op("gpsimd", (lambda wu_src, ws: lambda e: e.dma_start(
                    out=wu_sb[ws][:], in_=wu_src))(wu_src, ws),
                    deps=[t_wu_use[ws]], dma=True, key="mwu%d" % ws)
                for sub in range(2):
                    lastg = lastu = None
                    for c in range(16):
                        lastg = p.op("tensor", (lambda fc, c, ws, sub: lambda e: e.matmul(
                            ps_g[:], wg_sb[ws][:, c * 128:(c + 1) * 128],
                            h_sb[c][:, sub * 512:(sub + 1) * 512],
                            start=(c == 0), stop=(c == 15),
                            skip_group_check=True))(fc, c, ws, sub),
                            deps=([t_wgl, t_h[15], pv.get("psg_free")]
                                  if c == 0 else []))
                    t_sg = p.op("scalar", (lambda: lambda e: e.activation(
                        out=sg[:], in_=ps_g[:], func=AF.Silu))(),
                        deps=[lastg, pv.get("sg_free")])
                    pv["psg_free"] = t_sg
                    for c in range(16):
                        lastu = p.op("tensor", (lambda fc, c, ws, sub: lambda e: e.matmul(
                            ps_u[:], wu_sb[ws][:, c * 128:(c + 1) * 128],
                            h_sb[c][:, sub * 512:(sub + 1) * 512],
                            start=(c == 0), stop=(c == 15),
                            skip_group_check=True))(fc, c, ws, sub),
                            deps=([t_wul, pv.get("psu_free")] if c == 0 else []))
                    t_a1 = p.op("vector", (lambda: lambda e: e.tensor_mul(
                        a1[:], ps_u[:], sg[:]))(),
                        deps=[t_sg, lastu, pv.get("a1_free")])
                    pv["psu_free"] = t_a1
                    pv["sg_free"] = t_a1
                    if shared:
                        t_ae = p.op("scalar", (lambda fc, sub: lambda e: e.copy(
                            ae[fc][:, sub * 512:(sub + 1) * 512], a1[:]))(fc, sub),
                            deps=[t_a1, pv.get("aefree%d" % fc)])
                    else:
                        t_ae = p.op("vector", (lambda fc, sub: lambda e: e.tensor_mul(
                            ae[fc][:, sub * 512:(sub + 1) * 512], a1[:],
                            wbc[:, sub * 512:(sub + 1) * 512]))(fc, sub),
                            deps=[t_a1, t_wbc, pv.get("aefree%d" % fc)])
                    pv["a1_free"] = t_ae
                    pv["ae%d_%d" % (fc, sub)] = t_ae
                    t_wg_use[ws] = lastg
                    t_wu_use[ws] = lastu
            if not shared:
                pv["wbc_free"] = pv["ae7_1"]
            for oc in range(16):
                ds = oc % 2
                wd_src = dr["swd0"][oc] if shared else dr["wd0"][eidx, oc]
                t_wd = p.op("gpsimd", (lambda wd_src, ds: lambda e: e.dma_start(
                    out=wd_sb[ds][:], in_=wd_src))(wd_src, ds),
                    deps=[t_wd_use[ds]], dma=True, key="mwd%d" % ds)
                for sub in range(2):
                    lastd = None
                    for fc in range(8):
                        lastd = p.op("tensor", (lambda fc, ds, sub: lambda e: e.matmul(
                            ps_d[ds][:], wd_sb[ds][:, fc * 128:(fc + 1) * 128],
                            ae[fc][:, sub * 512:(sub + 1) * 512],
                            start=(fc == 0), stop=(fc == 7),
                            skip_group_check=True))(fc, ds, sub),
                            deps=([t_wd, pv["ae7_1"], pv.get("psd_free%d" % ds)]
                                  if fc == 0 else []))
                    if eidx == 0:
                        t_acc = p.op("vector", (lambda oc, ds, sub: lambda e: e.tensor_copy(
                            acc[oc][:, sub * 512:(sub + 1) * 512], ps_d[ds][:]))(oc, ds, sub),
                            deps=[lastd, pv.get("accfree%d" % oc)])
                    else:
                        t_acc = p.op("vector", (lambda oc, ds, sub: lambda e: e.tensor_add(
                            acc[oc][:, sub * 512:(sub + 1) * 512],
                            acc[oc][:, sub * 512:(sub + 1) * 512], ps_d[ds][:]))(oc, ds, sub),
                            deps=[lastd, pv.get("acc%d" % oc)])
                    pv["psd_free%d" % ds] = t_acc
                    pv["acc%d" % oc] = t_acc
                    t_wd_use[ds] = lastd
            for fc in range(8):
                pv["aefree%d" % fc] = pv["acc15"]
        for oc in range(16):
            rs = oc % 2
            t_l = p.op("sync", (lambda oc, tt, rs: lambda e: e.dma_start(
                out=r1t[rs][:],
                in_=dr["r1"][oc * 128:(oc + 1) * 128, tt * W:(tt + 1) * W]))(oc, tt, rs),
                deps=[pv.get("r1cfree%d" % rs)], dma=True, key="r1a%d" % rs)
            t_y = p.op("vector", (lambda oc, rs: lambda e: e.tensor_add(
                acc[oc][:], acc[oc][:], r1t[rs][:]))(oc, rs),
                deps=[t_l, pv["acc%d" % oc]])
            pv["r1cfree%d" % rs] = t_y
            pv["y%d" % oc] = t_y
        pv["mbm"] = pv.get("psg_free")
        pv["rbm"] = pv.get("psu_free")
        for sub in range(2):
            pv["stat_read"] = pv.get("rows_free") if sub == 0 else pv.get("stat_read")
            ysubs = [acc[oc][:, sub * 512:(sub + 1) * 512] for oc in range(16)]
            for oc in range(16):
                pv["y%d" % oc] = pv["y%d" % oc] if sub == 0 else pv["yprev%d" % oc]
            x1_outs = emit_ln_tail(p, dr, consts, lnb, ysubs, "x1",
                                   2 * tt + sub, pv)
            for oc in range(16):
                pv["yprev%d" % oc] = pv["yfree%d" % oc]
        pv["psg_free"] = pv["mb_cp"]
        pv["psu_free"] = pv["rb_cp"]
        pv["rows_free"] = pv["stat_read"]
        for oc in range(16):
            pv["hfree%d" % oc] = pv["yprev%d" % oc]
            pv["accfree%d" % oc] = pv["yprev%d" % oc]
    deps["moe0"] = {"x1": x1_outs}


_PHASE_FNS["moe0"] = phase_moe0


def emit_mbias(p, stack, dr, consts, L, mb):
    nc = p.nc
    nm = "mb%d" % L
    ident, ones_row = consts["ident"], consts["ones_row"]
    kn = _sb(stack, nc, nm + "_kn", [128, 1024])
    knmax = _sb(stack, nc, nm + "_knmax", [128, 16], F32)
    knmax16 = _sb(stack, nc, nm + "_knmax16", [128, 16])
    qmaxf = _sb(stack, nc, nm + "_qmaxf", [1, 16], F32)
    qmax16 = _sb(stack, nc, nm + "_qmax16", [1, 16])
    smkn = _sb(stack, nc, nm + "_smkn", [16, 1], F32)
    smm = _sb(stack, nc, nm + "_smm", [16, 1], F32)
    smm16 = _sb(stack, nc, nm + "_smm16", [16, 1])
    mrow = _sb(stack, nc, nm + "_mrow", [1, 16])
    with nc.psum_tensor(nm + "_pst", [128, 128], F32) as ps_t:
        ps_t_bf = ps_t.bitcast(BF16)
        t_kn = p.op("sync", (lambda L: lambda e: e.dma_start(
            out=kn[:],
            in_=dr[f"gout_n{L}"].rearrange("(p q) h -> p (q h)", p=128)))(L),
            dma=True, key="kn")
        t1 = p.op("vector", lambda e: e.reduce_max(
            knmax[:], kn[:].rearrange("p (q h) -> p h q", h=16), axis=AXX),
            deps=[t_kn])
        t2 = p.op("scalar", lambda e: e.copy(knmax16[:], knmax[:]), deps=[t1])
        t3 = p.op("tensor", lambda e: e.transpose(
            ps_t_bf[0:16, 0:128], knmax16[:], ident[:]), deps=[t2])
        t4 = p.op("vector", lambda e: e.reduce_max(
            smkn[:], ps_t_bf[0:16, 0:128], axis=AXX), deps=[t3])
        t5 = p.op("sync", (lambda L: lambda e: e.dma_start(
            out=qmaxf[:], in_=dr[f"qmax{L}"]))(L), dma=True, key="qmaxf")
        t6 = p.op("scalar", lambda e: e.copy(qmax16[:], qmaxf[:]), deps=[t5])
        t7 = p.op("tensor", lambda e: e.transpose(
            ps_t_bf[0:16, 0:1], qmax16[:], ident[0:1, 0:1]), deps=[t6, t4])
        t8 = p.op("vector", lambda e: e.tensor_add(
            smm[:], smkn[:], ps_t_bf[0:16, 0:1]), deps=[t7])
        t9 = p.op("vector", lambda e: e.tensor_scalar(
            out=smm[:], in0=smm[:], scalar1=-SCALE / 2.0, scalar2=None,
            op0=ALU.mult), deps=[t8])
        t10 = p.op("scalar", lambda e: e.copy(smm16[:], smm[:]), deps=[t9])
        t11 = p.op("tensor", lambda e: e.transpose(
            ps_t_bf[0:1, 0:16], smm16[:], ident[0:16, 0:16]), deps=[t10, t8])
        t12 = p.op("scalar", lambda e: e.copy(mrow[:], ps_t_bf[0:1, 0:16]),
                   deps=[t11])
        t13 = p.op("tensor", lambda e: e.matmul(
            ps_t[0:128, 0:16], ones_row[:], mrow[:], start=True, stop=True,
            skip_group_check=True), deps=[t12])
        t_mb = p.op("scalar", lambda e: e.copy(mb[:], ps_t[0:128, 0:16]),
                    deps=[t13])
    return t_mb


def phase_cols1(p, stack, dr, consts, deps):
    nc = p.nc
    nm = "c1"
    mb = _sb(stack, nc, nm + "_mb", [128, 16], F32)
    t_mb = emit_mbias(p, stack, dr, consts, 1, mb)
    kt = [_sb(stack, nc, f"{nm}_kt{s}", [128, S]) for s in range(2)]
    qh = [_sb(stack, nc, f"{nm}_qh{s}", [128, T]) for s in range(2)]
    e_all = [_sb(stack, nc, f"{nm}_e{s}", [128, S]) for s in range(2)]
    zacc = [_sb(stack, nc, nm + "_zacc%d" % s, [128, 16], F32) for s in range(2)]
    zrow = _sb(stack, nc, nm + "_z", [128, 1], F32)
    rr = _sb(stack, nc, nm + "_r", [128, 1], F32)
    r16 = _sb(stack, nc, nm + "_r16", [128, 1])
    csall = _sb(stack, nc, nm + "_csall", [16, S], F32)
    cs_tm = [_sb(stack, nc, f"{nm}_cstm{s}", [128, 16], F32) for s in range(2)]
    ps_s = [_ps(stack, nc, f"{nm}_pss{s}", [128, 1024]) for s in range(2)]
    ps_cs = [_ps(stack, nc, f"{nm}_pscs{b}", [128, 512]) for b in range(4)]
    csev = [_sb(stack, nc, f"{nm}_csev{s}", [128, 512], F32) for s in range(2)]
    cs_rows = nc.dram_tensor("c1_cs_rows", [16, S], F32, kind="Internal").ap()

    def bankrow(i):
        return ps_cs[i // 4], 32 * (i % 4)

    def emit_loads(h, d_kt, d_qh):
        s = h % 2
        tk = None
        for b4 in range(4):
            tk = p.op("sync", (lambda h, b4, s: lambda e: e.dma_start_transpose(
                out=kt[s][:, b4 * 2048:(b4 + 1) * 2048],
                in_=dr["gout_kv1"][b4 * 2048:(b4 + 1) * 2048,
                                   h * 128:(h + 1) * 128]))(h, b4, s),
                deps=[d_kt], dma=True, key="kt%d" % s)
        tq = p.op("sync", (lambda h, s: lambda e: e.dma_start(
            out=qh[s][:], in_=dr["qf1"][h]))(h, s), deps=[d_qh], dma=True,
            key="qh%d" % s)
        return tk, tq

    t_ld = [None, None]
    t_ld[0] = emit_loads(0, None, None)
    t_ld[1] = emit_loads(1, None, None)
    t_ms0 = None
    for b in range(4):
        t_ms0 = p.op("vector", (lambda b: lambda e: e.memset(
            ps_cs[b][:], 0.0))(b), deps=[t_ms0])
    t_e_strip = [None, None]
    t_zprev = [None, None]
    t_csev_dma = [None, None]
    t_cs_slot = [None, None]
    t_zred_prev = None
    t_r16_free = None
    t_csr_dma = []
    t_ev_prev = None
    for h in range(16):
        hs = h % 2
        t_kt_h, t_qh_h = t_ld[hs]
        t_slast = None
        pend_cs = None

        def emit_cs(qc, t_r16, t_e):
            t_cs = None
            for ki in range(16):
                bank, row = bankrow(ki)
                t_cs = p.op("tensor", (lambda qc, ki, bank, row: lambda e: e.matmul(
                    bank[row:row + 1, :], r16[:],
                    e_all[qc % 2][:, ki * 512:(ki + 1) * 512],
                    start=(qc == 0), stop=(qc == 15),
                    tile_position=(0, row), skip_group_check=True))(qc, ki, bank, row),
                    deps=[t_r16, t_e, t_ms0]
                         + ([t_ev_prev] if qc == 0 else []))
            t_cs_slot[qc % 2] = t_cs
            return t_cs

        for qc in range(16):
            t_e = None
            for jj in range(8):
                ki0 = 2 * jj
                sl = jj % 2
                t_s = None
                for half in range(2):
                    t_s = p.op("tensor", (lambda hs, qc, ki0, half, sl: lambda e: e.matmul(
                        ps_s[sl][:, half * 512:(half + 1) * 512],
                        qh[hs][:, qc * 128:(qc + 1) * 128],
                        kt[hs][:, (ki0 + half) * 512:(ki0 + half + 1) * 512],
                        start=True, stop=True, skip_group_check=True))(hs, qc, ki0, half, sl),
                        deps=([t_kt_h, t_qh_h, t_e_strip[sl], t_mb]
                              if half == 0 else []))
                t_e = p.op("scalar", (lambda qc, ki0, h, jj, sl: lambda e: e.activation(
                    out=e_all[qc % 2][:, ki0 * 512:(ki0 + 2) * 512], in_=ps_s[sl][:],
                    func=AF.Exp, bias=mb[:, h:h + 1], scale=SCALE,
                    accum_out=zacc[qc % 2][:, jj:jj + 1]))(qc, ki0, h, jj, sl),
                    deps=[t_s, t_cs_slot[qc % 2]]
                         + ([t_zprev[qc % 2]] if ki0 == 0 else []))
                t_e_strip[sl] = t_e
                t_slast = t_s
                if jj == 0 and pend_cs is not None:
                    t_r16_free = emit_cs(*pend_cs)
            t_zred_prev = p.op("vector", (lambda qc: lambda e: e.reduce_sum(
                zrow[:], zacc[qc % 2][:, 0:8], axis=AXX))(qc), deps=[t_e])
            t_zprev[qc % 2] = t_zred_prev
            t_rr = p.op("vector", lambda e: e.reciprocal(rr[:], zrow[:]),
                        deps=[t_zred_prev])
            t_r16 = p.op("vector", lambda e: e.tensor_copy(r16[:], rr[:]),
                         deps=[t_rr, t_r16_free])
            pend_cs = (qc, t_r16, t_e)
        t_r16_free = emit_cs(*pend_cs)
        pend_cs = None
        t_ev = None
        t_last_dma = None
        for b in range(4):
            es = b % 2
            t_ev = p.op("vector", (lambda b, es: lambda e: e.tensor_copy(
                csev[es][:], ps_cs[b][:]))(b, es),
                deps=[t_cs_slot[0], t_cs_slot[1], t_ev, t_csev_dma[es]])
            for ri in range(4):
                ki = b * 4 + ri
                t_last_dma = p.op("sync", (lambda h, ki, ri, es: lambda e: e.dma_start(
                    out=cs_rows[h:h + 1, ki * 512:(ki + 1) * 512],
                    in_=csev[es][32 * ri:32 * ri + 1, :]))(h, ki, ri, es),
                    deps=[t_ev], dma=True, key="csr%d" % es)
            t_csev_dma[es] = t_last_dma
        t_ev_prev = t_ev
        t_csr_dma.append(t_last_dma)
        if h + 2 < 16:
            t_ld[hs] = emit_loads(h + 2, t_slast, t_slast)
    t_ld2 = p.op("sync", lambda e: e.dma_start(out=csall[:], in_=cs_rows[:]),
                 deps=[t_csev_dma[0], t_csev_dma[1]], dma=True, key="csall")
    t_tm_dma = [None, None]
    t_tr_prev = t_slast
    for c64 in range(64):
        s = c64 % 2
        t_tr = p.op("tensor", (lambda c64: lambda e: e.transpose(
            ps_s[0][0:128, 0:16], csall[0:16, c64 * 128:(c64 + 1) * 128],
            consts["ident_f"][0:16, 0:16]))(c64),
            deps=[t_ld2, t_e_strip[0], t_e_strip[1], t_tr_prev,
                  t_zred_prev])
        t_cp = p.op("vector", (lambda c64, s: lambda e: e.tensor_copy(
            cs_tm[s][:], ps_s[0][0:128, 0:16]))(c64, s),
            deps=[t_tr, t_tm_dma[s]])
        t_tr_prev = t_cp
        t_tm_dma[s] = p.op("sync", (lambda c64, s: lambda e: e.dma_start(
            out=dr["cs_part"][c64 * 128:(c64 + 1) * 128, :], in_=cs_tm[s][:]))(c64, s),
            deps=[t_cp], dma=True, key="cstm%d" % s)
    deps["cols1"] = {"cs_part": [t_tm_dma[0], t_tm_dma[1]]}


_PHASE_FNS["cols1"] = phase_cols1


def phase_vout1(p, stack, dr, consts, deps, fused=False):
    nc = p.nc
    nm = "v1"
    cs4 = _sb(stack, nc, nm + "_cs4", [128, 256], F32)
    cs_my = _sb(stack, nc, nm + "_csmy", [128, 256], F32)
    v_tm = [_sb(stack, nc, f"{nm}_vtm{s}", [128, 2048]) for s in range(2)]
    vs16 = [_sb(stack, nc, f"{nm}_vs{s}", [128, 128]) for s in range(2)]
    vfm = [_sb(stack, nc, f"{nm}_vfm{h}", [128, T]) for h in range(16)]
    ps_t = [_ps(stack, nc, f"{nm}_pst{s}", [128, 128]) for s in range(2)]
    ident = consts["ident"]

    if "cs_my" in dr:
        t_l = p.op("sync", lambda e: e.dma_start(
            out=cs_my[:].rearrange("p (c h) -> p c h", h=16),
            in_=dr["cs_my"].rearrange("(c p) h -> p c h", p=128)),
            dma=True, key="cs4")
        t_cs = p.op("vector", lambda e: e.tensor_copy(cs4[:], cs_my[:]),
                    deps=[t_l])
        t_cs = p.op("vector", lambda e: e.tensor_copy(cs_my[:], cs4[:]),
                    deps=[t_cs])
    else:
        t_cs = None
        for i in range(4):
            t_l = p.op("sync", (lambda i: lambda e: e.dma_start(
                out=cs4[:].rearrange("p (c h) -> p c h", h=16),
                in_=dr["cs_stack"][i]))(i),
                deps=[t_cs], dma=True, key="cs4")
            if i == 0:
                t_cs = p.op("vector", lambda e: e.tensor_copy(cs_my[:], cs4[:]),
                            deps=[t_l])
            else:
                t_cs = p.op("vector", lambda e: e.tensor_add(
                    cs_my[:], cs_my[:], cs4[:]), deps=[t_l, t_cs])

    t_vfm = []
    t_vload = [None, None]
    t_vs_free = [None, None]
    t_pst_free = [None, None]
    t_cp_prev = None
    for tc in range(16):
        vslot = tc % 2
        t_vl = p.op("sync", (lambda tc, vslot: lambda e: e.dma_start(
            out=v_tm[vslot][:],
            in_=dr["gin_kv1"][tc * 128:(tc + 1) * 128, D:2 * D]))(tc, vslot),
            deps=[t_vload[vslot]], dma=True, key="v%d" % vslot)
        for h in range(16):
            s = h % 2
            t_vs = p.op("vector", (lambda tc, h, vslot, s: lambda e: e.tensor_scalar(
                out=vs16[s][:], in0=v_tm[vslot][:, h * 128:(h + 1) * 128],
                scalar1=cs_my[:, tc * 16 + h:tc * 16 + h + 1], scalar2=None,
                op0=ALU.mult))(tc, h, vslot, s),
                deps=[t_vl, t_cs, t_vs_free[s]])
            t_tr = p.op("tensor", (lambda s: lambda e: e.transpose(
                ps_t[s].bitcast(BF16)[0:128, 0:128], vs16[s][:], ident[:]))(s),
                deps=[t_vs, t_pst_free[s]])
            t_vs_free[s] = t_tr
            t_cp = p.op("scalar", (lambda tc, h, s: lambda e: e.copy(
                vfm[h][:, tc * 128:(tc + 1) * 128],
                ps_t[s].bitcast(BF16)[0:128, 0:128]))(tc, h, s),
                deps=[t_tr])
            t_pst_free[s] = t_cp
            t_cp_prev = t_cp
        t_vload[vslot] = t_cp
    t_vfm = [t_cp_prev]
    deps["vout1"] = emit_outproj_ln(p, stack, dr, consts, "v1o",
                                    None, "wo1", "x1", "r3",
                                    ain_sbuf=vfm, ain_sbuf_deps=t_vfm)


_FFN_TT = 4

def phase_ffn1(p, stack, dr, consts, deps):
    nc = p.nc
    nm = "f1"
    TT = 2048
    ps_g = _ps(stack, nc, f"{nm}_psg", [128, 512])
    ps_u = _ps(stack, nc, f"{nm}_psu", [128, 512])
    ps_d = [_ps(stack, nc, f"{nm}_psd{s}", [128, 512]) for s in range(2)]
    ps_rows = _ps(stack, nc, f"{nm}_psrows", [128, 512])
    ps_sq2 = _ps(stack, nc, f"{nm}_pssq2", [1, 512])
    h_dram = nc.dram_tensor("f1_h", [64, 128, TT], BF16, kind="Internal").ap()
    wd1b = nc.dram_tensor("f1_wd1b", [16, 128, 64, 128], BF16,
                          kind="Internal").ap()
    lnb = {"sqf": _sb(stack, nc, nm + "_lnsqf", [128, 512], F32),
           "mean": _sb(stack, nc, nm + "_lnmean", [1, 512], F32),
           "msq": _sb(stack, nc, nm + "_lnmsq", [1, 512], F32),
           "m2": _sb(stack, nc, nm + "_lnm2", [1, 512], F32),
           "var": _sb(stack, nc, nm + "_lnvar", [1, 512], F32),
           "sd": _sb(stack, nc, nm + "_lnsd", [1, 512], F32),
           "rstd": _sb(stack, nc, nm + "_lnrstd", [1, 512], F32),
           "nmean": _sb(stack, nc, nm + "_lnnmean", [1, 512], F32),
           "mbc": _sb(stack, nc, nm + "_lnmbc", [128, 512], F32),
           "rbc": _sb(stack, nc, nm + "_lnrbc", [128, 512], F32),
           "outt": [_sb(stack, nc, nm + "_lnoutt%d" % s, [128, 512], F32)
                    for s in range(2)],
           "ps_sum": ps_rows[0:1, :], "ps_sq": ps_sq2,
           "ps_mb": ps_g, "ps_rb": ps_u}
    pv = {}
    dep_r3 = (deps.get("vout1") or {}).get("r3") or []
    stackA = contextlib.ExitStack()
    xb = [_sb(stackA, nc, f"{nm}_x{c}", [128, TT]) for c in range(16)]
    wg_sb = [_sb(stackA, nc, f"{nm}_wg{s}", [128, 2048]) for s in range(2)]
    wu_sb = [_sb(stackA, nc, f"{nm}_wu{s}", [128, 2048]) for s in range(2)]
    plane = [_sb(stackA, nc, f"{nm}_pl{m}", [128, TT]) for m in range(4)]
    aa = [_sb(stackA, nc, f"{nm}_aa{m}", [128, TT]) for m in range(4)]
    tmp1 = _sb(stackA, nc, f"{nm}_t1", [128, TT])
    tmp2 = _sb(stackA, nc, f"{nm}_t2", [128, TT])
    tmp3 = _sb(stackA, nc, f"{nm}_t3", [128, TT])
    thr = _sb(stackA, nc, f"{nm}_thr", [128, TT])
    msk = _sb(stackA, nc, f"{nm}_msk", [128, TT])
    sg = _sb(stackA, nc, f"{nm}_sg", [128, 512])

    t_xb = []
    for c in range(16):
        t_xb.append(p.op("gpsimd", (lambda c: lambda e: e.dma_start(
            out=xb[c][:], in_=dr["r3"][c * 128:(c + 1) * 128, :]))(c),
            deps=dep_r3, dma=True, key="xb"))
    t_xb_all = t_xb[15]
    t_cv = []
    tc_prev = None
    for i in range(16):
        tc_prev = p.op("gpsimd", (lambda i: lambda e: e.dma_start(
            out=wd1b[i:i + 1], in_=dr["wd1"][i:i + 1]))(i),
            deps=[tc_prev], dma=True, key="wcvt")
        t_cv.append(tc_prev)
    t_wg_use = [None, None]
    t_wu_use = [None, None]
    for gc in range(16):
        t_pl = [None] * 4
        for m in range(4):
            oc = m * 16 + gc
            ws = oc % 2
            t_wgl = p.op("gpsimd", (lambda oc, ws: lambda e: e.dma_start(
                out=wg_sb[ws][:], in_=dr["wg1"][oc]))(oc, ws),
                deps=[t_wg_use[ws]], dma=True, key="mwg%d" % ws)
            t_wul = p.op("gpsimd", (lambda oc, ws: lambda e: e.dma_start(
                out=wu_sb[ws][:], in_=dr["wu1"][oc]))(oc, ws),
                deps=[t_wu_use[ws]], dma=True, key="mwu%d" % ws)
            lastg = lastu = None
            for sub in range(4):
                for c in range(16):
                    lastg = p.op("tensor", (lambda oc, c, ws, sub: lambda e: e.matmul(
                        ps_g[:], wg_sb[ws][:, c * 128:(c + 1) * 128],
                        xb[c][:, sub * 512:(sub + 1) * 512],
                        start=(c == 0), stop=(c == 15),
                        skip_group_check=True))(oc, c, ws, sub),
                        deps=([t_wgl, t_xb_all, pv.get("psg_free")]
                              if c == 0 else []))
                t_sg = p.op("scalar", (lambda: lambda e: e.activation(
                    out=sg[:], in_=ps_g[:], func=AF.Silu))(),
                    deps=[lastg, pv.get("sg_free")])
                pv["psg_free"] = t_sg
                for c in range(16):
                    lastu = p.op("tensor", (lambda oc, c, ws, sub: lambda e: e.matmul(
                        ps_u[:], wu_sb[ws][:, c * 128:(c + 1) * 128],
                        xb[c][:, sub * 512:(sub + 1) * 512],
                        start=(c == 0), stop=(c == 15),
                        skip_group_check=True))(oc, c, ws, sub),
                        deps=([t_wul, pv.get("psu_free")] if c == 0 else []))
                t_h = p.op("vector", (lambda m, sub: lambda e: e.tensor_mul(
                    plane[m][:, sub * 512:(sub + 1) * 512], ps_u[:], sg[:]))(m, sub),
                    deps=[t_sg, lastu, pv.get("plfree%d" % m)])
                pv["psu_free"] = t_h
                pv["sg_free"] = t_h
                t_pl[m] = t_h
            t_wg_use[ws] = lastg
            t_wu_use[ws] = lastu
        t_ab = []
        for m in range(4):
            t_ab.append(p.op("scalar", (lambda m: lambda e: e.activation(
                out=aa[m][:], in_=plane[m][:], func=AF.Abs))(m),
                deps=[t_pl[m], pv.get("aafree%d" % m)]))
        t1 = p.op("vector", lambda e: e.tensor_max(tmp1[:], aa[0][:], aa[1][:]),
                  deps=t_ab[:2] + [pv.get("t1free")])
        t2 = p.op("vector", lambda e: e.tensor_tensor(
            tmp2[:], aa[0][:], aa[1][:], op=ALU.min),
            deps=t_ab[:2] + [pv.get("t2free")])
        t3 = p.op("vector", lambda e: e.tensor_max(tmp3[:], aa[2][:], aa[3][:]),
                  deps=t_ab[2:] + [pv.get("t3free")])
        t4 = p.op("vector", lambda e: e.tensor_tensor(
            thr[:], aa[2][:], aa[3][:], op=ALU.min),
            deps=t_ab[2:] + [pv.get("thrfree")])
        t5 = p.op("vector", lambda e: e.tensor_tensor(
            tmp1[:], tmp1[:], tmp3[:], op=ALU.min), deps=[t1, t3])
        t6 = p.op("vector", lambda e: e.tensor_max(tmp2[:], tmp2[:], thr[:]),
                  deps=[t2, t4])
        t7 = p.op("vector", lambda e: e.tensor_max(thr[:], tmp1[:], tmp2[:]),
                  deps=[t5, t6])
        pv["t1free"] = t7
        pv["t2free"] = t7
        pv["t3free"] = t7
        t_hd = None
        for m in range(4):
            oc = m * 16 + gc
            t_m1 = p.op("vector", (lambda m: lambda e: e.tensor_tensor(
                msk[:], aa[m][:], thr[:], op=ALU.is_ge))(m),
                deps=[t7, t_ab[m], pv.get("mskfree")])
            t_m2 = p.op("vector", (lambda m: lambda e: e.tensor_mul(
                plane[m][:], plane[m][:], msk[:]))(m), deps=[t_m1])
            pv["mskfree"] = t_m2
            pv["aafree%d" % m] = t_m2
            t_hd = p.op("sync", (lambda m, oc: lambda e: e.dma_start(
                out=h_dram[oc], in_=plane[m][:]))(m, oc),
                deps=[t_m2, pv.get("hdma%d" % m)], dma=True, key="hd%d" % m)
            pv["hdma%d" % m] = t_hd
            pv["plfree%d" % m] = t_hd
        pv["thrfree"] = t_m2
    t_hd_all = [pv["hdma%d" % m] for m in range(4)]
    stackA.close()
    h_all = [_sb(stack, nc, f"{nm}_h{i}", [128, 512]) for i in range(64)]
    wd_sb = [_sb(stack, nc, f"{nm}_wd{s}", [128, 8192]) for s in range(2)]
    r3t = [_sb(stack, nc, f"{nm}_r3t{s}", [128, 512], F32) for s in range(2)]
    y_sb = [_sb(stack, nc, f"{nm}_y{c}", [128, 512], F32) for c in range(16)]
    t_wd_use = [None, None]
    y_outs = None
    for ttd in range(4):
        t_hl = []
        for fc in range(64):
            t_hl.append(p.op("gpsimd", (lambda fc, ttd: lambda e: e.dma_start(
                out=h_all[fc][:],
                in_=h_dram[fc][:, ttd * 512:(ttd + 1) * 512]))(fc, ttd),
                deps=t_hd_all + [pv.get("hafree")], dma=True, key="ha"))
        t_hl_all = t_hl[63]
        for oc in range(16):
            ds = oc % 2
            t_wd = p.op("sync", (lambda oc, ds: lambda e: e.dma_start(
                out=wd_sb[ds][:], in_=wd1b[oc]))(oc, ds),
                deps=[t_wd_use[ds], t_cv[oc]] + t_hd_all, dma=True,
                key="mwd%d" % ds)
            rs = oc % 2
            t_rl = p.op("sync", (lambda oc, ttd, rs: lambda e: e.dma_start(
                out=r3t[rs][:],
                in_=dr["r3"][oc * 128:(oc + 1) * 128,
                             ttd * 512:(ttd + 1) * 512]))(oc, ttd, rs),
                deps=[pv.get("r3free%d" % rs)] + t_hd_all, dma=True,
                key="r3t%d" % rs)
            lastd = None
            for fc in range(64):
                lastd = p.op("tensor", (lambda oc, fc, ds: lambda e: e.matmul(
                    ps_d[ds][:], wd_sb[ds][:, fc * 128:(fc + 1) * 128],
                    h_all[fc][:], start=(fc == 0), stop=(fc == 63),
                    skip_group_check=True))(oc, fc, ds),
                    deps=([t_wd, t_hl_all, pv.get("psd_free%d" % ds)]
                          if fc == 0 else []))
            t_wd_use[ds] = lastd
            t_y = p.op("vector", (lambda oc, ds, rs: lambda e: e.tensor_add(
                y_sb[oc][:], ps_d[ds][:], r3t[rs][:]))(oc, ds, rs),
                deps=[lastd, t_rl, pv.get("ydma%d" % oc)])
            pv["psd_free%d" % ds] = t_y
            pv["r3free%d" % rs] = t_y
            pv["y%d" % oc] = t_y
        pv["hafree"] = pv["y15"]
        pv["mbm"] = pv.get("psg_free")
        pv["rbm"] = pv.get("psu_free")
        y_outs = emit_ln_tail(p, dr, consts, lnb, y_sb, "y", ttd, pv)
        pv["psg_free"] = pv["mb_cp"]
        pv["psu_free"] = pv["rb_cp"]
        for oc in range(16):
            pv["ydma%d" % oc] = pv["yfree%d" % oc]
    deps["ffn1"] = {"y": y_outs}


_PHASE_FNS["vout1"] = phase_vout1
_PHASE_FNS["vout1f"] = phase_vout1
_PHASE_FNS["ffn1"] = phase_ffn1


_PHASE_FNS["attn0"] = phase_attn0
